# version 19
# baseline (speedup 1.0000x reference)
import numpy as np

import concourse.bass as bass
import concourse.mybir as mybir
from concourse.bass_utils import run_bass_kernel_spmd

F32 = mybir.dt.float32
BF16 = mybir.dt.bfloat16
AF = mybir.ActivationFunctionType
ALU = mybir.AluOpType
f32 = np.float32

ENGINES = ("sync", "gpsimd", "tensor", "vector", "scalar")
NW = 512


class CFG:
    def __init__(self, D=2048, Tl=2048, H=16, E=4, I=2048, R=4):
        self.D, self.Tl, self.H, self.E, self.I, self.R = D, Tl, H, E, I, R
        self.HD = 128
        assert H * self.HD == D
        self.S = R * Tl
        self.NCH = D // 128
        self.TCH = Tl // 128
        self.KCH = self.S // 128
        self.QW = min(1024, Tl)
        self.NQ = Tl // self.QW
        self.TE = Tl // E
        self.ICH = I // 128
        self.GW = min(512, I)
        self.NSLAB = (2 * I) // self.GW
        self.EPS = 1e-5
        self.SCALE = 1.0 / float(np.sqrt(self.HD))
        self.SHIFT = -5.0


class Plan:
    def __init__(self, sems):
        self.ops = {e: [] for e in ENGINES}
        self.ctr = {k: 0 for k in sems}
        self.sems = sems

    def emit(self, eng, fn):
        self.ops[eng].append(fn)

    def inc(self, sem, n=1):
        self.ctr[sem] += n
        return self.ctr[sem]

    def wait(self, eng, sem, v):
        if v > 0:
            s = self.sems[sem]
            self.ops[eng].append(lambda e, s=s, v=v: e.wait_ge(s, v))

    def drain(self, eng):
        self.ops[eng].append(lambda e: e.drain())


def _sl(width, nw=NW):
    out, o = [], 0
    while o < width:
        w = min(nw, width - o)
        out.append((o, w))
        o += w
    return out


def build_nc(cfg: CFG):
    nc = bass.Bass("TRN2", target_bir_lowering=False)
    D, Tl, H, E, I, R = cfg.D, cfg.Tl, cfg.H, cfg.E, cfg.I, cfg.R
    NCH, TCH, KCH, QW, NQ, TE, ICH = (cfg.NCH, cfg.TCH, cfg.KCH, cfg.QW,
                                      cfg.NQ, cfg.TE, cfg.ICH)
    S, GW, NSLAB = cfg.S, cfg.GW, cfg.NSLAB
    HP = H // 2

    xT = nc.dram_tensor("xT", [D, Tl], F32, kind="ExternalInput").ap()
    wqT = nc.dram_tensor("wqT", [D, D], F32, kind="ExternalInput").ap()
    wkT = nc.dram_tensor("wkT", [D, D], F32, kind="ExternalInput").ap()
    wvT = nc.dram_tensor("wvT", [D, D], F32, kind="ExternalInput").ap()
    woT = nc.dram_tensor("woT", [D, D], F32, kind="ExternalInput").ap()
    wgu = nc.dram_tensor("wgu", [E, D, 2 * I], F32, kind="ExternalInput").ap()
    wdn = nc.dram_tensor("wdn", [E, I, D], F32, kind="ExternalInput").ap()
    outT = nc.dram_tensor("outT", [D, Tl], F32, kind="ExternalOutput").ap()

    kloc = nc.dram_tensor("kloc", [D, Tl], BF16)
    vloc = nc.dram_tensor("vloc", [Tl, D], BF16)
    kgat = nc.dram_tensor("kgat", [R * D, Tl], BF16)
    vgat = nc.dram_tensor("vgat", [R * Tl, D], BF16)
    qdram = nc.dram_tensor("qdram", [D, Tl], BF16).ap()
    odram = nc.dram_tensor("odram", [D, Tl], BF16).ap()
    yadram = nc.dram_tensor("yadram", [D, Tl], F32).ap()
    x1dram = nc.dram_tensor("x1dram", [D, Tl], F32).ap()

    def sb(name, shape, dtype, off):
        return nc.alloc_sbuf_tensor_at(name, list(shape), dtype, offset=ARENA + off)

    A0 = 0
    a_sz = max(NCH * Tl * 2, 2 * 2 * S * 2)
    B0 = A0 + a_sz
    b_p6 = [("gsl", 2 * NCH * GW * 2), ("dsl", 2 * ICH * 128 * 2),
            ("sg", 2 * TE * 4), ("intr", ICH * TE * 2), ("outt", 2 * TE * 4)]
    b_sz = max(NCH * D * 2, sum(s for _, s in b_p6))
    C0 = B0 + b_sz
    c_sz = max(2 * Tl * 4 + 2 * Tl * 2 + 2 * Tl * 2, 2 * D * 2, KCH * 256 * 2, NCH * TE * 4)
    D0 = C0 + c_sz
    E0 = D0 + Tl * 4
    F0 = E0 + 2 * Tl * 4
    G0 = F0 + 2 * Tl * 4
    need = G0 + 1152 + TE * 4 + 4 * TE * 2
    assert need <= nc.sbuf_bytes_remaining, (need, nc.sbuf_bytes_remaining)
    _resv = nc.alloc_sbuf_tensor("arena_reserve", [128, need], mybir.dt.uint8)
    ARENA = nc.lookup_mloc(_resv).addr

    xb_all = sb("xb_all", [128, NCH * Tl], BF16, A0)
    kp_buf = sb("kp_buf", [128, 2 * 2 * S], BF16, A0)
    oall = sb("oall", [128, NCH * Tl], BF16, A0)

    wbuf = sb("wbuf", [128, NCH * D], BF16, B0)
    off = B0
    gslab = sb("gslab", [128, 2 * NCH * GW], BF16, off); off += b_p6[0][1]
    dslab = sb("dslab", [128, 2 * ICH * 128], BF16, off); off += b_p6[1][1]
    sg_t = sb("sg_t", [128, 2 * TE], F32, off); off += b_p6[2][1]
    inter = sb("inter", [128, ICH * TE], BF16, off); off += b_p6[3][1]
    out_t = sb("out_t", [128, 2 * TE], F32, off)

    xf_t = sb("xf_t", [128, 2 * Tl], F32, C0)
    x2b_t = sb("x2b_t", [128, 2 * Tl], BF16, C0 + 2 * Tl * 4)
    y1_t = sb("y1_t", [128, Tl], BF16, C0 + 2 * Tl * 4 + 2 * Tl * 2)
    ev_t = sb("ev_t", [128, 2 * D], BF16, C0)
    vp_buf = sb("vp_buf", [128, KCH * 256], BF16, C0)
    yab_t = sb("yab_t", [128, 2 * Tl], BF16, C0 + 2 * Tl * 4 + 2 * Tl * 2)
    yf_e = sb("yf_e", [128, NCH * TE], F32, C0)

    x1_t = sb("x1_t", [128, Tl], F32, D0)
    qh_buf = sb("qh_buf", [128, 2 * Tl], BF16, D0)
    rowA = sb("rowA", [1, Tl], F32, E0)
    pt_buf = sb("pt_buf", [128, 8 * QW], BF16, E0)
    rowC = sb("rowC", [1, Tl], F32, E0 + Tl * 4)
    rbb = sb("rbb", [128, Tl], F32, F0)
    ot_buf = sb("ot_buf", [128, 2 * QW], BF16, F0)
    mrbb = sb("mrbb", [128, Tl], F32, F0 + Tl * 4)
    accD = sb("accD", [128, QW], F32, F0 + Tl * 4)
    accP = sb("accP", [128, QW], F32, F0 + Tl * 4 + QW * 4)
    ones_bf = sb("ones_bf", [128, 1], BF16, G0)
    ones_row = sb("ones_row", [1, 128], F32, G0 + 32)
    konst = sb("konst", [1, 4], F32, G0 + 544)
    ones128b = sb("ones128b", [128, 128], BF16, G0 + 576)
    o1_t = sb("o1_t", [128, TE], F32, G0 + 1152)
    accB = sb("accB", [128, QW], BF16, G0 + 1152)
    sq6_t = sb("sq6_t", [128, 2 * TE], BF16, G0 + 1152 + TE * 4)
    cp6_t = sb("cp6_t", [128, 2 * TE], BF16, G0 + 1152 + TE * 4 + 2 * TE * 2)
    linv = sb("linv", [128, QW], F32, G0 + 1152 + TE * 4)

    pA = nc.alloc_psum_tensor("pA", [128, 2048], F32)
    pB = nc.alloc_psum_tensor("pB", [128, 2048], F32)

    class LazySems(dict):
        def __missing__(self, k):
            self[k] = nc.alloc_semaphore(k)
            return self[k]

    sems = LazySems()

    class Plan2:
        def __init__(self):
            self.ops = {e: [] for e in ENGINES}
            self.ctr = {}
            self.log = {e: [] for e in ENGINES}

        def emit(self, eng, fn):
            self.ops[eng].append(fn)

        def inc(self, sem, n=1, eng=None):
            self.ctr[sem] = self.ctr.get(sem, 0) + n
            if eng is not None:
                self.log[eng].append(("inc", sem, n, False))
            return self.ctr[sem]

        def wait(self, eng, sem, v):
            if v > 0:
                s = sems[sem]
                self.ops[eng].append(lambda e, s=s, v=v: e.wait_ge(s, v))
                self.log[eng].append(("wait", sem, v, False))

        def dma(self, eng, sem, fn):
            s = sems[sem]
            self.ops[eng].append(lambda e, fn=fn, s=s: fn(e).then_inc(s, 16))
            self.log[eng].append(("inc", sem, 16, True))
            return self.inc(sem, 16)

        def drain(self, eng):
            self.ops[eng].append(lambda e: e.drain())

    P = Plan2()
    tl_sl = _sl(Tl)
    d_sl = _sl(D)
    q_sl = _sl(QW)
    te_sl = _sl(TE)

    def emit_v(fn, inc=False):
        if inc:
            P.emit("vector", lambda e, fn=fn: fn(e).then_inc(sems["ve"], 1))
            return P.inc("ve", eng="vector")
        P.emit("vector", lambda e, fn=fn: fn(e))
        return None

    def emit_s(fn, inc=False):
        if inc:
            P.emit("scalar", lambda e, fn=fn: fn(e).then_inc(sems["ac"], 1))
            return P.inc("ac", eng="scalar")
        P.emit("scalar", lambda e, fn=fn: fn(e))
        return None

    def emit_t(fn, inc=False):
        if inc:
            P.emit("tensor", lambda e, fn=fn: fn(e).then_inc(sems["tm"], 1))
            return P.inc("tm", eng="tensor")
        P.emit("tensor", lambda e, fn=fn: fn(e))
        return None

    def emit_p(fn, inc=False):
        if inc:
            P.emit("gpsimd", lambda e, fn=fn: fn(e).then_inc(sems["po"], 1))
            return P.inc("po", eng="gpsimd")
        P.emit("gpsimd", lambda e, fn=fn: fn(e))
        return None

    def ln_rows2(width, wait_sem, wait_val, from_psum):
        P.wait("vector", wait_sem, wait_val)
        if from_psum:
            emit_v(lambda e: e.tensor_scalar_mul(rowA[0:1, 0:width], pA[0:1, 0:width], konst[0:1, 0:1]))
            v1 = emit_v(lambda e: e.tensor_scalar_mul(rowC[0:1, 0:width], pB[0:1, 0:width], konst[0:1, 0:1]), inc=True)
        else:
            emit_v(lambda e: e.tensor_scalar_mul(rowA[0:1, 0:width], rowA[0:1, 0:width], konst[0:1, 0:1]))
            v1 = emit_v(lambda e: e.tensor_scalar_mul(rowC[0:1, 0:width], rowC[0:1, 0:width], konst[0:1, 0:1]), inc=True)
        P.wait("scalar", "ve", v1)
        a1 = emit_s(lambda e: e.activation(out=mrbb[0:1, 0:width], in_=rowA[0:1, 0:width], func=AF.Square), inc=True)
        P.wait("vector", "ac", a1)
        v2 = emit_v(lambda e: e.tensor_tensor(rowC[0:1, 0:width], rowC[0:1, 0:width], mrbb[0:1, 0:width], ALU.subtract), inc=True)
        P.wait("scalar", "ve", v2)
        emit_s(lambda e: e.activation(out=rowC[0:1, 0:width], in_=rowC[0:1, 0:width], func=AF.Ln, bias=f32(cfg.EPS), scale=f32(1.0)))
        a2 = emit_s(lambda e: e.activation(out=rowC[0:1, 0:width], in_=rowC[0:1, 0:width], func=AF.Exp, bias=f32(0.0), scale=f32(-0.5)), inc=True)
        P.wait("vector", "ac", a2)
        v3 = emit_v(lambda e: e.tensor_tensor(rowA[0:1, 0:width], rowA[0:1, 0:width], rowC[0:1, 0:width], ALU.mult), inc=True)
        return v3

    def ln_bcast(width, ve_mr):
        P.wait("tensor", "ve", ve_mr)
        sl = _sl(width)
        for (o, w) in sl:
            emit_t(lambda e, o=o, w=w: e.matmul(pA[:, o:o + w], ones_row[:], rowC[0:1, o:o + w], start=True, stop=True))
        t1 = 0
        for gi, (o, w) in enumerate(sl):
            t1 = emit_t(lambda e, o=o, w=w: e.matmul(pB[:, o:o + w], ones_row[:], rowA[0:1, o:o + w], start=True, stop=True),
                        inc=(gi == len(sl) - 1))
        P.wait("vector", "tm", t1)
        emit_v(lambda e: e.tensor_copy(rbb[:, 0:width], pA[:, 0:width]))
        vb = emit_v(lambda e: e.tensor_copy(mrbb[:, 0:width], pB[:, 0:width]), inc=True)
        return vb

    emit_v(lambda e: e.memset(ones_bf[:], 1.0))
    emit_v(lambda e: e.memset(ones_row[:], 1.0))
    emit_v(lambda e: e.memset(ones128b[:], 1.0))
    emit_v(lambda e: e.memset(konst[0:1, 0:1], 1.0 / D), inc=True)

    ve_cast = [0] * NCH
    tm_st1 = [0] * NCH
    ldx = [0] * NCH
    for c in range(NCH):
        p = c % 2
        if c >= 2:
            P.wait("sync", "ve", ve_cast[c - 2])
        ldx[c] = P.dma("sync", f"ldx{p}", lambda e, c=c, p=p: e.dma_start(
            out=xf_t[:, p * Tl:(p + 1) * Tl], in_=xT[c * 128:(c + 1) * 128, :]))
        P.wait("vector", f"ldx{p}", ldx[c])
        ve_cast[c] = emit_v(lambda e, c=c, p=p: e.tensor_copy(
            xb_all[:, c * Tl:(c + 1) * Tl], xf_t[:, p * Tl:(p + 1) * Tl]), inc=True)
        P.wait("scalar", "ve", ve_cast[c])
        if c >= 1:
            P.wait("scalar", "tm", tm_st1[c - 1])
        ac1 = emit_s(lambda e, c=c: e.activation(
            out=x2b_t[:, 0:Tl], in_=xb_all[:, c * Tl:(c + 1) * Tl], func=AF.Square), inc=True)
        P.wait("tensor", "ac", ac1)
        for (o, w) in tl_sl:
            emit_t(lambda e, c=c, o=o, w=w: e.matmul(
                pA[0:1, o:o + w], ones_bf[:], xb_all[:, c * Tl + o:c * Tl + o + w],
                start=(c == 0), stop=(c == NCH - 1)))
        for gi, (o, w) in enumerate(tl_sl):
            tm_st1[c] = emit_t(lambda e, c=c, o=o, w=w: e.matmul(
                pB[0:1, o:o + w], ones_bf[:], x2b_t[:, o:o + w],
                start=(c == 0), stop=(c == NCH - 1)), inc=(gi == len(tl_sl) - 1))

    ve_mr1 = ln_rows2(Tl, "tm", tm_st1[NCH - 1], True)
    ve_b1 = ln_bcast(Tl, ve_mr1)
    P.wait("vector", "ve", ve_b1)
    ve_n1 = 0
    for c in range(NCH):
        emit_v(lambda e, c=c: e.tensor_tensor(
            y1_t[:, 0:Tl], xb_all[:, c * Tl:(c + 1) * Tl], rbb[:, 0:Tl], ALU.mult))
        ve_n1 = emit_v(lambda e, c=c: e.tensor_tensor(
            xb_all[:, c * Tl:(c + 1) * Tl], y1_t[:, 0:Tl], mrbb[:, 0:Tl], ALU.subtract), inc=True)

    wl_w = {}

    def load_weight(tag, src_ap, gate_fn):
        dstf = wbuf[:].rearrange("p (dc e) -> p dc e", dc=NCH)
        for g in range(4):
            P.wait("gpsimd", "tm", gate_fn(g))
            src = src_ap[:, g * 512:(g + 1) * 512].rearrange("(dc p) e -> p dc e", p=128)
            dst = dstf[:, :, g * 512:(g + 1) * 512]
            wl_w[(tag, g)] = P.dma("gpsimd", f"wl{g}", lambda e, src=src, dst=dst: e.dma_start(out=dst, in_=src))

    tm_proj = {}
    ve_evac = {}
    sd_store = {}

    def emit_proj_fm(tag, dst_rows_fn, prev_tag=None):
        for ech in range(NCH):
            reg = pA if ech % 2 == 0 else pB
            p = ech % 2
            if ech % 4 == 0:
                P.wait("tensor", f"wl{ech // 4}", wl_w[(tag, ech // 4)])
            if ech == 0:
                P.wait("tensor", "ve", ve_n1)
            if ech < 2 and prev_tag is not None:
                P.wait("tensor", "ve", ve_evac[(prev_tag, NCH - 2 + ech)])
            if ech >= 2:
                P.wait("tensor", "ve", ve_evac[(tag, ech - 2)])
            tmv = 0
            for dc in range(NCH):
                for gi, (o, w) in enumerate(tl_sl):
                    tmv = emit_t(lambda e, ech=ech, dc=dc, o=o, w=w, reg=reg: e.matmul(
                        reg[:, o:o + w],
                        wbuf[:, dc * D + ech * 128:dc * D + ech * 128 + 128],
                        xb_all[:, dc * Tl + o:dc * Tl + o + w],
                        start=(dc == 0), stop=(dc == NCH - 1)),
                        inc=(dc == NCH - 1 and gi == len(tl_sl) - 1)) or tmv
            tm_proj[(tag, ech)] = tmv
            P.wait("vector", "tm", tmv)
            if ech >= 2:
                P.wait("vector", f"sde{p}", sd_store[(tag, ech - 2)])
            elif prev_tag is not None:
                P.wait("vector", f"sde{p}", sd_store[(prev_tag, NCH - 2 + ech)])
            ve_evac[(tag, ech)] = emit_v(lambda e, p=p, reg=reg: e.tensor_copy(
                ev_t[:, p * D:p * D + Tl], reg[:, 0:Tl]), inc=True)
            P.wait("sync", "ve", ve_evac[(tag, ech)])
            sd_store[(tag, ech)] = P.dma("sync", f"sde{p}", lambda e, ech=ech, p=p, fn=dst_rows_fn: e.dma_start(
                out=fn(ech), in_=ev_t[:, p * D:p * D + Tl]))

    rg = [[0, 1, 2, 3], [4, 5, 6, 7]]
    load_weight("k", wkT, lambda g: 0)
    emit_proj_fm("k", lambda ech: kloc.ap()[ech * 128:(ech + 1) * 128, :])
    P.wait("gpsimd", "sde0", P.ctr.get("sde0", 0))
    P.wait("gpsimd", "sde1", P.ctr.get("sde1", 0))
    P.emit("gpsimd", lambda e: e.collective_compute(
        "AllGather", ALU.bypass, replica_groups=rg,
        ins=[kloc.ap().opt()], outs=[kgat.ap().opt()]).then_inc(sems["cc"], 1))
    P.inc("cc", eng="gpsimd")
    load_weight("v", wvT, lambda g: tm_proj[("k", 4 * g + 3)])

    for tt in range(TCH):
        reg = pA if tt % 2 == 0 else pB
        p = tt % 2
        if tt == 0:
            for g4 in range(4):
                P.wait("tensor", f"wl{g4}", wl_w[("v", g4)])
        if tt < 2:
            P.wait("tensor", "ve", ve_evac[("k", NCH - 2 + tt)])
        if tt >= 2:
            P.wait("tensor", "ve", ve_evac[("v", tt - 2)])
        tmv = 0
        for dc in range(NCH):
            for gi, (o, w) in enumerate(d_sl):
                tmv = emit_t(lambda e, tt=tt, dc=dc, o=o, w=w, reg=reg: e.matmul(
                    reg[:, o:o + w],
                    xb_all[:, dc * Tl + tt * 128:dc * Tl + tt * 128 + 128],
                    wbuf[:, dc * D + o:dc * D + o + w],
                    start=(dc == 0), stop=(dc == NCH - 1)),
                    inc=(dc == NCH - 1 and gi == len(d_sl) - 1)) or tmv
        tm_proj[("v", tt)] = tmv
        P.wait("vector", "tm", tmv)
        if tt >= 2:
            P.wait("vector", f"sde{p}", sd_store[("v", tt - 2)])
        else:
            P.wait("vector", f"sde{p}", sd_store[("k", NCH - 2 + tt)])
        ve_evac[("v", tt)] = emit_v(lambda e, p=p, reg=reg: e.tensor_copy(
            ev_t[:, p * D:(p + 1) * D], reg[:, 0:D]), inc=True)
        P.wait("sync", "ve", ve_evac[("v", tt)])
        sd_store[("v", tt)] = P.dma("sync", f"sde{p}", lambda e, tt=tt, p=p: e.dma_start(
            out=vloc.ap()[tt * 128:(tt + 1) * 128, :], in_=ev_t[:, p * D:(p + 1) * D]))

    P.wait("gpsimd", "sde0", P.ctr.get("sde0", 0))
    P.wait("gpsimd", "sde1", P.ctr.get("sde1", 0))
    P.emit("gpsimd", lambda e: e.collective_compute(
        "AllGather", ALU.bypass, replica_groups=rg,
        ins=[vloc.ap().opt()], outs=[vgat.ap().opt()]).then_inc(sems["cc"], 1))
    cc_v = P.inc("cc", eng="gpsimd")
    load_weight("q", wqT, lambda g: tm_proj[("v", TCH - 1)])
    emit_proj_fm("q", lambda ech: qdram[ech * 128:(ech + 1) * 128, :], prev_tag="v")
    sde_tot = [P.ctr.get("sde0", 0), P.ctr.get("sde1", 0)]
    load_weight("o", woT, lambda g: tm_proj[("q", 4 * g + 3)])

    kgat_r = kgat.ap().rearrange("(r d) t -> d r t", r=R)
    vgat_ap = vgat.ap()
    NVQ = 4
    JQ = KCH // NVQ

    def pool_own(jj):
        return jj % 3 == 0 and jj < 63

    ld_k = [0] * HP
    ld_v = [[0] * NVQ for _ in range(HP)]
    ld_qh = [0] * H
    tm_sc_g = {}
    tm_pv_g = {}
    ac_g = {}
    po_g = {}
    ve_add_g = {}
    ve_acs = {}
    tm_col = {}
    ve_rc = {}
    ve_ot = {}
    sd_od = {}

    def emit_kload(hp):
        slot = hp % 2
        for hh in range(2):
            h = hp * 2 + hh
            kdst = kp_buf[:, (slot * 2 + hh) * S:(slot * 2 + hh + 1) * S].rearrange(
                "p (r t) -> p r t", r=R)
            ld_k[hp] = P.dma("sync", f"lk{slot}", lambda e, h=h, kdst=kdst: e.dma_start(
                out=kdst, in_=kgat_r[h * 128:(h + 1) * 128, :, :]))

    def emit_vload(hp, eng):
        for vq in range(NVQ):
            vsrc = vgat_ap[vq * (S // NVQ):(vq + 1) * (S // NVQ),
                           hp * 256:(hp + 1) * 256].rearrange("(j p) u -> p j u", p=128)
            vdst = vp_buf[:, vq * JQ * 256:(vq + 1) * JQ * 256].rearrange(
                "p (j u) -> p j u", j=JQ)
            ld_v[hp][vq] = P.dma(eng, f"lv{vq}", lambda e, vsrc=vsrc, vdst=vdst: e.dma_start(
                out=vdst, in_=vsrc))

    tm_v_done = tm_proj[("q", NCH - 1)]
    P.wait("sync", "tm", tm_v_done)
    P.wait("sync", "cc", cc_v)
    P.wait("sync", "sde0", sde_tot[0])
    P.wait("sync", "sde1", sde_tot[1])
    emit_kload(0)
    emit_vload(0, "sync")
    if HP > 1:
        emit_kload(1)

    it = 0
    for hp in range(HP):
        slot = hp % 2
        for hh in range(2):
            h = hp * 2 + hh
            hq = h % 2
            if h >= 2:
                P.wait("sync", "tm",
                       tm_sc_g[((h - 2) * NQ + NQ - 1) * KCH + KCH - 1])
            ld_qh[h] = P.dma("sync", f"lq{hq}", lambda e, h=h, hq=hq: e.dma_start(
                out=qh_buf[:, hq * Tl:(hq + 1) * Tl], in_=qdram[h * 128:(h + 1) * 128, :]))

            for qh in range(NQ):
                qb = qh * QW
                bg = it * KCH
                s_pv = (it % 2) * QW
                it_loc = it

                def emit_scores_exp_pair(mm, bg=None, qb=None, hh=None, hq=None, slot=None, hp=None, h=None):
                    for jj in (2 * mm, 2 * mm + 1):
                        g = bg + jj
                        if g >= 2:
                            P.wait("tensor", "ac", ac_g[g - 2])
                        if jj == 0:
                            P.wait("tensor", f"lk{slot}", ld_k[hp])
                            P.wait("tensor", f"lq{hq}", ld_qh[h])
                        ps = (g % 2) * QW
                        for gi, (o, w) in enumerate(q_sl):
                            r = emit_t(lambda e, hh=hh, jj=jj, o=o, w=w, qb=qb, hq=hq,
                                       slot=slot, ps=ps: e.matmul(
                                pA[:, ps + o:ps + o + w],
                                kp_buf[:, (slot * 2 + hh) * S + jj * 128:(slot * 2 + hh) * S + jj * 128 + 128],
                                qh_buf[:, hq * Tl + qb + o:hq * Tl + qb + o + w],
                                start=True, stop=True),
                                inc=(gi == len(q_sl) - 1))
                            if r:
                                tm_sc_g[g] = r
                    for jj in (2 * mm, 2 * mm + 1):
                        g = bg + jj
                        P.wait("scalar", "tm", tm_sc_g[g])
                        if jj % 2 == 0 and g >= 8:
                            P.wait("scalar", "tm", tm_pv_g[g - 7])
                            pov = [po_g[c] for c in (g - 8, g - 7) if pool_own(c % KCH)]
                            vev = [ve_add_g[c] for c in (g - 8, g - 7) if not pool_own(c % KCH)]
                            if pov:
                                P.wait("scalar", "po", max(pov))
                            if vev:
                                P.wait("scalar", "ve", max(vev))
                        ac_g[g] = emit_s(lambda e, g=g: e.activation(
                            out=pt_buf[:, (g % 8) * QW:(g % 8) * QW + QW],
                            in_=pA[:, (g % 2) * QW:(g % 2) * QW + QW],
                            func=AF.Exp, bias=f32(cfg.SHIFT), scale=f32(cfg.SCALE)), inc=True)

                def emit_pv_acc_pair(mm, bg=None, hh=None, s_pv=None, hp=None, it_loc=None):
                    for jj in (2 * mm, 2 * mm + 1):
                        g = bg + jj
                        P.wait("tensor", "ac", ac_g[g])
                        if jj % JQ == 0:
                            P.wait("tensor", f"lv{jj // JQ}", ld_v[hp][jj // JQ])
                        if jj == 0 and it_loc >= 1:
                            P.wait("tensor", "ve", ve_rc[it_loc - 1])
                        tpv = 0
                        for gi, (o, w) in enumerate(q_sl):
                            tpv = emit_t(lambda e, jj=jj, hh=hh, o=o, w=w, s_pv=s_pv,
                                         pq=(g % 8) * QW: e.matmul(
                                pB[:, s_pv + o:s_pv + o + w],
                                vp_buf[:, jj * 256 + hh * 128:jj * 256 + hh * 128 + 128],
                                pt_buf[:, pq + o:pq + o + w],
                                start=(jj == 0), stop=(jj == KCH - 1)),
                                inc=(gi == len(q_sl) - 1)) or tpv
                        tm_pv_g[g] = tpv
                    for jj in (2 * mm, 2 * mm + 1):
                        g = bg + jj
                        if pool_own(jj):
                            P.wait("gpsimd", "ac", ac_g[g])
                            if jj == 0 and it_loc >= 1:
                                P.wait("gpsimd", "ve", ve_acs[it_loc - 1])
                            if jj == 0:
                                po_g[g] = emit_p(lambda e, g=g: e.tensor_copy(
                                    accP[:, 0:QW], pt_buf[:, (g % 8) * QW:(g % 8) * QW + QW]),
                                    inc=True)
                            else:
                                po_g[g] = emit_p(lambda e, g=g: e.tensor_tensor(
                                    accP[:, 0:QW], accP[:, 0:QW],
                                    pt_buf[:, (g % 8) * QW:(g % 8) * QW + QW], ALU.add),
                                    inc=True)
                        else:
                            P.wait("vector", "ac", ac_g[g])
                            if jj == 1:
                                ve_add_g[g] = emit_v(lambda e, g=g: e.tensor_copy(
                                    accD[:, 0:QW], pt_buf[:, (g % 8) * QW:(g % 8) * QW + QW]),
                                    inc=True)
                            else:
                                ve_add_g[g] = emit_v(lambda e, g=g: e.tensor_tensor(
                                    accD[:, 0:QW], accD[:, 0:QW],
                                    pt_buf[:, (g % 8) * QW:(g % 8) * QW + QW], ALU.add),
                                    inc=True)

                def emit_tail(pit):
                    pc = ((pit + 1) % 2) * QW
                    pv = (pit % 2) * QW
                    P.wait("tensor", "ve", ve_acs[pit])
                    if pit >= 1:
                        P.wait("tensor", "ve", ve_ot[pit - 1])
                    tcol = 0
                    for gi, (o, w) in enumerate(q_sl):
                        tcol = emit_t(lambda e, o=o, w=w, pc=pc: e.matmul(
                            pB[:, pc + o:pc + o + w], ones128b[:], accB[:, o:o + w],
                            start=True, stop=True), inc=(gi == len(q_sl) - 1)) or tcol
                    tm_col[pit] = tcol
                    P.wait("vector", "tm", tm_col[pit])
                    ve_rc[pit] = emit_v(lambda e, pc=pc: e.reciprocal(
                        linv[:, 0:QW], pB[:, pc:pc + QW]), inc=True)
                    osel = pit % 2
                    P.wait("vector", "tm", tm_pv_g[pit * KCH + KCH - 1])
                    if pit >= 2:
                        P.wait("vector", f"sod{osel}", sd_od[pit - 2])
                    ve_ot[pit] = emit_v(lambda e, pv=pv, osel=osel: e.tensor_tensor(
                        ot_buf[:, osel * QW:(osel + 1) * QW], pB[:, pv:pv + QW],
                        linv[:, 0:QW], ALU.mult), inc=True)
                    P.wait("sync", "ve", ve_ot[pit])
                    ph = pit % (NQ * H) // NQ
                    pqb = (pit % NQ) * QW
                    sd_od[pit] = P.dma("sync", f"sod{osel}", lambda e, ph=ph, pqb=pqb, osel=osel: e.dma_start(
                        out=odram[ph * 128:(ph + 1) * 128, pqb:pqb + QW],
                        in_=ot_buf[:, osel * QW:(osel + 1) * QW]))

                kw_se = dict(bg=bg, qb=qb, hh=hh, hq=hq, slot=slot, hp=hp, h=h)
                kw_pv = dict(bg=bg, hh=hh, s_pv=s_pv, hp=hp, it_loc=it_loc)
                for m in range(KCH // 2):
                    emit_scores_exp_pair(m, **kw_se)
                    if m == 1 and it_loc >= 1:
                        emit_tail(it_loc - 1)
                    if m >= 1:
                        emit_pv_acc_pair(m - 1, **kw_pv)
                emit_pv_acc_pair(KCH // 2 - 1, **kw_pv)
                P.wait("vector", "po", po_g[bg + 60])
                ve_acs[it_loc] = emit_v(lambda e: e.tensor_tensor(
                    accB[:, 0:QW], accD[:, 0:QW], accP[:, 0:QW], ALU.add), inc=True)
                if it_loc == H * NQ - 1:
                    emit_tail(it_loc)
                it += 1

        if hp + 1 < HP:
            P.wait("gpsimd", "tm", tm_pv_g[it * KCH - 1])
            emit_vload(hp + 1, "gpsimd")
        if hp + 2 < HP:
            P.wait("sync", "tm", tm_sc_g[it * KCH - 1])
            emit_kload(hp + 2)

    sod_tot = [P.ctr.get("sod0", 0), P.ctr.get("sod1", 0)]

    P.wait("sync", "sod0", sod_tot[0])
    P.wait("sync", "sod1", sod_tot[1])
    ld_oall = 0
    for dc in range(NCH):
        ld_oall = P.dma("sync", "lo", lambda e, dc=dc: e.dma_start(
            out=oall[:, dc * Tl:(dc + 1) * Tl], in_=odram[dc * 128:(dc + 1) * 128, :]))

    emit_v(lambda e: e.memset(rowA[0:1, 0:Tl], 0.0))
    ve_z5 = emit_v(lambda e: e.memset(rowC[0:1, 0:Tl], 0.0), inc=True)

    ve_ya = [0] * NCH
    sd_ya = [0] * NCH
    ac_sq5 = [0] * NCH
    ve_srC = 0
    tm_sum5 = [0] * NCH
    tm_sq5v = [0] * NCH
    tm_op = 0

    def emit_stats5(k):
        nonlocal ve_srC
        p = k % 2
        P.wait("tensor", "ac", ac_sq5[k])
        P.wait("tensor", "ve", ve_z5 if k == 0 else ve_srC)
        t1 = 0
        for gi, (o, w) in enumerate(tl_sl):
            t1 = emit_t(lambda e, p=p, o=o, w=w: e.matmul(
                pB[0:1, o:o + w], ones_bf[:], yab_t[:, p * Tl + o:p * Tl + o + w],
                start=True, stop=True), inc=(gi == len(tl_sl) - 1)) or t1
        tm_sum5[k] = t1
        P.wait("vector", "tm", t1)
        ve_srA = emit_v(lambda e: e.tensor_tensor(
            rowA[0:1, 0:Tl], rowA[0:1, 0:Tl], pB[0:1, 0:Tl], ALU.add), inc=True)
        P.wait("tensor", "ve", ve_srA)
        t2 = 0
        for gi, (o, w) in enumerate(tl_sl):
            t2 = emit_t(lambda e, p=p, o=o, w=w: e.matmul(
                pB[0:1, o:o + w], ones_bf[:], x2b_t[:, p * Tl + o:p * Tl + o + w],
                start=True, stop=True), inc=(gi == len(tl_sl) - 1)) or t2
        tm_sq5v[k] = t2
        P.wait("vector", "tm", t2)
        ve_srC = emit_v(lambda e: e.tensor_tensor(
            rowC[0:1, 0:Tl], rowC[0:1, 0:Tl], pB[0:1, 0:Tl], ALU.add), inc=True)

    for ech in range(NCH):
        p = ech % 2
        if ech % 4 == 0:
            P.wait("tensor", f"wl{ech // 4}", wl_w[("o", ech // 4)])
        if ech == 0:
            P.wait("tensor", "lo", ld_oall)
        else:
            P.wait("tensor", "ve", ve_ya[ech - 1])
        for dc in range(NCH):
            for gi, (o, w) in enumerate(tl_sl):
                tm_op = emit_t(lambda e, ech=ech, dc=dc, o=o, w=w: e.matmul(
                    pA[:, o:o + w],
                    wbuf[:, dc * D + ech * 128:dc * D + ech * 128 + 128],
                    oall[:, dc * Tl + o:dc * Tl + o + w],
                    start=(dc == 0), stop=(dc == NCH - 1)),
                    inc=(dc == NCH - 1 and gi == len(tl_sl) - 1)) or tm_op
        if ech >= 1:
            emit_stats5(ech - 1)
        P.wait("vector", "tm", tm_op)
        if ech >= 2:
            P.wait("vector", f"sya{p}", sd_ya[ech - 2])
            P.wait("vector", "ac", ac_sq5[ech - 2])
        ve_ya[ech] = emit_v(lambda e, p=p: e.tensor_copy(
            xf_t[:, p * Tl:(p + 1) * Tl], pA[:, 0:Tl]), inc=True)
        P.wait("sync", "ve", ve_ya[ech])
        sd_ya[ech] = P.dma("sync", f"sya{p}", lambda e, ech=ech, p=p: e.dma_start(
            out=yadram[ech * 128:(ech + 1) * 128, :], in_=xf_t[:, p * Tl:(p + 1) * Tl]))
        if ech >= 2:
            P.wait("vector", "tm", tm_sum5[ech - 2])
        ve_yab = emit_v(lambda e, p=p: e.tensor_copy(
            yab_t[:, p * Tl:(p + 1) * Tl], xf_t[:, p * Tl:(p + 1) * Tl]), inc=True)
        P.wait("scalar", "ve", ve_yab)
        if ech >= 2:
            P.wait("scalar", "tm", tm_sq5v[ech - 2])
        ac_sq5[ech] = emit_s(lambda e, p=p: e.activation(
            out=x2b_t[:, p * Tl:(p + 1) * Tl], in_=xf_t[:, p * Tl:(p + 1) * Tl],
            func=AF.Square), inc=True)
    emit_stats5(NCH - 1)

    ve_mr2 = ln_rows2(Tl, "ve", ve_srC, False)
    ve_b2 = ln_bcast(Tl, ve_mr2)

    sd_x1 = [0] * NCH
    ve_x1b = [0] * NCH
    ac_sq3 = [0] * NCH
    tm_sq3 = 0
    for c in range(NCH):
        p = c % 2
        if c < 2:
            P.wait("sync", "ac", ac_sq5[NCH - 1])
        else:
            P.wait("sync", "ve", ve_x1b[c - 2])
        P.wait("sync", f"sya{p}", sd_ya[NCH - 2 + p] if NCH >= 2 else sd_ya[0])
        P.dma("sync", f"lya{p}", lambda e, c=c, p=p: e.dma_start(
            out=xf_t[:, p * Tl:(p + 1) * Tl], in_=yadram[c * 128:(c + 1) * 128, :]))
        ld_ya = P.ctr[f"lya{p}"]
        if c >= 1:
            P.wait("sync", "sx1", sd_x1[c - 1])
            P.wait("sync", "ac", ac_sq3[c - 1])
        ld_x = P.dma("sync", "lx1", lambda e, c=c: e.dma_start(
            out=x1_t[:, 0:Tl], in_=xT[c * 128:(c + 1) * 128, :]))
        P.wait("vector", f"lya{p}", ld_ya)
        P.wait("vector", "lx1", ld_x)
        if c == 0:
            P.wait("vector", "ve", ve_b2)
        emit_v(lambda e, p=p: e.tensor_tensor(
            xf_t[:, p * Tl:(p + 1) * Tl], xf_t[:, p * Tl:(p + 1) * Tl], rbb[:, 0:Tl], ALU.mult))
        emit_v(lambda e, p=p: e.tensor_tensor(
            xf_t[:, p * Tl:(p + 1) * Tl], xf_t[:, p * Tl:(p + 1) * Tl], mrbb[:, 0:Tl], ALU.subtract))
        emit_v(lambda e, p=p: e.tensor_tensor(
            x1_t[:, 0:Tl], x1_t[:, 0:Tl], xf_t[:, p * Tl:(p + 1) * Tl], ALU.add))
        ve_x1b[c] = emit_v(lambda e, c=c: e.tensor_copy(
            xb_all[:, c * Tl:(c + 1) * Tl], x1_t[:, 0:Tl]), inc=True)
        P.wait("sync", "ve", ve_x1b[c])
        sd_x1[c] = P.dma("sync", "sx1", lambda e, c=c: e.dma_start(
            out=x1dram[c * 128:(c + 1) * 128, :], in_=x1_t[:, 0:Tl]))
        P.wait("scalar", "ve", ve_x1b[c])
        if c >= 1:
            P.wait("scalar", "tm", tm_sq3)
        ac_sq3[c] = emit_s(lambda e, c=c: e.activation(
            out=x2b_t[:, 0:Tl], in_=x1_t[:, 0:Tl], func=AF.Square), inc=True)
        P.wait("tensor", "ac", ac_sq3[c])
        if c == 0:
            P.wait("tensor", "ve", ve_b2)
        for (o, w) in tl_sl:
            emit_t(lambda e, c=c, o=o, w=w: e.matmul(
                pA[0:1, o:o + w], ones_bf[:], xb_all[:, c * Tl + o:c * Tl + o + w],
                start=(c == 0), stop=(c == NCH - 1)))
        for gi, (o, w) in enumerate(tl_sl):
            tm_sq3 = emit_t(lambda e, c=c, o=o, w=w: e.matmul(
                pB[0:1, o:o + w], ones_bf[:], x2b_t[:, o:o + w],
                start=(c == 0), stop=(c == NCH - 1)), inc=(gi == len(tl_sl) - 1)) or tm_sq3

    ve_mr3 = ln_rows2(Tl, "tm", tm_sq3, True)
    ve_b3 = ln_bcast(Tl, ve_mr3)
    P.wait("vector", "ve", ve_b3)
    ve_n3 = 0
    for c in range(NCH):
        emit_v(lambda e, c=c: e.tensor_tensor(
            yab_t[:, 0:Tl], xb_all[:, c * Tl:(c + 1) * Tl], rbb[:, 0:Tl], ALU.mult))
        ve_n3 = emit_v(lambda e, c=c: e.tensor_tensor(
            xb_all[:, c * Tl:(c + 1) * Tl], yab_t[:, 0:Tl], mrbb[:, 0:Tl], ALU.subtract), inc=True)

    n_out = 0
    so_out = {}
    tm_gu_done = {}
    wl_gsl = {}
    ve_nrm_last = 0
    ve_dn_prev = 0
    ve_b4 = 0
    sx1_tot = P.ctr.get("sx1", 0)
    for ee in range(E):
        ecol = ee * TE
        CPS = GW // 128
        ngroups = NSLAB // 2

        def emit_gu_piece(ee, slot, j):
            if ee == 0 and j < CPS:
                P.wait("gpsimd", "tm", tm_op)
            elif j < CPS:
                P.wait("gpsimd", "tm", tm_gu_done[
                    (ee - 1, (ICH - CPS + j) + (0 if slot == 0 else ICH))])
            else:
                P.wait("gpsimd", "tm", tm_gu_done[
                    (ee, (j - CPS) + (0 if slot == 0 else ICH))])
            col = j * 128 + (0 if slot == 0 else I)
            srcw = wgu[ee, :, col:col + 128].rearrange("(dc p) i -> p dc i", p=128)
            sl = gslab[:, slot * NCH * GW:(slot + 1) * NCH * GW].rearrange(
                "p (dc i) -> p dc i", dc=NCH)
            gdst = sl[:, :, (j % CPS) * 128:(j % CPS) * 128 + 128]
            wl_gsl[(ee, slot, j)] = P.dma("gpsimd", f"wg{slot}{j % CPS}",
                                          lambda e, srcw=srcw, gdst=gdst: e.dma_start(out=gdst, in_=srcw))

        for j0 in range(CPS):
            emit_gu_piece(ee, 0, j0)
            emit_gu_piece(ee, 1, j0)

        ac_sg = [0] * ICH
        ve_in = [0] * ICH
        for ihalf in range(ICH):
            for which in ("g", "u"):
                icg = ihalf if which == "g" else ihalf + ICH
                offw = (ihalf % CPS) * 128
                slot = 0 if which == "g" else 1
                base = (ihalf % 2) * 1024 + (0 if which == "g" else 512)
                P.wait("tensor", f"wg{slot}{ihalf % CPS}", wl_gsl[(ee, slot, ihalf)])
                if ee == 0 and ihalf == 0 and which == "g":
                    P.wait("tensor", "ve", ve_n3)
                if ihalf < 2 and ee > 0:
                    if which == "g":
                        P.wait("tensor", "ac", ac_sg_prev[ihalf])
                    else:
                        P.wait("tensor", "ve", ve_in_prev[ihalf])
                if ihalf >= 2:
                    if which == "g":
                        P.wait("tensor", "ac", ac_sg[ihalf - 2])
                    else:
                        P.wait("tensor", "ve", ve_in[ihalf - 2])
                tmg = 0
                for dc in range(NCH):
                    for gi, (o, w) in enumerate(te_sl):
                        tmg = emit_t(lambda e, offw=offw, dc=dc, o=o, w=w,
                                     base=base, slot=slot, ecol=ecol: e.matmul(
                            pA[:, base + o:base + o + w],
                            gslab[:, slot * NCH * GW + dc * GW + offw:
                                  slot * NCH * GW + dc * GW + offw + 128],
                            xb_all[:, dc * Tl + ecol + o:dc * Tl + ecol + o + w],
                            start=(dc == 0), stop=(dc == NCH - 1)),
                            inc=(dc == NCH - 1 and gi == len(te_sl) - 1)) or tmg
                tm_gu_done[(ee, icg)] = tmg
            P.wait("scalar", "tm", tm_gu_done[(ee, ihalf)])
            if ihalf >= 2:
                P.wait("scalar", "ve", ve_in[ihalf - 2])
            ac_sg[ihalf] = emit_s(lambda e, ihalf=ihalf: e.activation(
                out=sg_t[:, (ihalf % 2) * TE:(ihalf % 2 + 1) * TE],
                in_=pA[:, (ihalf % 2) * 1024:(ihalf % 2) * 1024 + TE], func=AF.Silu), inc=True)
            P.wait("vector", "ac", ac_sg[ihalf])
            P.wait("vector", "tm", tm_gu_done[(ee, ihalf + ICH)])
            ve_in[ihalf] = emit_v(lambda e, ihalf=ihalf: e.tensor_tensor(
                inter[:, ihalf * TE:(ihalf + 1) * TE],
                pA[:, (ihalf % 2) * 1024 + 512:(ihalf % 2) * 1024 + 512 + TE],
                sg_t[:, (ihalf % 2) * TE:(ihalf % 2 + 1) * TE], ALU.mult), inc=True)
            if ihalf + CPS < ICH:
                emit_gu_piece(ee, 0, ihalf + CPS)
                emit_gu_piece(ee, 1, ihalf + CPS)

        ac_sg_prev = [ac_sg[ICH - 2], ac_sg[ICH - 1]]
        ve_in_prev = [ve_in[ICH - 2], ve_in[ICH - 1]]

        ve_dn = [0] * NCH
        ac_c6 = [0] * NCH
        ve_c6 = [0] * NCH
        tm_sum6 = [0] * NCH
        tm_sq6 = [0] * NCH
        tm_st6 = 0

        def emit_stats6(k):
            nonlocal tm_st6
            pk = k % 2
            P.wait("tensor", "ac", ac_c6[k])
            if k == 0 and ee == 0:
                P.wait("tensor", "ve", ve_b3)
            if k == 0 and ee > 0:
                P.wait("tensor", "ve", ve_b4)
            t1 = 0
            for gi, (o, w) in enumerate(te_sl):
                t1 = emit_t(lambda e, k=k, pk=pk, o=o, w=w: e.matmul(
                    pB[0:1, 1024 + o:1024 + o + w], ones_bf[:], cp6_t[:, pk * TE + o:pk * TE + o + w],
                    start=(k == 0), stop=(k == NCH - 1)), inc=(gi == len(te_sl) - 1)) or t1
            tm_sum6[k] = t1
            t2 = 0
            for gi, (o, w) in enumerate(te_sl):
                t2 = emit_t(lambda e, k=k, pk=pk, o=o, w=w: e.matmul(
                    pB[0:1, 1536 + o:1536 + o + w], ones_bf[:], sq6_t[:, pk * TE + o:pk * TE + o + w],
                    start=(k == 0), stop=(k == NCH - 1)), inc=(gi == len(te_sl) - 1)) or t2
            tm_sq6[k] = t2
            tm_st6 = t2

        for hc in range(NCH):
            hp2 = hc % 2
            src = wdn[ee, :, hc * 128:(hc + 1) * 128].rearrange("(ic p) hh -> p ic hh", p=128)
            if ee == 0 and hc < 2:
                P.wait("gpsimd", "tm", tm_op)
            elif hc >= 2:
                P.wait("gpsimd", "tm", tm_gu_done[("dn", ee, hc - 2)])
            else:
                P.wait("gpsimd", "tm", tm_gu_done[("dn", ee - 1, NCH - 1)])
            ddst = dslab[:, hp2 * ICH * 128:(hp2 + 1) * ICH * 128].rearrange(
                "p (ic hh) -> p ic hh", ic=ICH)
            wl_dn = P.dma("gpsimd", f"wd{hp2}", lambda e, src=src, ddst=ddst: e.dma_start(
                out=ddst, in_=src))
            P.wait("tensor", f"wd{hp2}", wl_dn)
            if hc == 0:
                P.wait("tensor", "ve", ve_in[ICH - 1])
                if ee > 0:
                    P.wait("tensor", "ve", ve_dn_prev)
            if hc >= 2:
                P.wait("tensor", "ve", ve_dn[hc - 2])
            tm_dn = 0
            for ic in range(ICH):
                for gi, (o, w) in enumerate(te_sl):
                    tm_dn = emit_t(lambda e, hp2=hp2, ic=ic, o=o, w=w: e.matmul(
                        pB[:, hp2 * 512 + o:hp2 * 512 + o + w],
                        dslab[:, hp2 * ICH * 128 + ic * 128:hp2 * ICH * 128 + ic * 128 + 128],
                        inter[:, ic * TE + o:ic * TE + o + w],
                        start=(ic == 0), stop=(ic == ICH - 1)),
                        inc=(ic == ICH - 1 and gi == len(te_sl) - 1)) or tm_dn
            tm_gu_done[("dn", ee, hc)] = tm_dn
            if hc >= 1:
                emit_stats6(hc - 1)
            P.wait("vector", "tm", tm_dn)
            if ee > 0 and hc == 0:
                P.wait("vector", "ve", ve_nrm_last)
            ve_dn[hc] = emit_v(lambda e, hc=hc, hp2=hp2: e.tensor_copy(
                yf_e[:, hc * TE:(hc + 1) * TE], pB[:, hp2 * 512:hp2 * 512 + TE]), inc=True)
            if hc >= 2:
                P.wait("vector", "tm", tm_sum6[hc - 2])
            ve_c6[hc] = emit_v(lambda e, hc=hc, hp2=hp2: e.tensor_copy(
                cp6_t[:, hp2 * TE:(hp2 + 1) * TE], yf_e[:, hc * TE:(hc + 1) * TE]), inc=True)
            P.wait("scalar", "ve", ve_c6[hc])
            if hc >= 2:
                P.wait("scalar", "tm", tm_sq6[hc - 2])
            ac_c6[hc] = emit_s(lambda e, hc=hc, hp2=hp2: e.activation(
                out=sq6_t[:, hp2 * TE:(hp2 + 1) * TE], in_=yf_e[:, hc * TE:(hc + 1) * TE],
                func=AF.Square), inc=True)
        emit_stats6(NCH - 1)
        ve_dn_prev = ve_dn[NCH - 1]

        P.wait("vector", "tm", tm_st6)
        emit_v(lambda e: e.tensor_scalar_mul(rowA[0:1, 0:TE], pB[0:1, 1024:1024 + TE], konst[0:1, 0:1]))
        v1 = emit_v(lambda e: e.tensor_scalar_mul(rowC[0:1, 0:TE], pB[0:1, 1536:1536 + TE], konst[0:1, 0:1]), inc=True)
        P.wait("scalar", "ve", v1)
        a1 = emit_s(lambda e: e.activation(out=mrbb[0:1, 0:TE], in_=rowA[0:1, 0:TE], func=AF.Square), inc=True)
        P.wait("vector", "ac", a1)
        v2 = emit_v(lambda e: e.tensor_tensor(rowC[0:1, 0:TE], rowC[0:1, 0:TE], mrbb[0:1, 0:TE], ALU.subtract), inc=True)
        P.wait("scalar", "ve", v2)
        emit_s(lambda e: e.activation(out=rowC[0:1, 0:TE], in_=rowC[0:1, 0:TE], func=AF.Ln, bias=f32(cfg.EPS), scale=f32(1.0)))
        a2 = emit_s(lambda e: e.activation(out=rowC[0:1, 0:TE], in_=rowC[0:1, 0:TE], func=AF.Exp, bias=f32(0.0), scale=f32(-0.5)), inc=True)
        P.wait("vector", "ac", a2)
        v3 = emit_v(lambda e: e.tensor_tensor(rowA[0:1, 0:TE], rowA[0:1, 0:TE], rowC[0:1, 0:TE], ALU.mult), inc=True)
        P.wait("tensor", "ve", v3)
        for (o, w) in te_sl:
            emit_t(lambda e, o=o, w=w: e.matmul(
                pB[:, 1024 + o:1024 + o + w], ones_row[:], rowC[0:1, o:o + w], start=True, stop=True))
        tm_b4 = 0
        for gi, (o, w) in enumerate(te_sl):
            tm_b4 = emit_t(lambda e, o=o, w=w: e.matmul(
                pB[:, 1536 + o:1536 + o + w], ones_row[:], rowA[0:1, o:o + w],
                start=True, stop=True), inc=(gi == len(te_sl) - 1)) or tm_b4
        P.wait("vector", "tm", tm_b4)
        emit_v(lambda e: e.tensor_copy(rbb[:, 0:TE], pB[:, 1024:1024 + TE]))
        ve_b4 = emit_v(lambda e: e.tensor_copy(mrbb[:, 0:TE], pB[:, 1536:1536 + TE]), inc=True)

        ve_nrm = {}
        for c in range(NCH):
            slot = c % 4
            if ee == 0 and c == 0:
                P.wait("sync", "ac", ac_sq3[NCH - 1])
                P.wait("sync", "sx1", sx1_tot)
            if c >= 4:
                P.wait("sync", "ve", ve_nrm[c - 4])
            elif ee > 0:
                P.wait("sync", "ve", ve_nrm_last)
            ld_x1c = P.dma("sync", f"lx6{slot}", lambda e, c=c, ecol=ecol, slot=slot: e.dma_start(
                out=x1_t[:, slot * TE:(slot + 1) * TE],
                in_=x1dram[c * 128:(c + 1) * 128, ecol:ecol + TE]))
            P.wait("vector", f"lx6{slot}", ld_x1c)
            if c == 0:
                P.wait("vector", "ve", ve_b4)
            osel = n_out % 2
            if n_out >= 2:
                P.wait("vector", f"sout{osel}", so_out[n_out - 2])
            emit_v(lambda e, c=c: e.tensor_tensor(
                o1_t[:, 0:TE], yf_e[:, c * TE:(c + 1) * TE], rbb[:, 0:TE], ALU.mult))
            emit_v(lambda e: e.tensor_tensor(
                o1_t[:, 0:TE], o1_t[:, 0:TE], mrbb[:, 0:TE], ALU.subtract))
            ve_nrm[c] = emit_v(lambda e, slot=slot, osel=osel: e.tensor_tensor(
                out_t[:, osel * TE:(osel + 1) * TE],
                x1_t[:, slot * TE:(slot + 1) * TE], o1_t[:, 0:TE], ALU.add), inc=True)
            P.wait("sync", "ve", ve_nrm[c])
            so_out[n_out] = P.dma("sync", f"sout{osel}", lambda e, c=c, ecol=ecol, osel=osel: e.dma_start(
                out=outT[c * 128:(c + 1) * 128, ecol:ecol + TE],
                in_=out_t[:, osel * TE:(osel + 1) * TE]))
            n_out += 1
        ve_nrm_last = ve_nrm[NCH - 1]

    P.wait("sync", "sout0", P.ctr.get("sout0", 0))
    P.wait("sync", "sout1", P.ctr.get("sout1", 0))
    P.wait("gpsimd", "sout0", P.ctr.get("sout0", 0))
    P.wait("gpsimd", "sout1", P.ctr.get("sout1", 0))

    with nc.Block() as block:
        @block.sync
        def _(e):
            for fn in P.ops["sync"]:
                fn(e)

        @block.gpsimd
        def _(e):
            for fn in P.ops["gpsimd"]:
                fn(e)

        @block.tensor
        def _(e):
            for fn in P.ops["tensor"]:
                fn(e)

        @block.vector
        def _(e):
            for fn in P.ops["vector"]:
                fn(e)

        @block.scalar
        def _(e):
            for fn in P.ops["scalar"]:
                fn(e)

    return nc


def _prep_core_inputs(x, wq, wk, wv, wo, gup, dnp, cfg, core):
    b = core // cfg.R
    t = core % cfg.R
    perm = np.concatenate([np.arange(e, cfg.Tl, cfg.E) for e in range(cfg.E)])
    sl = x[b, t * cfg.Tl:(t + 1) * cfg.Tl, :][perm, :]
    return {
        "xT": np.ascontiguousarray(sl.T),
        "wqT": np.ascontiguousarray(wq.T),
        "wkT": np.ascontiguousarray(wk.T),
        "wvT": np.ascontiguousarray(wv.T),
        "woT": np.ascontiguousarray(wo.T),
        "wgu": np.ascontiguousarray(gup),
        "wdn": np.ascontiguousarray(dnp),
    }, perm


def kernel(L0_attn_x, L0_attn_w_q, L0_attn_w_k, L0_attn_w_v, L0_attn_w_o,
           L0_ffn_gate_up_proj, L0_ffn_down_proj):
    cfg = CFG()
    x = np.asarray(L0_attn_x, dtype=np.float32)
    wq = np.asarray(L0_attn_w_q, dtype=np.float32)
    wk = np.asarray(L0_attn_w_k, dtype=np.float32)
    wv = np.asarray(L0_attn_w_v, dtype=np.float32)
    wo = np.asarray(L0_attn_w_o, dtype=np.float32)
    gup = np.asarray(L0_ffn_gate_up_proj, dtype=np.float32)
    dnp = np.asarray(L0_ffn_down_proj, dtype=np.float32)

    in_maps = []
    perms = []
    for core in range(8):
        m, perm = _prep_core_inputs(x, wq, wk, wv, wo, gup, dnp, cfg, core)
        in_maps.append(m)
        perms.append(perm)

    nc = build_nc(cfg)
    res = run_bass_kernel_spmd(nc, in_maps, core_ids=list(range(8)))

    out = np.empty((2, 8192, 2048), dtype=np.float32)
    for core in range(8):
        b = core // cfg.R
        t = core % cfg.R
        shard = np.asarray(res.results[core]["outT"], dtype=np.float32).T
        sl = np.empty_like(shard)
        sl[perms[core], :] = shard
        out[b, t * cfg.Tl:(t + 1) * cfg.Tl, :] = sl
    return out


# version 20
# speedup vs baseline: 1.0061x; 1.0061x over previous
import numpy as np

import concourse.bass as bass
import concourse.mybir as mybir
from concourse.bass_utils import run_bass_kernel_spmd

F32 = mybir.dt.float32
BF16 = mybir.dt.bfloat16
AF = mybir.ActivationFunctionType
ALU = mybir.AluOpType
f32 = np.float32

ENGINES = ("sync", "gpsimd", "tensor", "vector", "scalar")
NW = 512


class CFG:
    def __init__(self, D=2048, Tl=2048, H=16, E=4, I=2048, R=4):
        self.D, self.Tl, self.H, self.E, self.I, self.R = D, Tl, H, E, I, R
        self.HD = 128
        assert H * self.HD == D
        self.S = R * Tl
        self.NCH = D // 128
        self.TCH = Tl // 128
        self.KCH = self.S // 128
        self.QW = min(1024, Tl)
        self.NQ = Tl // self.QW
        self.TE = Tl // E
        self.ICH = I // 128
        self.GW = min(512, I)
        self.NSLAB = (2 * I) // self.GW
        self.EPS = 1e-5
        self.SCALE = 1.0 / float(np.sqrt(self.HD))
        self.SHIFT = -5.0


class Plan:
    def __init__(self, sems):
        self.ops = {e: [] for e in ENGINES}
        self.ctr = {k: 0 for k in sems}
        self.sems = sems

    def emit(self, eng, fn):
        self.ops[eng].append(fn)

    def inc(self, sem, n=1):
        self.ctr[sem] += n
        return self.ctr[sem]

    def wait(self, eng, sem, v):
        if v > 0:
            s = self.sems[sem]
            self.ops[eng].append(lambda e, s=s, v=v: e.wait_ge(s, v))

    def drain(self, eng):
        self.ops[eng].append(lambda e: e.drain())


def _sl(width, nw=NW):
    out, o = [], 0
    while o < width:
        w = min(nw, width - o)
        out.append((o, w))
        o += w
    return out


def build_nc(cfg: CFG):
    nc = bass.Bass("TRN2", target_bir_lowering=False)
    D, Tl, H, E, I, R = cfg.D, cfg.Tl, cfg.H, cfg.E, cfg.I, cfg.R
    NCH, TCH, KCH, QW, NQ, TE, ICH = (cfg.NCH, cfg.TCH, cfg.KCH, cfg.QW,
                                      cfg.NQ, cfg.TE, cfg.ICH)
    S, GW, NSLAB = cfg.S, cfg.GW, cfg.NSLAB
    HP = H // 2

    xT = nc.dram_tensor("xT", [D, Tl], F32, kind="ExternalInput").ap()
    wqT = nc.dram_tensor("wqT", [D, D], F32, kind="ExternalInput").ap()
    wkT = nc.dram_tensor("wkT", [D, D], F32, kind="ExternalInput").ap()
    wvT = nc.dram_tensor("wvT", [D, D], F32, kind="ExternalInput").ap()
    woT = nc.dram_tensor("woT", [D, D], F32, kind="ExternalInput").ap()
    wgu = nc.dram_tensor("wgu", [E, D, 2 * I], F32, kind="ExternalInput").ap()
    wdn = nc.dram_tensor("wdn", [E, I, D], F32, kind="ExternalInput").ap()
    outT = nc.dram_tensor("outT", [D, Tl], F32, kind="ExternalOutput").ap()

    kloc = nc.dram_tensor("kloc", [D, Tl], BF16)
    vloc = nc.dram_tensor("vloc", [Tl, D], BF16)
    kgat = nc.dram_tensor("kgat", [R * D, Tl], BF16)
    vgat = nc.dram_tensor("vgat", [R * Tl, D], BF16)
    qdram = nc.dram_tensor("qdram", [D, Tl], BF16).ap()
    odram = nc.dram_tensor("odram", [D, Tl], BF16).ap()
    yadram = nc.dram_tensor("yadram", [D, Tl], F32).ap()
    x1dram = nc.dram_tensor("x1dram", [D, Tl], F32).ap()

    def sb(name, shape, dtype, off):
        return nc.alloc_sbuf_tensor_at(name, list(shape), dtype, offset=ARENA + off)

    A0 = 0
    a_sz = max(NCH * Tl * 2, 2 * 2 * S * 2)
    B0 = A0 + a_sz
    b_p6 = [("gsl", 2 * NCH * GW * 2), ("dsl", 2 * ICH * 128 * 2),
            ("sg", 2 * TE * 4), ("intr", ICH * TE * 2), ("outt", 2 * TE * 4)]
    b_sz = max(NCH * D * 2, sum(s for _, s in b_p6))
    C0 = B0 + b_sz
    c_sz = max(2 * Tl * 4 + 2 * Tl * 2 + 2 * Tl * 2, 2 * D * 2, KCH * 256 * 2, NCH * TE * 4)
    D0 = C0 + c_sz
    E0 = D0 + Tl * 4
    F0 = E0 + 2 * Tl * 4
    G0 = F0 + 2 * Tl * 4
    need = G0 + 1152 + TE * 4 + 4 * TE * 2
    assert need <= nc.sbuf_bytes_remaining, (need, nc.sbuf_bytes_remaining)
    _resv = nc.alloc_sbuf_tensor("arena_reserve", [128, need], mybir.dt.uint8)
    ARENA = nc.lookup_mloc(_resv).addr

    xb_all = sb("xb_all", [128, NCH * Tl], BF16, A0)
    kp_buf = sb("kp_buf", [128, 2 * 2 * S], BF16, A0)
    oall = sb("oall", [128, NCH * Tl], BF16, A0)

    wbuf = sb("wbuf", [128, NCH * D], BF16, B0)
    off = B0
    gslab = sb("gslab", [128, 2 * NCH * GW], BF16, off); off += b_p6[0][1]
    dslab = sb("dslab", [128, 2 * ICH * 128], BF16, off); off += b_p6[1][1]
    sg_t = sb("sg_t", [128, 2 * TE], F32, off); off += b_p6[2][1]
    inter = sb("inter", [128, ICH * TE], BF16, off); off += b_p6[3][1]
    out_t = sb("out_t", [128, 2 * TE], F32, off)

    xf_t = sb("xf_t", [128, 2 * Tl], F32, C0)
    x2b_t = sb("x2b_t", [128, 2 * Tl], BF16, C0 + 2 * Tl * 4)
    y1_t = sb("y1_t", [128, Tl], BF16, C0 + 2 * Tl * 4 + 2 * Tl * 2)
    ev_t = sb("ev_t", [128, 2 * D], BF16, C0)
    vp_buf = sb("vp_buf", [128, KCH * 256], BF16, C0)
    yab_t = sb("yab_t", [128, 2 * Tl], BF16, C0 + 2 * Tl * 4 + 2 * Tl * 2)
    yf_e = sb("yf_e", [128, NCH * TE], F32, C0)

    x1_t = sb("x1_t", [128, Tl], F32, D0)
    qh_buf = sb("qh_buf", [128, 2 * Tl], BF16, D0)
    rowA = sb("rowA", [1, Tl], F32, E0)
    pt_buf = sb("pt_buf", [128, 8 * QW], BF16, E0)
    rowC = sb("rowC", [1, Tl], F32, E0 + Tl * 4)
    rbb = sb("rbb", [128, Tl], F32, F0)
    ot_buf = sb("ot_buf", [128, 2 * QW], BF16, F0)
    mrbb = sb("mrbb", [128, Tl], F32, F0 + Tl * 4)
    accD = sb("accD", [128, QW], F32, F0 + Tl * 4)
    accP = sb("accP", [128, QW], F32, F0 + Tl * 4 + QW * 4)
    ones_bf = sb("ones_bf", [128, 1], BF16, G0)
    ones_row = sb("ones_row", [1, 128], F32, G0 + 32)
    konst = sb("konst", [1, 4], F32, G0 + 544)
    ones128b = sb("ones128b", [128, 128], BF16, G0 + 576)
    o1_t = sb("o1_t", [128, TE], F32, G0 + 1152)
    accB = sb("accB", [128, QW], BF16, G0 + 1152)
    sq6_t = sb("sq6_t", [128, 2 * TE], BF16, G0 + 1152 + TE * 4)
    cp6_t = sb("cp6_t", [128, 2 * TE], BF16, G0 + 1152 + TE * 4 + 2 * TE * 2)
    linv = sb("linv", [128, QW], F32, G0 + 1152 + TE * 4)

    pA = nc.alloc_psum_tensor("pA", [128, 2048], F32)
    pB = nc.alloc_psum_tensor("pB", [128, 2048], F32)

    class LazySems(dict):
        def __missing__(self, k):
            self[k] = nc.alloc_semaphore(k)
            return self[k]

    sems = LazySems()

    class Plan2:
        def __init__(self):
            self.ops = {e: [] for e in ENGINES}
            self.ctr = {}
            self.log = {e: [] for e in ENGINES}

        def emit(self, eng, fn):
            self.ops[eng].append(fn)

        def inc(self, sem, n=1, eng=None):
            self.ctr[sem] = self.ctr.get(sem, 0) + n
            if eng is not None:
                self.log[eng].append(("inc", sem, n, False))
            return self.ctr[sem]

        def wait(self, eng, sem, v):
            if v > 0:
                s = sems[sem]
                self.ops[eng].append(lambda e, s=s, v=v: e.wait_ge(s, v))
                self.log[eng].append(("wait", sem, v, False))

        def dma(self, eng, sem, fn):
            s = sems[sem]
            self.ops[eng].append(lambda e, fn=fn, s=s: fn(e).then_inc(s, 16))
            self.log[eng].append(("inc", sem, 16, True))
            return self.inc(sem, 16)

        def drain(self, eng):
            self.ops[eng].append(lambda e: e.drain())

    P = Plan2()
    tl_sl = _sl(Tl)
    d_sl = _sl(D)
    q_sl = _sl(QW)
    te_sl = _sl(TE)

    def emit_v(fn, inc=False):
        if inc:
            P.emit("vector", lambda e, fn=fn: fn(e).then_inc(sems["ve"], 1))
            return P.inc("ve", eng="vector")
        P.emit("vector", lambda e, fn=fn: fn(e))
        return None

    def emit_s(fn, inc=False):
        if inc:
            P.emit("scalar", lambda e, fn=fn: fn(e).then_inc(sems["ac"], 1))
            return P.inc("ac", eng="scalar")
        P.emit("scalar", lambda e, fn=fn: fn(e))
        return None

    def emit_t(fn, inc=False):
        if inc:
            P.emit("tensor", lambda e, fn=fn: fn(e).then_inc(sems["tm"], 1))
            return P.inc("tm", eng="tensor")
        P.emit("tensor", lambda e, fn=fn: fn(e))
        return None

    def emit_p(fn, inc=False):
        if inc:
            P.emit("gpsimd", lambda e, fn=fn: fn(e).then_inc(sems["po"], 1))
            return P.inc("po", eng="gpsimd")
        P.emit("gpsimd", lambda e, fn=fn: fn(e))
        return None

    def ln_rows2(width, wait_sem, wait_val, from_psum):
        P.wait("vector", wait_sem, wait_val)
        if from_psum:
            emit_v(lambda e: e.tensor_scalar_mul(rowA[0:1, 0:width], pA[0:1, 0:width], konst[0:1, 0:1]))
            v1 = emit_v(lambda e: e.tensor_scalar_mul(rowC[0:1, 0:width], pB[0:1, 0:width], konst[0:1, 0:1]), inc=True)
        else:
            emit_v(lambda e: e.tensor_scalar_mul(rowA[0:1, 0:width], rowA[0:1, 0:width], konst[0:1, 0:1]))
            v1 = emit_v(lambda e: e.tensor_scalar_mul(rowC[0:1, 0:width], rowC[0:1, 0:width], konst[0:1, 0:1]), inc=True)
        P.wait("scalar", "ve", v1)
        a1 = emit_s(lambda e: e.activation(out=mrbb[0:1, 0:width], in_=rowA[0:1, 0:width], func=AF.Square), inc=True)
        P.wait("vector", "ac", a1)
        v2 = emit_v(lambda e: e.tensor_tensor(rowC[0:1, 0:width], rowC[0:1, 0:width], mrbb[0:1, 0:width], ALU.subtract), inc=True)
        P.wait("scalar", "ve", v2)
        emit_s(lambda e: e.activation(out=rowC[0:1, 0:width], in_=rowC[0:1, 0:width], func=AF.Ln, bias=f32(cfg.EPS), scale=f32(1.0)))
        a2 = emit_s(lambda e: e.activation(out=rowC[0:1, 0:width], in_=rowC[0:1, 0:width], func=AF.Exp, bias=f32(0.0), scale=f32(-0.5)), inc=True)
        P.wait("vector", "ac", a2)
        v3 = emit_v(lambda e: e.tensor_tensor(rowA[0:1, 0:width], rowA[0:1, 0:width], rowC[0:1, 0:width], ALU.mult), inc=True)
        return v3

    def ln_bcast(width, ve_mr):
        P.wait("tensor", "ve", ve_mr)
        sl = _sl(width)
        for (o, w) in sl:
            emit_t(lambda e, o=o, w=w: e.matmul(pA[:, o:o + w], ones_row[:], rowC[0:1, o:o + w], start=True, stop=True))
        t1 = 0
        for gi, (o, w) in enumerate(sl):
            t1 = emit_t(lambda e, o=o, w=w: e.matmul(pB[:, o:o + w], ones_row[:], rowA[0:1, o:o + w], start=True, stop=True),
                        inc=(gi == len(sl) - 1))
        P.wait("vector", "tm", t1)
        emit_v(lambda e: e.tensor_copy(rbb[:, 0:width], pA[:, 0:width]))
        vb = emit_v(lambda e: e.tensor_copy(mrbb[:, 0:width], pB[:, 0:width]), inc=True)
        return vb

    emit_v(lambda e: e.memset(ones_bf[:], 1.0))
    emit_v(lambda e: e.memset(ones_row[:], 1.0))
    emit_v(lambda e: e.memset(ones128b[:], 1.0))
    emit_v(lambda e: e.memset(konst[0:1, 0:1], 1.0 / D), inc=True)

    ve_cast = [0] * NCH
    tm_st1 = [0] * NCH
    ldx = [0] * NCH
    for c in range(NCH):
        p = c % 2
        if c >= 2:
            P.wait("sync", "ve", ve_cast[c - 2])
        ldx[c] = P.dma("sync", f"ldx{p}", lambda e, c=c, p=p: e.dma_start(
            out=xf_t[:, p * Tl:(p + 1) * Tl], in_=xT[c * 128:(c + 1) * 128, :]))
        P.wait("vector", f"ldx{p}", ldx[c])
        ve_cast[c] = emit_v(lambda e, c=c, p=p: e.tensor_copy(
            xb_all[:, c * Tl:(c + 1) * Tl], xf_t[:, p * Tl:(p + 1) * Tl]), inc=True)
        P.wait("scalar", "ve", ve_cast[c])
        if c >= 1:
            P.wait("scalar", "tm", tm_st1[c - 1])
        ac1 = emit_s(lambda e, c=c: e.activation(
            out=x2b_t[:, 0:Tl], in_=xb_all[:, c * Tl:(c + 1) * Tl], func=AF.Square), inc=True)
        P.wait("tensor", "ac", ac1)
        for (o, w) in tl_sl:
            emit_t(lambda e, c=c, o=o, w=w: e.matmul(
                pA[0:1, o:o + w], ones_bf[:], xb_all[:, c * Tl + o:c * Tl + o + w],
                start=(c == 0), stop=(c == NCH - 1)))
        for gi, (o, w) in enumerate(tl_sl):
            tm_st1[c] = emit_t(lambda e, c=c, o=o, w=w: e.matmul(
                pB[0:1, o:o + w], ones_bf[:], x2b_t[:, o:o + w],
                start=(c == 0), stop=(c == NCH - 1)), inc=(gi == len(tl_sl) - 1))

    ve_mr1 = ln_rows2(Tl, "tm", tm_st1[NCH - 1], True)
    ve_b1 = ln_bcast(Tl, ve_mr1)
    P.wait("vector", "ve", ve_b1)
    ve_n1 = 0
    for c in range(NCH):
        emit_v(lambda e, c=c: e.tensor_tensor(
            y1_t[:, 0:Tl], xb_all[:, c * Tl:(c + 1) * Tl], rbb[:, 0:Tl], ALU.mult))
        ve_n1 = emit_v(lambda e, c=c: e.tensor_tensor(
            xb_all[:, c * Tl:(c + 1) * Tl], y1_t[:, 0:Tl], mrbb[:, 0:Tl], ALU.subtract), inc=True)

    wl_w = {}

    def load_weight(tag, src_ap, gate_fn):
        dstf = wbuf[:].rearrange("p (dc e) -> p dc e", dc=NCH)
        for g in range(4):
            P.wait("gpsimd", "tm", gate_fn(g))
            src = src_ap[:, g * 512:(g + 1) * 512].rearrange("(dc p) e -> p dc e", p=128)
            dst = dstf[:, :, g * 512:(g + 1) * 512]
            wl_w[(tag, g)] = P.dma("gpsimd", f"wl{g}", lambda e, src=src, dst=dst: e.dma_start(out=dst, in_=src))

    tm_proj = {}
    ve_evac = {}
    sd_store = {}

    def emit_proj_fm(tag, dst_rows_fn, prev_tag=None):
        for ech in range(NCH):
            reg = pA if ech % 2 == 0 else pB
            p = ech % 2
            if ech % 4 == 0:
                P.wait("tensor", f"wl{ech // 4}", wl_w[(tag, ech // 4)])
            if ech == 0:
                P.wait("tensor", "ve", ve_n1)
            if ech < 2 and prev_tag is not None:
                P.wait("tensor", "ve", ve_evac[(prev_tag, NCH - 2 + ech)])
            if ech >= 2:
                P.wait("tensor", "ve", ve_evac[(tag, ech - 2)])
            tmv = 0
            for dc in range(NCH):
                for gi, (o, w) in enumerate(tl_sl):
                    tmv = emit_t(lambda e, ech=ech, dc=dc, o=o, w=w, reg=reg: e.matmul(
                        reg[:, o:o + w],
                        wbuf[:, dc * D + ech * 128:dc * D + ech * 128 + 128],
                        xb_all[:, dc * Tl + o:dc * Tl + o + w],
                        start=(dc == 0), stop=(dc == NCH - 1)),
                        inc=(dc == NCH - 1 and gi == len(tl_sl) - 1)) or tmv
            tm_proj[(tag, ech)] = tmv
            P.wait("vector", "tm", tmv)
            if ech >= 2:
                P.wait("vector", f"sde{p}", sd_store[(tag, ech - 2)])
            elif prev_tag is not None:
                P.wait("vector", f"sde{p}", sd_store[(prev_tag, NCH - 2 + ech)])
            ve_evac[(tag, ech)] = emit_v(lambda e, p=p, reg=reg: e.tensor_copy(
                ev_t[:, p * D:p * D + Tl], reg[:, 0:Tl]), inc=True)
            P.wait("sync", "ve", ve_evac[(tag, ech)])
            sd_store[(tag, ech)] = P.dma("sync", f"sde{p}", lambda e, ech=ech, p=p, fn=dst_rows_fn: e.dma_start(
                out=fn(ech), in_=ev_t[:, p * D:p * D + Tl]))

    rg = [[0, 1, 2, 3], [4, 5, 6, 7]]
    load_weight("k", wkT, lambda g: 0)
    emit_proj_fm("k", lambda ech: kloc.ap()[ech * 128:(ech + 1) * 128, :])
    P.wait("gpsimd", "sde0", P.ctr.get("sde0", 0))
    P.wait("gpsimd", "sde1", P.ctr.get("sde1", 0))
    P.emit("gpsimd", lambda e: e.collective_compute(
        "AllGather", ALU.bypass, replica_groups=rg,
        ins=[kloc.ap().opt()], outs=[kgat.ap().opt()]).then_inc(sems["cc"], 1))
    P.inc("cc", eng="gpsimd")
    load_weight("v", wvT, lambda g: tm_proj[("k", 4 * g + 3)])

    for tt in range(TCH):
        reg = pA if tt % 2 == 0 else pB
        p = tt % 2
        if tt == 0:
            for g4 in range(4):
                P.wait("tensor", f"wl{g4}", wl_w[("v", g4)])
        if tt < 2:
            P.wait("tensor", "ve", ve_evac[("k", NCH - 2 + tt)])
        if tt >= 2:
            P.wait("tensor", "ve", ve_evac[("v", tt - 2)])
        tmv = 0
        for dc in range(NCH):
            for gi, (o, w) in enumerate(d_sl):
                tmv = emit_t(lambda e, tt=tt, dc=dc, o=o, w=w, reg=reg: e.matmul(
                    reg[:, o:o + w],
                    xb_all[:, dc * Tl + tt * 128:dc * Tl + tt * 128 + 128],
                    wbuf[:, dc * D + o:dc * D + o + w],
                    start=(dc == 0), stop=(dc == NCH - 1)),
                    inc=(dc == NCH - 1 and gi == len(d_sl) - 1)) or tmv
        tm_proj[("v", tt)] = tmv
        P.wait("vector", "tm", tmv)
        if tt >= 2:
            P.wait("vector", f"sde{p}", sd_store[("v", tt - 2)])
        else:
            P.wait("vector", f"sde{p}", sd_store[("k", NCH - 2 + tt)])
        ve_evac[("v", tt)] = emit_v(lambda e, p=p, reg=reg: e.tensor_copy(
            ev_t[:, p * D:(p + 1) * D], reg[:, 0:D]), inc=True)
        P.wait("sync", "ve", ve_evac[("v", tt)])
        sd_store[("v", tt)] = P.dma("sync", f"sde{p}", lambda e, tt=tt, p=p: e.dma_start(
            out=vloc.ap()[tt * 128:(tt + 1) * 128, :], in_=ev_t[:, p * D:(p + 1) * D]))

    P.wait("gpsimd", "sde0", P.ctr.get("sde0", 0))
    P.wait("gpsimd", "sde1", P.ctr.get("sde1", 0))
    P.emit("gpsimd", lambda e: e.collective_compute(
        "AllGather", ALU.bypass, replica_groups=rg,
        ins=[vloc.ap().opt()], outs=[vgat.ap().opt()]).then_inc(sems["cc"], 1))
    cc_v = P.inc("cc", eng="gpsimd")
    load_weight("q", wqT, lambda g: tm_proj[("v", TCH - 1)])
    emit_proj_fm("q", lambda ech: qdram[ech * 128:(ech + 1) * 128, :], prev_tag="v")
    sde_tot = [P.ctr.get("sde0", 0), P.ctr.get("sde1", 0)]
    load_weight("o", woT, lambda g: tm_proj[("q", 4 * g + 3)])

    kgat_r = kgat.ap().rearrange("(r d) t -> d r t", r=R)
    vgat_ap = vgat.ap()
    NVQ = 4
    JQ = KCH // NVQ

    def pool_own(jj):
        return jj % 3 == 0 and jj < 63

    ld_k = [0] * HP
    ld_v = [[0] * NVQ for _ in range(HP)]
    ld_qh = [0] * H
    tm_sc_g = {}
    tm_pv_g = {}
    ac_g = {}
    po_g = {}
    ve_add_g = {}
    ve_acs = {}
    tm_col = {}
    ve_rc = {}
    ve_ot = {}
    sd_od = {}

    def emit_kload(hp):
        slot = hp % 2
        for hh in range(2):
            h = hp * 2 + hh
            kdst = kp_buf[:, (slot * 2 + hh) * S:(slot * 2 + hh + 1) * S].rearrange(
                "p (r t) -> p r t", r=R)
            ld_k[hp] = P.dma("sync", f"lk{slot}", lambda e, h=h, kdst=kdst: e.dma_start(
                out=kdst, in_=kgat_r[h * 128:(h + 1) * 128, :, :]))

    def emit_vload(hp, eng):
        for vq in range(NVQ):
            vsrc = vgat_ap[vq * (S // NVQ):(vq + 1) * (S // NVQ),
                           hp * 256:(hp + 1) * 256].rearrange("(j p) u -> p j u", p=128)
            vdst = vp_buf[:, vq * JQ * 256:(vq + 1) * JQ * 256].rearrange(
                "p (j u) -> p j u", j=JQ)
            ld_v[hp][vq] = P.dma(eng, f"lv{vq}", lambda e, vsrc=vsrc, vdst=vdst: e.dma_start(
                out=vdst, in_=vsrc))

    tm_v_done = tm_proj[("q", NCH - 1)]
    P.wait("sync", "tm", tm_v_done)
    P.wait("sync", "cc", cc_v)
    P.wait("sync", "sde0", sde_tot[0])
    P.wait("sync", "sde1", sde_tot[1])
    emit_kload(0)
    emit_vload(0, "sync")
    if HP > 1:
        emit_kload(1)

    it = 0
    for hp in range(HP):
        slot = hp % 2
        for hh in range(2):
            h = hp * 2 + hh
            hq = h % 2
            if h >= 2:
                P.wait("sync", "tm",
                       tm_sc_g[((h - 2) * NQ + NQ - 1) * KCH + KCH - 1])
            ld_qh[h] = P.dma("sync", f"lq{hq}", lambda e, h=h, hq=hq: e.dma_start(
                out=qh_buf[:, hq * Tl:(hq + 1) * Tl], in_=qdram[h * 128:(h + 1) * 128, :]))

            for qh in range(NQ):
                qb = qh * QW
                bg = it * KCH
                s_pv = (it % 2) * QW
                it_loc = it

                def emit_scores_exp_pair(mm, bg=None, qb=None, hh=None, hq=None, slot=None, hp=None, h=None):
                    for jj in (2 * mm, 2 * mm + 1):
                        g = bg + jj
                        if g >= 2:
                            P.wait("tensor", "ac", ac_g[g - 2])
                        if jj == 0:
                            P.wait("tensor", f"lk{slot}", ld_k[hp])
                            P.wait("tensor", f"lq{hq}", ld_qh[h])
                        ps = (g % 2) * QW
                        for gi, (o, w) in enumerate(q_sl):
                            r = emit_t(lambda e, hh=hh, jj=jj, o=o, w=w, qb=qb, hq=hq,
                                       slot=slot, ps=ps: e.matmul(
                                pA[:, ps + o:ps + o + w],
                                kp_buf[:, (slot * 2 + hh) * S + jj * 128:(slot * 2 + hh) * S + jj * 128 + 128],
                                qh_buf[:, hq * Tl + qb + o:hq * Tl + qb + o + w],
                                start=True, stop=True),
                                inc=(gi == len(q_sl) - 1))
                            if r:
                                tm_sc_g[g] = r
                    for jj in (2 * mm, 2 * mm + 1):
                        g = bg + jj
                        P.wait("scalar", "tm", tm_sc_g[g])
                        if jj % 2 == 0 and g >= 8:
                            P.wait("scalar", "tm", tm_pv_g[g - 7])
                            pov = [po_g[c] for c in (g - 8, g - 7) if pool_own(c % KCH)]
                            vev = [ve_add_g[c] for c in (g - 8, g - 7) if not pool_own(c % KCH)]
                            if pov:
                                P.wait("scalar", "po", max(pov))
                            if vev:
                                P.wait("scalar", "ve", max(vev))
                        ac_g[g] = emit_s(lambda e, g=g: e.activation(
                            out=pt_buf[:, (g % 8) * QW:(g % 8) * QW + QW],
                            in_=pA[:, (g % 2) * QW:(g % 2) * QW + QW],
                            func=AF.Exp, bias=f32(cfg.SHIFT), scale=f32(cfg.SCALE)), inc=True)

                def emit_pv_acc_pair(mm, bg=None, hh=None, s_pv=None, hp=None, it_loc=None):
                    for jj in (2 * mm, 2 * mm + 1):
                        g = bg + jj
                        P.wait("tensor", "ac", ac_g[g])
                        if jj % JQ == 0:
                            P.wait("tensor", f"lv{jj // JQ}", ld_v[hp][jj // JQ])
                        if jj == 0 and it_loc >= 1:
                            P.wait("tensor", "ve", ve_rc[it_loc - 1])
                        tpv = 0
                        for gi, (o, w) in enumerate(q_sl):
                            tpv = emit_t(lambda e, jj=jj, hh=hh, o=o, w=w, s_pv=s_pv,
                                         pq=(g % 8) * QW: e.matmul(
                                pB[:, s_pv + o:s_pv + o + w],
                                vp_buf[:, jj * 256 + hh * 128:jj * 256 + hh * 128 + 128],
                                pt_buf[:, pq + o:pq + o + w],
                                start=(jj == 0), stop=(jj == KCH - 1)),
                                inc=(gi == len(q_sl) - 1)) or tpv
                        tm_pv_g[g] = tpv
                    for jj in (2 * mm, 2 * mm + 1):
                        g = bg + jj
                        if pool_own(jj):
                            P.wait("gpsimd", "ac", ac_g[g])
                            if jj == 0 and it_loc >= 1:
                                P.wait("gpsimd", "ve", ve_acs[it_loc - 1])
                            if jj == 0:
                                po_g[g] = emit_p(lambda e, g=g: e.tensor_copy(
                                    accP[:, 0:QW], pt_buf[:, (g % 8) * QW:(g % 8) * QW + QW]),
                                    inc=True)
                            else:
                                po_g[g] = emit_p(lambda e, g=g: e.tensor_tensor(
                                    accP[:, 0:QW], accP[:, 0:QW],
                                    pt_buf[:, (g % 8) * QW:(g % 8) * QW + QW], ALU.add),
                                    inc=True)
                        else:
                            P.wait("vector", "ac", ac_g[g])
                            if jj == 1:
                                ve_add_g[g] = emit_v(lambda e, g=g: e.tensor_copy(
                                    accD[:, 0:QW], pt_buf[:, (g % 8) * QW:(g % 8) * QW + QW]),
                                    inc=True)
                            else:
                                ve_add_g[g] = emit_v(lambda e, g=g: e.tensor_tensor(
                                    accD[:, 0:QW], accD[:, 0:QW],
                                    pt_buf[:, (g % 8) * QW:(g % 8) * QW + QW], ALU.add),
                                    inc=True)

                def emit_tail(pit):
                    pc = ((pit + 1) % 2) * QW
                    pv = (pit % 2) * QW
                    P.wait("tensor", "ve", ve_acs[pit])
                    if pit >= 1:
                        P.wait("tensor", "ve", ve_ot[pit - 1])
                    tcol = 0
                    for gi, (o, w) in enumerate(q_sl):
                        tcol = emit_t(lambda e, o=o, w=w, pc=pc: e.matmul(
                            pB[:, pc + o:pc + o + w], ones128b[:], accB[:, o:o + w],
                            start=True, stop=True), inc=(gi == len(q_sl) - 1)) or tcol
                    tm_col[pit] = tcol
                    P.wait("vector", "tm", tm_col[pit])
                    ve_rc[pit] = emit_v(lambda e, pc=pc: e.reciprocal(
                        linv[:, 0:QW], pB[:, pc:pc + QW]), inc=True)
                    osel = pit % 2
                    P.wait("vector", "tm", tm_pv_g[pit * KCH + KCH - 1])
                    if pit >= 2:
                        P.wait("vector", f"sod{osel}", sd_od[pit - 2])
                    ve_ot[pit] = emit_v(lambda e, pv=pv, osel=osel: e.tensor_tensor(
                        ot_buf[:, osel * QW:(osel + 1) * QW], pB[:, pv:pv + QW],
                        linv[:, 0:QW], ALU.mult), inc=True)
                    P.wait("sync", "ve", ve_ot[pit])
                    ph = pit % (NQ * H) // NQ
                    pqb = (pit % NQ) * QW
                    sd_od[pit] = P.dma("sync", f"sod{osel}", lambda e, ph=ph, pqb=pqb, osel=osel: e.dma_start(
                        out=odram[ph * 128:(ph + 1) * 128, pqb:pqb + QW],
                        in_=ot_buf[:, osel * QW:(osel + 1) * QW]))

                kw_se = dict(bg=bg, qb=qb, hh=hh, hq=hq, slot=slot, hp=hp, h=h)
                kw_pv = dict(bg=bg, hh=hh, s_pv=s_pv, hp=hp, it_loc=it_loc)
                for m in range(KCH // 2):
                    emit_scores_exp_pair(m, **kw_se)
                    if m == 1 and it_loc >= 1:
                        emit_tail(it_loc - 1)
                    if m >= 1:
                        emit_pv_acc_pair(m - 1, **kw_pv)
                emit_pv_acc_pair(KCH // 2 - 1, **kw_pv)
                P.wait("vector", "po", po_g[bg + 60])
                ve_acs[it_loc] = emit_v(lambda e: e.tensor_tensor(
                    accB[:, 0:QW], accD[:, 0:QW], accP[:, 0:QW], ALU.add), inc=True)
                if it_loc == H * NQ - 1:
                    emit_tail(it_loc)
                it += 1

        if hp + 1 < HP:
            P.wait("gpsimd", "tm", tm_pv_g[it * KCH - 1])
            emit_vload(hp + 1, "gpsimd")
        if hp + 2 < HP:
            P.wait("sync", "tm", tm_sc_g[it * KCH - 1])
            emit_kload(hp + 2)

    sod_tot = [P.ctr.get("sod0", 0), P.ctr.get("sod1", 0)]

    P.wait("sync", "sod0", sod_tot[0])
    P.wait("sync", "sod1", sod_tot[1])
    ld_oall = 0
    for dc in range(NCH):
        ld_oall = P.dma("sync", "lo", lambda e, dc=dc: e.dma_start(
            out=oall[:, dc * Tl:(dc + 1) * Tl], in_=odram[dc * 128:(dc + 1) * 128, :]))

    emit_v(lambda e: e.memset(rowA[0:1, 0:Tl], 0.0))
    ve_z5 = emit_v(lambda e: e.memset(rowC[0:1, 0:Tl], 0.0), inc=True)

    ve_ya = [0] * NCH
    sd_ya = [0] * NCH
    ac_sq5 = [0] * NCH
    ve_srC = 0
    tm_sum5 = [0] * NCH
    tm_sq5v = [0] * NCH
    tm_op = 0

    def emit_stats5(k):
        nonlocal ve_srC
        p = k % 2
        P.wait("tensor", "ac", ac_sq5[k])
        P.wait("tensor", "ve", ve_z5 if k == 0 else ve_srC)
        t1 = 0
        for gi, (o, w) in enumerate(tl_sl):
            t1 = emit_t(lambda e, p=p, o=o, w=w: e.matmul(
                pB[0:1, o:o + w], ones_bf[:], yab_t[:, p * Tl + o:p * Tl + o + w],
                start=True, stop=True), inc=(gi == len(tl_sl) - 1)) or t1
        tm_sum5[k] = t1
        P.wait("vector", "tm", t1)
        ve_srA = emit_v(lambda e: e.tensor_tensor(
            rowA[0:1, 0:Tl], rowA[0:1, 0:Tl], pB[0:1, 0:Tl], ALU.add), inc=True)
        P.wait("tensor", "ve", ve_srA)
        t2 = 0
        for gi, (o, w) in enumerate(tl_sl):
            t2 = emit_t(lambda e, p=p, o=o, w=w: e.matmul(
                pB[0:1, o:o + w], ones_bf[:], x2b_t[:, p * Tl + o:p * Tl + o + w],
                start=True, stop=True), inc=(gi == len(tl_sl) - 1)) or t2
        tm_sq5v[k] = t2
        P.wait("vector", "tm", t2)
        ve_srC = emit_v(lambda e: e.tensor_tensor(
            rowC[0:1, 0:Tl], rowC[0:1, 0:Tl], pB[0:1, 0:Tl], ALU.add), inc=True)

    for ech in range(NCH):
        p = ech % 2
        if ech % 4 == 0:
            P.wait("tensor", f"wl{ech // 4}", wl_w[("o", ech // 4)])
        if ech == 0:
            P.wait("tensor", "lo", ld_oall)
        else:
            P.wait("tensor", "ve", ve_ya[ech - 1])
        for dc in range(NCH):
            for gi, (o, w) in enumerate(tl_sl):
                tm_op = emit_t(lambda e, ech=ech, dc=dc, o=o, w=w: e.matmul(
                    pA[:, o:o + w],
                    wbuf[:, dc * D + ech * 128:dc * D + ech * 128 + 128],
                    oall[:, dc * Tl + o:dc * Tl + o + w],
                    start=(dc == 0), stop=(dc == NCH - 1)),
                    inc=(dc == NCH - 1 and gi == len(tl_sl) - 1)) or tm_op
        if ech >= 1:
            emit_stats5(ech - 1)
        P.wait("vector", "tm", tm_op)
        if ech >= 2:
            P.wait("vector", f"sya{p}", sd_ya[ech - 2])
            P.wait("vector", "ac", ac_sq5[ech - 2])
        ve_ya[ech] = emit_v(lambda e, p=p: e.tensor_copy(
            xf_t[:, p * Tl:(p + 1) * Tl], pA[:, 0:Tl]), inc=True)
        P.wait("sync", "ve", ve_ya[ech])
        sd_ya[ech] = P.dma("sync", f"sya{p}", lambda e, ech=ech, p=p: e.dma_start(
            out=yadram[ech * 128:(ech + 1) * 128, :], in_=xf_t[:, p * Tl:(p + 1) * Tl]))
        if ech >= 2:
            P.wait("vector", "tm", tm_sum5[ech - 2])
        ve_yab = emit_v(lambda e, p=p: e.tensor_copy(
            yab_t[:, p * Tl:(p + 1) * Tl], xf_t[:, p * Tl:(p + 1) * Tl]), inc=True)
        P.wait("scalar", "ve", ve_yab)
        if ech >= 2:
            P.wait("scalar", "tm", tm_sq5v[ech - 2])
        ac_sq5[ech] = emit_s(lambda e, p=p: e.activation(
            out=x2b_t[:, p * Tl:(p + 1) * Tl], in_=xf_t[:, p * Tl:(p + 1) * Tl],
            func=AF.Square), inc=True)
    emit_stats5(NCH - 1)

    ve_mr2 = ln_rows2(Tl, "ve", ve_srC, False)
    ve_b2 = ln_bcast(Tl, ve_mr2)

    sd_x1 = [0] * NCH
    ve_x1b = [0] * NCH
    ac_sq3 = [0] * NCH
    tm_sq3 = 0
    for c in range(NCH):
        p = c % 2
        if c < 2:
            P.wait("sync", "ac", ac_sq5[NCH - 1])
        else:
            P.wait("sync", "ve", ve_x1b[c - 2])
        P.wait("sync", f"sya{p}", sd_ya[NCH - 2 + p] if NCH >= 2 else sd_ya[0])
        P.dma("sync", f"lya{p}", lambda e, c=c, p=p: e.dma_start(
            out=xf_t[:, p * Tl:(p + 1) * Tl], in_=yadram[c * 128:(c + 1) * 128, :]))
        ld_ya = P.ctr[f"lya{p}"]
        if c >= 1:
            P.wait("sync", "sx1", sd_x1[c - 1])
            P.wait("sync", "ac", ac_sq3[c - 1])
        ld_x = P.dma("sync", "lx1", lambda e, c=c: e.dma_start(
            out=x1_t[:, 0:Tl], in_=xT[c * 128:(c + 1) * 128, :]))
        P.wait("vector", f"lya{p}", ld_ya)
        P.wait("vector", "lx1", ld_x)
        if c == 0:
            P.wait("vector", "ve", ve_b2)
        emit_v(lambda e, p=p: e.tensor_tensor(
            xf_t[:, p * Tl:(p + 1) * Tl], xf_t[:, p * Tl:(p + 1) * Tl], rbb[:, 0:Tl], ALU.mult))
        emit_v(lambda e, p=p: e.tensor_tensor(
            xf_t[:, p * Tl:(p + 1) * Tl], xf_t[:, p * Tl:(p + 1) * Tl], mrbb[:, 0:Tl], ALU.subtract))
        emit_v(lambda e, p=p: e.tensor_tensor(
            x1_t[:, 0:Tl], x1_t[:, 0:Tl], xf_t[:, p * Tl:(p + 1) * Tl], ALU.add))
        ve_x1b[c] = emit_v(lambda e, c=c: e.tensor_copy(
            xb_all[:, c * Tl:(c + 1) * Tl], x1_t[:, 0:Tl]), inc=True)
        P.wait("sync", "ve", ve_x1b[c])
        sd_x1[c] = P.dma("sync", "sx1", lambda e, c=c: e.dma_start(
            out=x1dram[c * 128:(c + 1) * 128, :], in_=x1_t[:, 0:Tl]))
        P.wait("scalar", "ve", ve_x1b[c])
        if c >= 1:
            P.wait("scalar", "tm", tm_sq3)
        ac_sq3[c] = emit_s(lambda e, c=c: e.activation(
            out=x2b_t[:, 0:Tl], in_=x1_t[:, 0:Tl], func=AF.Square), inc=True)
        P.wait("tensor", "ac", ac_sq3[c])
        if c == 0:
            P.wait("tensor", "ve", ve_b2)
        for (o, w) in tl_sl:
            emit_t(lambda e, c=c, o=o, w=w: e.matmul(
                pA[0:1, o:o + w], ones_bf[:], xb_all[:, c * Tl + o:c * Tl + o + w],
                start=(c == 0), stop=(c == NCH - 1)))
        for gi, (o, w) in enumerate(tl_sl):
            tm_sq3 = emit_t(lambda e, c=c, o=o, w=w: e.matmul(
                pB[0:1, o:o + w], ones_bf[:], x2b_t[:, o:o + w],
                start=(c == 0), stop=(c == NCH - 1)), inc=(gi == len(tl_sl) - 1)) or tm_sq3

    ve_mr3 = ln_rows2(Tl, "tm", tm_sq3, True)
    ve_b3 = ln_bcast(Tl, ve_mr3)
    P.wait("vector", "ve", ve_b3)
    ve_n3 = 0
    for c in range(NCH):
        emit_v(lambda e, c=c: e.tensor_tensor(
            yab_t[:, 0:Tl], xb_all[:, c * Tl:(c + 1) * Tl], rbb[:, 0:Tl], ALU.mult))
        ve_n3 = emit_v(lambda e, c=c: e.tensor_tensor(
            xb_all[:, c * Tl:(c + 1) * Tl], yab_t[:, 0:Tl], mrbb[:, 0:Tl], ALU.subtract), inc=True)

    n_out = 0
    so_out = {}
    tm_gu_done = {}
    wl_gsl = {}
    ve_nrm_last = 0
    ve_dn_prev = 0
    ve_b4 = 0
    sx1_tot = P.ctr.get("sx1", 0)
    for ee in range(E):
        ecol = ee * TE
        CPS = GW // 128
        ngroups = NSLAB // 2

        def emit_gu_piece(ee, slot, j):
            if ee == 0 and j < CPS:
                P.wait("gpsimd", "tm", tm_op)
            elif j < CPS:
                P.wait("gpsimd", "tm", tm_gu_done[
                    (ee - 1, (ICH - CPS + j) + (0 if slot == 0 else ICH))])
            else:
                P.wait("gpsimd", "tm", tm_gu_done[
                    (ee, (j - CPS) + (0 if slot == 0 else ICH))])
            col = j * 128 + (0 if slot == 0 else I)
            srcw = wgu[ee, :, col:col + 128].rearrange("(dc p) i -> p dc i", p=128)
            sl = gslab[:, slot * NCH * GW:(slot + 1) * NCH * GW].rearrange(
                "p (dc i) -> p dc i", dc=NCH)
            gdst = sl[:, :, (j % CPS) * 128:(j % CPS) * 128 + 128]
            wl_gsl[(ee, slot, j)] = P.dma("gpsimd", f"wg{slot}{j % CPS}",
                                          lambda e, srcw=srcw, gdst=gdst: e.dma_start(out=gdst, in_=srcw))

        for j0 in range(CPS):
            emit_gu_piece(ee, 0, j0)
            emit_gu_piece(ee, 1, j0)

        ac_sg = [0] * ICH
        ve_in = [0] * ICH
        for ihalf in range(ICH):
            for which in ("g", "u"):
                icg = ihalf if which == "g" else ihalf + ICH
                offw = (ihalf % CPS) * 128
                slot = 0 if which == "g" else 1
                base = (ihalf % 2) * 1024 + (0 if which == "g" else 512)
                P.wait("tensor", f"wg{slot}{ihalf % CPS}", wl_gsl[(ee, slot, ihalf)])
                if ee == 0 and ihalf == 0 and which == "g":
                    P.wait("tensor", "ve", ve_n3)
                if ihalf < 2 and ee > 0:
                    if which == "g":
                        P.wait("tensor", "ac", ac_sg_prev[ihalf])
                    else:
                        P.wait("tensor", "ve", ve_in_prev[ihalf])
                if ihalf >= 2:
                    if which == "g":
                        P.wait("tensor", "ac", ac_sg[ihalf - 2])
                    else:
                        P.wait("tensor", "ve", ve_in[ihalf - 2])
                tmg = 0
                for dc in range(NCH):
                    for gi, (o, w) in enumerate(te_sl):
                        tmg = emit_t(lambda e, offw=offw, dc=dc, o=o, w=w,
                                     base=base, slot=slot, ecol=ecol: e.matmul(
                            pA[:, base + o:base + o + w],
                            gslab[:, slot * NCH * GW + dc * GW + offw:
                                  slot * NCH * GW + dc * GW + offw + 128],
                            xb_all[:, dc * Tl + ecol + o:dc * Tl + ecol + o + w],
                            start=(dc == 0), stop=(dc == NCH - 1)),
                            inc=(dc == NCH - 1 and gi == len(te_sl) - 1)) or tmg
                tm_gu_done[(ee, icg)] = tmg
            P.wait("scalar", "tm", tm_gu_done[(ee, ihalf)])
            if ihalf >= 2:
                P.wait("scalar", "ve", ve_in[ihalf - 2])
            ac_sg[ihalf] = emit_s(lambda e, ihalf=ihalf: e.activation(
                out=sg_t[:, (ihalf % 2) * TE:(ihalf % 2 + 1) * TE],
                in_=pA[:, (ihalf % 2) * 1024:(ihalf % 2) * 1024 + TE], func=AF.Silu), inc=True)
            P.wait("vector", "ac", ac_sg[ihalf])
            P.wait("vector", "tm", tm_gu_done[(ee, ihalf + ICH)])
            ve_in[ihalf] = emit_v(lambda e, ihalf=ihalf: e.tensor_tensor(
                inter[:, ihalf * TE:(ihalf + 1) * TE],
                pA[:, (ihalf % 2) * 1024 + 512:(ihalf % 2) * 1024 + 512 + TE],
                sg_t[:, (ihalf % 2) * TE:(ihalf % 2 + 1) * TE], ALU.mult), inc=True)
            if ihalf + CPS < ICH:
                emit_gu_piece(ee, 0, ihalf + CPS)
                emit_gu_piece(ee, 1, ihalf + CPS)

        ac_sg_prev = [ac_sg[ICH - 2], ac_sg[ICH - 1]]
        ve_in_prev = [ve_in[ICH - 2], ve_in[ICH - 1]]

        ve_dn = [0] * NCH
        ac_c6 = [0] * NCH
        ve_c6 = [0] * NCH
        tm_sum6 = [0] * NCH
        tm_sq6 = [0] * NCH
        tm_st6 = 0

        def emit_stats6(k):
            nonlocal tm_st6
            pk = k % 2
            P.wait("tensor", "ac", ac_c6[k])
            if k == 0 and ee == 0:
                P.wait("tensor", "ve", ve_b3)
            if k == 0 and ee > 0:
                P.wait("tensor", "ve", ve_b4)
            t1 = 0
            for gi, (o, w) in enumerate(te_sl):
                t1 = emit_t(lambda e, k=k, pk=pk, o=o, w=w: e.matmul(
                    pB[0:1, 1024 + o:1024 + o + w], ones_bf[:], cp6_t[:, pk * TE + o:pk * TE + o + w],
                    start=(k == 0), stop=(k == NCH - 1)), inc=(gi == len(te_sl) - 1)) or t1
            tm_sum6[k] = t1
            t2 = 0
            for gi, (o, w) in enumerate(te_sl):
                t2 = emit_t(lambda e, k=k, pk=pk, o=o, w=w: e.matmul(
                    pB[0:1, 1536 + o:1536 + o + w], ones_bf[:], sq6_t[:, pk * TE + o:pk * TE + o + w],
                    start=(k == 0), stop=(k == NCH - 1)), inc=(gi == len(te_sl) - 1)) or t2
            tm_sq6[k] = t2
            tm_st6 = t2

        for hc in range(NCH):
            hp2 = hc % 2
            src = wdn[ee, :, hc * 128:(hc + 1) * 128].rearrange("(ic p) hh -> p ic hh", p=128)
            if ee == 0 and hc < 2:
                P.wait("gpsimd", "tm", tm_op)
            elif hc >= 2:
                P.wait("gpsimd", "tm", tm_gu_done[("dn", ee, hc - 2)])
            else:
                P.wait("gpsimd", "tm", tm_gu_done[("dn", ee - 1, NCH - 1)])
            ddst = dslab[:, hp2 * ICH * 128:(hp2 + 1) * ICH * 128].rearrange(
                "p (ic hh) -> p ic hh", ic=ICH)
            wl_dn = P.dma("gpsimd", f"wd{hp2}", lambda e, src=src, ddst=ddst: e.dma_start(
                out=ddst, in_=src))
            P.wait("tensor", f"wd{hp2}", wl_dn)
            if hc == 0:
                P.wait("tensor", "ve", ve_in[ICH - 1])
                if ee > 0:
                    P.wait("tensor", "ve", ve_dn_prev)
            if hc >= 2:
                P.wait("tensor", "ve", ve_dn[hc - 2])
            tm_dn = 0
            for ic in range(ICH):
                for gi, (o, w) in enumerate(te_sl):
                    tm_dn = emit_t(lambda e, hp2=hp2, ic=ic, o=o, w=w: e.matmul(
                        pB[:, hp2 * 512 + o:hp2 * 512 + o + w],
                        dslab[:, hp2 * ICH * 128 + ic * 128:hp2 * ICH * 128 + ic * 128 + 128],
                        inter[:, ic * TE + o:ic * TE + o + w],
                        start=(ic == 0), stop=(ic == ICH - 1)),
                        inc=(ic == ICH - 1 and gi == len(te_sl) - 1)) or tm_dn
            tm_gu_done[("dn", ee, hc)] = tm_dn
            if hc >= 1:
                emit_stats6(hc - 1)
            P.wait("vector", "tm", tm_dn)
            if ee > 0:
                P.wait("vector", "ve", ve_nrm_prev[hc])
            ve_dn[hc] = emit_v(lambda e, hc=hc, hp2=hp2: e.tensor_copy(
                yf_e[:, hc * TE:(hc + 1) * TE], pB[:, hp2 * 512:hp2 * 512 + TE]), inc=True)
            if hc >= 2:
                P.wait("vector", "tm", tm_sum6[hc - 2])
            ve_c6[hc] = emit_v(lambda e, hc=hc, hp2=hp2: e.tensor_copy(
                cp6_t[:, hp2 * TE:(hp2 + 1) * TE], yf_e[:, hc * TE:(hc + 1) * TE]), inc=True)
            P.wait("scalar", "ve", ve_c6[hc])
            if hc >= 2:
                P.wait("scalar", "tm", tm_sq6[hc - 2])
            ac_c6[hc] = emit_s(lambda e, hc=hc, hp2=hp2: e.activation(
                out=sq6_t[:, hp2 * TE:(hp2 + 1) * TE], in_=yf_e[:, hc * TE:(hc + 1) * TE],
                func=AF.Square), inc=True)
        emit_stats6(NCH - 1)
        ve_dn_prev = ve_dn[NCH - 1]

        P.wait("vector", "tm", tm_st6)
        emit_v(lambda e: e.tensor_scalar_mul(rowA[0:1, 0:TE], pB[0:1, 1024:1024 + TE], konst[0:1, 0:1]))
        v1 = emit_v(lambda e: e.tensor_scalar_mul(rowC[0:1, 0:TE], pB[0:1, 1536:1536 + TE], konst[0:1, 0:1]), inc=True)
        P.wait("scalar", "ve", v1)
        a1 = emit_s(lambda e: e.activation(out=mrbb[0:1, 0:TE], in_=rowA[0:1, 0:TE], func=AF.Square), inc=True)
        P.wait("vector", "ac", a1)
        v2 = emit_v(lambda e: e.tensor_tensor(rowC[0:1, 0:TE], rowC[0:1, 0:TE], mrbb[0:1, 0:TE], ALU.subtract), inc=True)
        P.wait("scalar", "ve", v2)
        emit_s(lambda e: e.activation(out=rowC[0:1, 0:TE], in_=rowC[0:1, 0:TE], func=AF.Ln, bias=f32(cfg.EPS), scale=f32(1.0)))
        a2 = emit_s(lambda e: e.activation(out=rowC[0:1, 0:TE], in_=rowC[0:1, 0:TE], func=AF.Exp, bias=f32(0.0), scale=f32(-0.5)), inc=True)
        P.wait("vector", "ac", a2)
        v3 = emit_v(lambda e: e.tensor_tensor(rowA[0:1, 0:TE], rowA[0:1, 0:TE], rowC[0:1, 0:TE], ALU.mult), inc=True)
        P.wait("tensor", "ve", v3)
        for (o, w) in te_sl:
            emit_t(lambda e, o=o, w=w: e.matmul(
                pB[:, 1024 + o:1024 + o + w], ones_row[:], rowC[0:1, o:o + w], start=True, stop=True))
        tm_b4 = 0
        for gi, (o, w) in enumerate(te_sl):
            tm_b4 = emit_t(lambda e, o=o, w=w: e.matmul(
                pB[:, 1536 + o:1536 + o + w], ones_row[:], rowA[0:1, o:o + w],
                start=True, stop=True), inc=(gi == len(te_sl) - 1)) or tm_b4
        P.wait("vector", "tm", tm_b4)
        emit_v(lambda e: e.tensor_copy(rbb[:, 0:TE], pB[:, 1024:1024 + TE]))
        ve_b4 = emit_v(lambda e: e.tensor_copy(mrbb[:, 0:TE], pB[:, 1536:1536 + TE]), inc=True)

        ve_nrm = {}
        for c in range(NCH):
            slot = c % 4
            if ee == 0 and c == 0:
                P.wait("sync", "ac", ac_sq3[NCH - 1])
                P.wait("sync", "sx1", sx1_tot)
            if c >= 4:
                P.wait("sync", "ve", ve_nrm[c - 4])
            elif ee > 0:
                P.wait("sync", "ve", ve_nrm_last)
            ld_x1c = P.dma("sync", f"lx6{slot}", lambda e, c=c, ecol=ecol, slot=slot: e.dma_start(
                out=x1_t[:, slot * TE:(slot + 1) * TE],
                in_=x1dram[c * 128:(c + 1) * 128, ecol:ecol + TE]))
            P.wait("vector", f"lx6{slot}", ld_x1c)
            if c == 0:
                P.wait("vector", "ve", ve_b4)
            osel = n_out % 2
            if n_out >= 2:
                P.wait("vector", f"sout{osel}", so_out[n_out - 2])
            emit_v(lambda e, c=c: e.tensor_tensor(
                o1_t[:, 0:TE], yf_e[:, c * TE:(c + 1) * TE], rbb[:, 0:TE], ALU.mult))
            emit_v(lambda e: e.tensor_tensor(
                o1_t[:, 0:TE], o1_t[:, 0:TE], mrbb[:, 0:TE], ALU.subtract))
            ve_nrm[c] = emit_v(lambda e, slot=slot, osel=osel: e.tensor_tensor(
                out_t[:, osel * TE:(osel + 1) * TE],
                x1_t[:, slot * TE:(slot + 1) * TE], o1_t[:, 0:TE], ALU.add), inc=True)
            P.wait("sync", "ve", ve_nrm[c])
            so_out[n_out] = P.dma("sync", f"sout{osel}", lambda e, c=c, ecol=ecol, osel=osel: e.dma_start(
                out=outT[c * 128:(c + 1) * 128, ecol:ecol + TE],
                in_=out_t[:, osel * TE:(osel + 1) * TE]))
            n_out += 1
        ve_nrm_last = ve_nrm[NCH - 1]
        ve_nrm_prev = dict(ve_nrm)

    P.wait("sync", "sout0", P.ctr.get("sout0", 0))
    P.wait("sync", "sout1", P.ctr.get("sout1", 0))
    P.wait("gpsimd", "sout0", P.ctr.get("sout0", 0))
    P.wait("gpsimd", "sout1", P.ctr.get("sout1", 0))

    with nc.Block() as block:
        @block.sync
        def _(e):
            for fn in P.ops["sync"]:
                fn(e)

        @block.gpsimd
        def _(e):
            for fn in P.ops["gpsimd"]:
                fn(e)

        @block.tensor
        def _(e):
            for fn in P.ops["tensor"]:
                fn(e)

        @block.vector
        def _(e):
            for fn in P.ops["vector"]:
                fn(e)

        @block.scalar
        def _(e):
            for fn in P.ops["scalar"]:
                fn(e)

    return nc


def _prep_core_inputs(x, wq, wk, wv, wo, gup, dnp, cfg, core):
    b = core // cfg.R
    t = core % cfg.R
    perm = np.concatenate([np.arange(e, cfg.Tl, cfg.E) for e in range(cfg.E)])
    sl = x[b, t * cfg.Tl:(t + 1) * cfg.Tl, :][perm, :]
    return {
        "xT": np.ascontiguousarray(sl.T),
        "wqT": np.ascontiguousarray(wq.T),
        "wkT": np.ascontiguousarray(wk.T),
        "wvT": np.ascontiguousarray(wv.T),
        "woT": np.ascontiguousarray(wo.T),
        "wgu": np.ascontiguousarray(gup),
        "wdn": np.ascontiguousarray(dnp),
    }, perm


def kernel(L0_attn_x, L0_attn_w_q, L0_attn_w_k, L0_attn_w_v, L0_attn_w_o,
           L0_ffn_gate_up_proj, L0_ffn_down_proj):
    cfg = CFG()
    x = np.asarray(L0_attn_x, dtype=np.float32)
    wq = np.asarray(L0_attn_w_q, dtype=np.float32)
    wk = np.asarray(L0_attn_w_k, dtype=np.float32)
    wv = np.asarray(L0_attn_w_v, dtype=np.float32)
    wo = np.asarray(L0_attn_w_o, dtype=np.float32)
    gup = np.asarray(L0_ffn_gate_up_proj, dtype=np.float32)
    dnp = np.asarray(L0_ffn_down_proj, dtype=np.float32)

    in_maps = []
    perms = []
    for core in range(8):
        m, perm = _prep_core_inputs(x, wq, wk, wv, wo, gup, dnp, cfg, core)
        in_maps.append(m)
        perms.append(perm)

    nc = build_nc(cfg)
    res = run_bass_kernel_spmd(nc, in_maps, core_ids=list(range(8)))

    out = np.empty((2, 8192, 2048), dtype=np.float32)
    for core in range(8):
        b = core // cfg.R
        t = core % cfg.R
        shard = np.asarray(res.results[core]["outT"], dtype=np.float32).T
        sl = np.empty_like(shard)
        sl[perms[core], :] = shard
        out[b, t * cfg.Tl:(t + 1) * cfg.Tl, :] = sl
    return out


# version 21
# speedup vs baseline: 1.0073x; 1.0012x over previous
import numpy as np

import concourse.bass as bass
import concourse.mybir as mybir
from concourse.bass_utils import run_bass_kernel_spmd

F32 = mybir.dt.float32
BF16 = mybir.dt.bfloat16
AF = mybir.ActivationFunctionType
ALU = mybir.AluOpType
f32 = np.float32

ENGINES = ("sync", "gpsimd", "tensor", "vector", "scalar")
NW = 512


class CFG:
    def __init__(self, D=2048, Tl=2048, H=16, E=4, I=2048, R=4):
        self.D, self.Tl, self.H, self.E, self.I, self.R = D, Tl, H, E, I, R
        self.HD = 128
        assert H * self.HD == D
        self.S = R * Tl
        self.NCH = D // 128
        self.TCH = Tl // 128
        self.KCH = self.S // 128
        self.QW = min(1024, Tl)
        self.NQ = Tl // self.QW
        self.TE = Tl // E
        self.ICH = I // 128
        self.GW = min(512, I)
        self.NSLAB = (2 * I) // self.GW
        self.EPS = 1e-5
        self.SCALE = 1.0 / float(np.sqrt(self.HD))
        self.SHIFT = -5.0


class Plan:
    def __init__(self, sems):
        self.ops = {e: [] for e in ENGINES}
        self.ctr = {k: 0 for k in sems}
        self.sems = sems

    def emit(self, eng, fn):
        self.ops[eng].append(fn)

    def inc(self, sem, n=1):
        self.ctr[sem] += n
        return self.ctr[sem]

    def wait(self, eng, sem, v):
        if v > 0:
            s = self.sems[sem]
            self.ops[eng].append(lambda e, s=s, v=v: e.wait_ge(s, v))

    def drain(self, eng):
        self.ops[eng].append(lambda e: e.drain())


def _sl(width, nw=NW):
    out, o = [], 0
    while o < width:
        w = min(nw, width - o)
        out.append((o, w))
        o += w
    return out


def build_nc(cfg: CFG):
    nc = bass.Bass("TRN2", target_bir_lowering=False)
    D, Tl, H, E, I, R = cfg.D, cfg.Tl, cfg.H, cfg.E, cfg.I, cfg.R
    NCH, TCH, KCH, QW, NQ, TE, ICH = (cfg.NCH, cfg.TCH, cfg.KCH, cfg.QW,
                                      cfg.NQ, cfg.TE, cfg.ICH)
    S, GW, NSLAB = cfg.S, cfg.GW, cfg.NSLAB
    HP = H // 2

    xT = nc.dram_tensor("xT", [D, Tl], F32, kind="ExternalInput").ap()
    wqT = nc.dram_tensor("wqT", [D, D], F32, kind="ExternalInput").ap()
    wkT = nc.dram_tensor("wkT", [D, D], F32, kind="ExternalInput").ap()
    wvT = nc.dram_tensor("wvT", [D, D], F32, kind="ExternalInput").ap()
    woT = nc.dram_tensor("woT", [D, D], F32, kind="ExternalInput").ap()
    wgu = nc.dram_tensor("wgu", [E, D, 2 * I], F32, kind="ExternalInput").ap()
    wdn = nc.dram_tensor("wdn", [E, I, D], F32, kind="ExternalInput").ap()
    outT = nc.dram_tensor("outT", [D, Tl], F32, kind="ExternalOutput").ap()

    kloc = nc.dram_tensor("kloc", [D, Tl], BF16)
    vloc = nc.dram_tensor("vloc", [Tl, D], BF16)
    kgat = nc.dram_tensor("kgat", [R * D, Tl], BF16)
    vgat = nc.dram_tensor("vgat", [R * Tl, D], BF16)
    qdram = nc.dram_tensor("qdram", [D, Tl], BF16).ap()
    odram = nc.dram_tensor("odram", [D, Tl], BF16).ap()
    yadram = nc.dram_tensor("yadram", [D, Tl], F32).ap()
    x1dram = nc.dram_tensor("x1dram", [D, Tl], F32).ap()

    def sb(name, shape, dtype, off):
        return nc.alloc_sbuf_tensor_at(name, list(shape), dtype, offset=ARENA + off)

    A0 = 0
    a_sz = max(NCH * Tl * 2, 2 * 2 * S * 2)
    B0 = A0 + a_sz
    b_p6 = [("gsl", 2 * NCH * GW * 2), ("dsl", 2 * ICH * 128 * 2),
            ("sg", 2 * TE * 4), ("intr", ICH * TE * 2), ("outt", 2 * TE * 4)]
    b_sz = max(NCH * D * 2, sum(s for _, s in b_p6))
    C0 = B0 + b_sz
    c_sz = max(2 * Tl * 4 + 2 * Tl * 2 + 2 * Tl * 2, 2 * D * 2, KCH * 256 * 2, NCH * TE * 4)
    D0 = C0 + c_sz
    E0 = D0 + Tl * 4
    F0 = E0 + 2 * Tl * 4
    G0 = F0 + 2 * Tl * 4
    need = G0 + 1152 + TE * 4 + 4 * TE * 2
    assert need <= nc.sbuf_bytes_remaining, (need, nc.sbuf_bytes_remaining)
    _resv = nc.alloc_sbuf_tensor("arena_reserve", [128, need], mybir.dt.uint8)
    ARENA = nc.lookup_mloc(_resv).addr

    xb_all = sb("xb_all", [128, NCH * Tl], BF16, A0)
    kp_buf = sb("kp_buf", [128, 2 * 2 * S], BF16, A0)
    oall = sb("oall", [128, NCH * Tl], BF16, A0)

    wbuf = sb("wbuf", [128, NCH * D], BF16, B0)
    off = B0
    gslab = sb("gslab", [128, 2 * NCH * GW], BF16, off); off += b_p6[0][1]
    dslab = sb("dslab", [128, 2 * ICH * 128], BF16, off); off += b_p6[1][1]
    sg_t = sb("sg_t", [128, 2 * TE], F32, off); off += b_p6[2][1]
    inter = sb("inter", [128, ICH * TE], BF16, off); off += b_p6[3][1]
    out_t = sb("out_t", [128, 2 * TE], F32, off)

    xf_t = sb("xf_t", [128, 2 * Tl], F32, C0)
    x2b_t = sb("x2b_t", [128, 2 * Tl], BF16, C0 + 2 * Tl * 4)
    y1_t = sb("y1_t", [128, Tl], BF16, C0 + 2 * Tl * 4 + 2 * Tl * 2)
    ev_t = sb("ev_t", [128, 2 * D], BF16, C0)
    vp_buf = sb("vp_buf", [128, KCH * 256], BF16, C0)
    yab_t = sb("yab_t", [128, 2 * Tl], BF16, C0 + 2 * Tl * 4 + 2 * Tl * 2)
    yf_e = sb("yf_e", [128, NCH * TE], F32, C0)

    x1_t = sb("x1_t", [128, Tl], F32, D0)
    qh_buf = sb("qh_buf", [128, 2 * Tl], BF16, D0)
    rowA = sb("rowA", [1, Tl], F32, E0)
    pt_buf = sb("pt_buf", [128, 8 * QW], BF16, E0)
    rowC = sb("rowC", [1, Tl], F32, E0 + Tl * 4)
    rbb = sb("rbb", [128, Tl], F32, F0)
    ot_buf = sb("ot_buf", [128, 2 * QW], BF16, F0)
    lb = sb("lb", [128, QW], BF16, F0 + 2 * QW * 2)
    mrbb = sb("mrbb", [128, Tl], F32, F0 + Tl * 4)
    accD = sb("accD", [128, QW], F32, F0 + Tl * 4)
    accP = sb("accP", [128, QW], F32, F0 + Tl * 4 + QW * 4)
    ones_bf = sb("ones_bf", [128, 1], BF16, G0)
    ones_row = sb("ones_row", [1, 128], F32, G0 + 32)
    konst = sb("konst", [1, 4], F32, G0 + 544)
    ones128b = sb("ones128b", [128, 128], BF16, G0 + 576)
    o1_t = sb("o1_t", [128, TE], F32, G0 + 1152)
    accB = sb("accB", [128, QW], BF16, G0 + 1152)
    sq6_t = sb("sq6_t", [128, 2 * TE], BF16, G0 + 1152 + TE * 4)
    cp6_t = sb("cp6_t", [128, 2 * TE], BF16, G0 + 1152 + TE * 4 + 2 * TE * 2)
    linv = sb("linv", [128, QW], F32, G0 + 1152 + TE * 4)

    pA = nc.alloc_psum_tensor("pA", [128, 2048], F32)
    pB = nc.alloc_psum_tensor("pB", [128, 2048], F32)

    class LazySems(dict):
        def __missing__(self, k):
            self[k] = nc.alloc_semaphore(k)
            return self[k]

    sems = LazySems()

    class Plan2:
        def __init__(self):
            self.ops = {e: [] for e in ENGINES}
            self.ctr = {}
            self.log = {e: [] for e in ENGINES}

        def emit(self, eng, fn):
            self.ops[eng].append(fn)

        def inc(self, sem, n=1, eng=None):
            self.ctr[sem] = self.ctr.get(sem, 0) + n
            if eng is not None:
                self.log[eng].append(("inc", sem, n, False))
            return self.ctr[sem]

        def wait(self, eng, sem, v):
            if v > 0:
                s = sems[sem]
                self.ops[eng].append(lambda e, s=s, v=v: e.wait_ge(s, v))
                self.log[eng].append(("wait", sem, v, False))

        def dma(self, eng, sem, fn):
            s = sems[sem]
            self.ops[eng].append(lambda e, fn=fn, s=s: fn(e).then_inc(s, 16))
            self.log[eng].append(("inc", sem, 16, True))
            return self.inc(sem, 16)

        def drain(self, eng):
            self.ops[eng].append(lambda e: e.drain())

    P = Plan2()
    tl_sl = _sl(Tl)
    d_sl = _sl(D)
    q_sl = _sl(QW)
    te_sl = _sl(TE)

    def emit_v(fn, inc=False):
        if inc:
            P.emit("vector", lambda e, fn=fn: fn(e).then_inc(sems["ve"], 1))
            return P.inc("ve", eng="vector")
        P.emit("vector", lambda e, fn=fn: fn(e))
        return None

    def emit_s(fn, inc=False):
        if inc:
            P.emit("scalar", lambda e, fn=fn: fn(e).then_inc(sems["ac"], 1))
            return P.inc("ac", eng="scalar")
        P.emit("scalar", lambda e, fn=fn: fn(e))
        return None

    def emit_t(fn, inc=False):
        if inc:
            P.emit("tensor", lambda e, fn=fn: fn(e).then_inc(sems["tm"], 1))
            return P.inc("tm", eng="tensor")
        P.emit("tensor", lambda e, fn=fn: fn(e))
        return None

    def emit_p(fn, inc=False):
        if inc:
            P.emit("gpsimd", lambda e, fn=fn: fn(e).then_inc(sems["po"], 1))
            return P.inc("po", eng="gpsimd")
        P.emit("gpsimd", lambda e, fn=fn: fn(e))
        return None

    def ln_rows2(width, wait_sem, wait_val, from_psum):
        P.wait("vector", wait_sem, wait_val)
        if from_psum:
            emit_v(lambda e: e.tensor_scalar_mul(rowA[0:1, 0:width], pA[0:1, 0:width], konst[0:1, 0:1]))
            v1 = emit_v(lambda e: e.tensor_scalar_mul(rowC[0:1, 0:width], pB[0:1, 0:width], konst[0:1, 0:1]), inc=True)
        else:
            emit_v(lambda e: e.tensor_scalar_mul(rowA[0:1, 0:width], rowA[0:1, 0:width], konst[0:1, 0:1]))
            v1 = emit_v(lambda e: e.tensor_scalar_mul(rowC[0:1, 0:width], rowC[0:1, 0:width], konst[0:1, 0:1]), inc=True)
        P.wait("scalar", "ve", v1)
        a1 = emit_s(lambda e: e.activation(out=mrbb[0:1, 0:width], in_=rowA[0:1, 0:width], func=AF.Square), inc=True)
        P.wait("vector", "ac", a1)
        v2 = emit_v(lambda e: e.tensor_tensor(rowC[0:1, 0:width], rowC[0:1, 0:width], mrbb[0:1, 0:width], ALU.subtract), inc=True)
        P.wait("scalar", "ve", v2)
        emit_s(lambda e: e.activation(out=rowC[0:1, 0:width], in_=rowC[0:1, 0:width], func=AF.Ln, bias=f32(cfg.EPS), scale=f32(1.0)))
        a2 = emit_s(lambda e: e.activation(out=rowC[0:1, 0:width], in_=rowC[0:1, 0:width], func=AF.Exp, bias=f32(0.0), scale=f32(-0.5)), inc=True)
        P.wait("vector", "ac", a2)
        v3 = emit_v(lambda e: e.tensor_tensor(rowA[0:1, 0:width], rowA[0:1, 0:width], rowC[0:1, 0:width], ALU.mult), inc=True)
        return v3

    def ln_bcast(width, ve_mr):
        P.wait("tensor", "ve", ve_mr)
        sl = _sl(width)
        for (o, w) in sl:
            emit_t(lambda e, o=o, w=w: e.matmul(pA[:, o:o + w], ones_row[:], rowC[0:1, o:o + w], start=True, stop=True))
        t1 = 0
        for gi, (o, w) in enumerate(sl):
            t1 = emit_t(lambda e, o=o, w=w: e.matmul(pB[:, o:o + w], ones_row[:], rowA[0:1, o:o + w], start=True, stop=True),
                        inc=(gi == len(sl) - 1))
        P.wait("vector", "tm", t1)
        emit_v(lambda e: e.tensor_copy(rbb[:, 0:width], pA[:, 0:width]))
        vb = emit_v(lambda e: e.tensor_copy(mrbb[:, 0:width], pB[:, 0:width]), inc=True)
        return vb

    emit_v(lambda e: e.memset(ones_bf[:], 1.0))
    emit_v(lambda e: e.memset(ones_row[:], 1.0))
    emit_v(lambda e: e.memset(ones128b[:], 1.0))
    emit_v(lambda e: e.memset(konst[0:1, 0:1], 1.0 / D), inc=True)

    ve_cast = [0] * NCH
    tm_st1 = [0] * NCH
    ldx = [0] * NCH
    for c in range(NCH):
        p = c % 2
        if c >= 2:
            P.wait("sync", "ve", ve_cast[c - 2])
        ldx[c] = P.dma("sync", f"ldx{p}", lambda e, c=c, p=p: e.dma_start(
            out=xf_t[:, p * Tl:(p + 1) * Tl], in_=xT[c * 128:(c + 1) * 128, :]))
        P.wait("vector", f"ldx{p}", ldx[c])
        ve_cast[c] = emit_v(lambda e, c=c, p=p: e.tensor_copy(
            xb_all[:, c * Tl:(c + 1) * Tl], xf_t[:, p * Tl:(p + 1) * Tl]), inc=True)
        P.wait("scalar", "ve", ve_cast[c])
        if c >= 1:
            P.wait("scalar", "tm", tm_st1[c - 1])
        ac1 = emit_s(lambda e, c=c: e.activation(
            out=x2b_t[:, 0:Tl], in_=xb_all[:, c * Tl:(c + 1) * Tl], func=AF.Square), inc=True)
        P.wait("tensor", "ac", ac1)
        for (o, w) in tl_sl:
            emit_t(lambda e, c=c, o=o, w=w: e.matmul(
                pA[0:1, o:o + w], ones_bf[:], xb_all[:, c * Tl + o:c * Tl + o + w],
                start=(c == 0), stop=(c == NCH - 1)))
        for gi, (o, w) in enumerate(tl_sl):
            tm_st1[c] = emit_t(lambda e, c=c, o=o, w=w: e.matmul(
                pB[0:1, o:o + w], ones_bf[:], x2b_t[:, o:o + w],
                start=(c == 0), stop=(c == NCH - 1)), inc=(gi == len(tl_sl) - 1))

    ve_mr1 = ln_rows2(Tl, "tm", tm_st1[NCH - 1], True)
    ve_b1 = ln_bcast(Tl, ve_mr1)
    P.wait("vector", "ve", ve_b1)
    ve_n1 = 0
    for c in range(NCH):
        emit_v(lambda e, c=c: e.tensor_tensor(
            y1_t[:, 0:Tl], xb_all[:, c * Tl:(c + 1) * Tl], rbb[:, 0:Tl], ALU.mult))
        ve_n1 = emit_v(lambda e, c=c: e.tensor_tensor(
            xb_all[:, c * Tl:(c + 1) * Tl], y1_t[:, 0:Tl], mrbb[:, 0:Tl], ALU.subtract), inc=True)

    wl_w = {}

    def load_weight(tag, src_ap, gate_fn):
        dstf = wbuf[:].rearrange("p (dc e) -> p dc e", dc=NCH)
        for g in range(4):
            P.wait("gpsimd", "tm", gate_fn(g))
            src = src_ap[:, g * 512:(g + 1) * 512].rearrange("(dc p) e -> p dc e", p=128)
            dst = dstf[:, :, g * 512:(g + 1) * 512]
            wl_w[(tag, g)] = P.dma("gpsimd", f"wl{g}", lambda e, src=src, dst=dst: e.dma_start(out=dst, in_=src))

    tm_proj = {}
    ve_evac = {}
    sd_store = {}

    def emit_proj_fm(tag, dst_rows_fn, prev_tag=None):
        for ech in range(NCH):
            reg = pA if ech % 2 == 0 else pB
            p = ech % 2
            if ech % 4 == 0:
                P.wait("tensor", f"wl{ech // 4}", wl_w[(tag, ech // 4)])
            if ech == 0:
                P.wait("tensor", "ve", ve_n1)
            if ech < 2 and prev_tag is not None:
                P.wait("tensor", "ve", ve_evac[(prev_tag, NCH - 2 + ech)])
            if ech >= 2:
                P.wait("tensor", "ve", ve_evac[(tag, ech - 2)])
            tmv = 0
            for dc in range(NCH):
                for gi, (o, w) in enumerate(tl_sl):
                    tmv = emit_t(lambda e, ech=ech, dc=dc, o=o, w=w, reg=reg: e.matmul(
                        reg[:, o:o + w],
                        wbuf[:, dc * D + ech * 128:dc * D + ech * 128 + 128],
                        xb_all[:, dc * Tl + o:dc * Tl + o + w],
                        start=(dc == 0), stop=(dc == NCH - 1)),
                        inc=(dc == NCH - 1 and gi == len(tl_sl) - 1)) or tmv
            tm_proj[(tag, ech)] = tmv
            P.wait("vector", "tm", tmv)
            if ech >= 2:
                P.wait("vector", f"sde{p}", sd_store[(tag, ech - 2)])
            elif prev_tag is not None:
                P.wait("vector", f"sde{p}", sd_store[(prev_tag, NCH - 2 + ech)])
            ve_evac[(tag, ech)] = emit_v(lambda e, p=p, reg=reg: e.tensor_copy(
                ev_t[:, p * D:p * D + Tl], reg[:, 0:Tl]), inc=True)
            P.wait("sync", "ve", ve_evac[(tag, ech)])
            sd_store[(tag, ech)] = P.dma("sync", f"sde{p}", lambda e, ech=ech, p=p, fn=dst_rows_fn: e.dma_start(
                out=fn(ech), in_=ev_t[:, p * D:p * D + Tl]))

    rg = [[0, 1, 2, 3], [4, 5, 6, 7]]
    load_weight("k", wkT, lambda g: 0)
    emit_proj_fm("k", lambda ech: kloc.ap()[ech * 128:(ech + 1) * 128, :])
    P.wait("gpsimd", "sde0", P.ctr.get("sde0", 0))
    P.wait("gpsimd", "sde1", P.ctr.get("sde1", 0))
    P.emit("gpsimd", lambda e: e.collective_compute(
        "AllGather", ALU.bypass, replica_groups=rg,
        ins=[kloc.ap().opt()], outs=[kgat.ap().opt()]).then_inc(sems["cc"], 1))
    P.inc("cc", eng="gpsimd")
    load_weight("v", wvT, lambda g: tm_proj[("k", 4 * g + 3)])

    for tt in range(TCH):
        reg = pA if tt % 2 == 0 else pB
        p = tt % 2
        if tt == 0:
            for g4 in range(4):
                P.wait("tensor", f"wl{g4}", wl_w[("v", g4)])
        if tt < 2:
            P.wait("tensor", "ve", ve_evac[("k", NCH - 2 + tt)])
        if tt >= 2:
            P.wait("tensor", "ve", ve_evac[("v", tt - 2)])
        tmv = 0
        for dc in range(NCH):
            for gi, (o, w) in enumerate(d_sl):
                tmv = emit_t(lambda e, tt=tt, dc=dc, o=o, w=w, reg=reg: e.matmul(
                    reg[:, o:o + w],
                    xb_all[:, dc * Tl + tt * 128:dc * Tl + tt * 128 + 128],
                    wbuf[:, dc * D + o:dc * D + o + w],
                    start=(dc == 0), stop=(dc == NCH - 1)),
                    inc=(dc == NCH - 1 and gi == len(d_sl) - 1)) or tmv
        tm_proj[("v", tt)] = tmv
        P.wait("vector", "tm", tmv)
        if tt >= 2:
            P.wait("vector", f"sde{p}", sd_store[("v", tt - 2)])
        else:
            P.wait("vector", f"sde{p}", sd_store[("k", NCH - 2 + tt)])
        ve_evac[("v", tt)] = emit_v(lambda e, p=p, reg=reg: e.tensor_copy(
            ev_t[:, p * D:(p + 1) * D], reg[:, 0:D]), inc=True)
        P.wait("sync", "ve", ve_evac[("v", tt)])
        sd_store[("v", tt)] = P.dma("sync", f"sde{p}", lambda e, tt=tt, p=p: e.dma_start(
            out=vloc.ap()[tt * 128:(tt + 1) * 128, :], in_=ev_t[:, p * D:(p + 1) * D]))

    P.wait("gpsimd", "sde0", P.ctr.get("sde0", 0))
    P.wait("gpsimd", "sde1", P.ctr.get("sde1", 0))
    P.emit("gpsimd", lambda e: e.collective_compute(
        "AllGather", ALU.bypass, replica_groups=rg,
        ins=[vloc.ap().opt()], outs=[vgat.ap().opt()]).then_inc(sems["cc"], 1))
    cc_v = P.inc("cc", eng="gpsimd")
    load_weight("q", wqT, lambda g: tm_proj[("v", TCH - 1)])
    emit_proj_fm("q", lambda ech: qdram[ech * 128:(ech + 1) * 128, :], prev_tag="v")
    sde_tot = [P.ctr.get("sde0", 0), P.ctr.get("sde1", 0)]
    load_weight("o", woT, lambda g: tm_proj[("q", 4 * g + 3)])

    kgat_r = kgat.ap().rearrange("(r d) t -> d r t", r=R)
    vgat_ap = vgat.ap()
    NVQ = 4
    JQ = KCH // NVQ

    def pool_own(jj):
        return jj % 3 == 0 and jj < 63

    ld_k = [0] * HP
    ld_v = [[0] * NVQ for _ in range(HP)]
    ld_qh = [0] * H
    tm_sc_g = {}
    tm_pv_g = {}
    ac_g = {}
    po_g = {}
    ve_add_g = {}
    ve_acs = {}
    tm_col = {}
    ac_ln = {}
    ac_rc = {}
    ve_ot = {}
    sd_od = {}

    def emit_kload(hp):
        slot = hp % 2
        for hh in range(2):
            h = hp * 2 + hh
            kdst = kp_buf[:, (slot * 2 + hh) * S:(slot * 2 + hh + 1) * S].rearrange(
                "p (r t) -> p r t", r=R)
            ld_k[hp] = P.dma("sync", f"lk{slot}", lambda e, h=h, kdst=kdst: e.dma_start(
                out=kdst, in_=kgat_r[h * 128:(h + 1) * 128, :, :]))

    def emit_vload(hp, eng):
        for vq in range(NVQ):
            vsrc = vgat_ap[vq * (S // NVQ):(vq + 1) * (S // NVQ),
                           hp * 256:(hp + 1) * 256].rearrange("(j p) u -> p j u", p=128)
            vdst = vp_buf[:, vq * JQ * 256:(vq + 1) * JQ * 256].rearrange(
                "p (j u) -> p j u", j=JQ)
            ld_v[hp][vq] = P.dma(eng, f"lv{vq}", lambda e, vsrc=vsrc, vdst=vdst: e.dma_start(
                out=vdst, in_=vsrc))

    tm_v_done = tm_proj[("q", NCH - 1)]
    P.wait("sync", "tm", tm_v_done)
    P.wait("sync", "cc", cc_v)
    P.wait("sync", "sde0", sde_tot[0])
    P.wait("sync", "sde1", sde_tot[1])
    emit_kload(0)
    emit_vload(0, "sync")
    if HP > 1:
        emit_kload(1)

    it = 0
    for hp in range(HP):
        slot = hp % 2
        for hh in range(2):
            h = hp * 2 + hh
            hq = h % 2
            if h >= 2:
                P.wait("sync", "tm",
                       tm_sc_g[((h - 2) * NQ + NQ - 1) * KCH + KCH - 1])
            ld_qh[h] = P.dma("sync", f"lq{hq}", lambda e, h=h, hq=hq: e.dma_start(
                out=qh_buf[:, hq * Tl:(hq + 1) * Tl], in_=qdram[h * 128:(h + 1) * 128, :]))

            for qh in range(NQ):
                qb = qh * QW
                bg = it * KCH
                s_pv = (it % 2) * QW
                it_loc = it

                def emit_scores_exp_pair(mm, bg=None, qb=None, hh=None, hq=None, slot=None, hp=None, h=None):
                    for jj in (2 * mm, 2 * mm + 1):
                        g = bg + jj
                        if g >= 2:
                            P.wait("tensor", "ac", ac_g[g - 2])
                        if jj == 0:
                            P.wait("tensor", f"lk{slot}", ld_k[hp])
                            P.wait("tensor", f"lq{hq}", ld_qh[h])
                        ps = (g % 2) * QW
                        for gi, (o, w) in enumerate(q_sl):
                            r = emit_t(lambda e, hh=hh, jj=jj, o=o, w=w, qb=qb, hq=hq,
                                       slot=slot, ps=ps: e.matmul(
                                pA[:, ps + o:ps + o + w],
                                kp_buf[:, (slot * 2 + hh) * S + jj * 128:(slot * 2 + hh) * S + jj * 128 + 128],
                                qh_buf[:, hq * Tl + qb + o:hq * Tl + qb + o + w],
                                start=True, stop=True),
                                inc=(gi == len(q_sl) - 1))
                            if r:
                                tm_sc_g[g] = r
                    for jj in (2 * mm, 2 * mm + 1):
                        g = bg + jj
                        P.wait("scalar", "tm", tm_sc_g[g])
                        if jj % 2 == 0 and g >= 8:
                            P.wait("scalar", "tm", tm_pv_g[g - 7])
                            pov = [po_g[c] for c in (g - 8, g - 7) if pool_own(c % KCH)]
                            vev = [ve_add_g[c] for c in (g - 8, g - 7) if not pool_own(c % KCH)]
                            if pov:
                                P.wait("scalar", "po", max(pov))
                            if vev:
                                P.wait("scalar", "ve", max(vev))
                        ac_g[g] = emit_s(lambda e, g=g: e.activation(
                            out=pt_buf[:, (g % 8) * QW:(g % 8) * QW + QW],
                            in_=pA[:, (g % 2) * QW:(g % 2) * QW + QW],
                            func=AF.Exp, bias=f32(cfg.SHIFT), scale=f32(cfg.SCALE)), inc=True)

                def emit_pv_acc_pair(mm, bg=None, hh=None, s_pv=None, hp=None, it_loc=None):
                    for jj in (2 * mm, 2 * mm + 1):
                        g = bg + jj
                        P.wait("tensor", "ac", ac_g[g])
                        if jj % JQ == 0:
                            P.wait("tensor", f"lv{jj // JQ}", ld_v[hp][jj // JQ])
                        if jj == 0 and it_loc >= 1:
                            P.wait("tensor", "ac", ac_ln[it_loc - 1])
                        tpv = 0
                        for gi, (o, w) in enumerate(q_sl):
                            tpv = emit_t(lambda e, jj=jj, hh=hh, o=o, w=w, s_pv=s_pv,
                                         pq=(g % 8) * QW: e.matmul(
                                pB[:, s_pv + o:s_pv + o + w],
                                vp_buf[:, jj * 256 + hh * 128:jj * 256 + hh * 128 + 128],
                                pt_buf[:, pq + o:pq + o + w],
                                start=(jj == 0), stop=(jj == KCH - 1)),
                                inc=(gi == len(q_sl) - 1)) or tpv
                        tm_pv_g[g] = tpv
                    for jj in (2 * mm, 2 * mm + 1):
                        g = bg + jj
                        if pool_own(jj):
                            P.wait("gpsimd", "ac", ac_g[g])
                            if jj == 0 and it_loc >= 1:
                                P.wait("gpsimd", "ve", ve_acs[it_loc - 1])
                            if jj == 0:
                                po_g[g] = emit_p(lambda e, g=g: e.tensor_copy(
                                    accP[:, 0:QW], pt_buf[:, (g % 8) * QW:(g % 8) * QW + QW]),
                                    inc=True)
                            else:
                                po_g[g] = emit_p(lambda e, g=g: e.tensor_tensor(
                                    accP[:, 0:QW], accP[:, 0:QW],
                                    pt_buf[:, (g % 8) * QW:(g % 8) * QW + QW], ALU.add),
                                    inc=True)
                        else:
                            P.wait("vector", "ac", ac_g[g])
                            if jj == 1:
                                ve_add_g[g] = emit_v(lambda e, g=g: e.tensor_copy(
                                    accD[:, 0:QW], pt_buf[:, (g % 8) * QW:(g % 8) * QW + QW]),
                                    inc=True)
                            else:
                                ve_add_g[g] = emit_v(lambda e, g=g: e.tensor_tensor(
                                    accD[:, 0:QW], accD[:, 0:QW],
                                    pt_buf[:, (g % 8) * QW:(g % 8) * QW + QW], ALU.add),
                                    inc=True)

                def emit_tail(pit):
                    pc = ((pit + 1) % 2) * QW
                    pv = (pit % 2) * QW
                    P.wait("tensor", "ve", ve_acs[pit])
                    if pit >= 1:
                        P.wait("tensor", "ve", ve_ot[pit - 1])
                    tcol = 0
                    for gi, (o, w) in enumerate(q_sl):
                        tcol = emit_t(lambda e, o=o, w=w, pc=pc: e.matmul(
                            pB[:, pc + o:pc + o + w], ones128b[:], accB[:, o:o + w],
                            start=True, stop=True), inc=(gi == len(q_sl) - 1)) or tcol
                    tm_col[pit] = tcol
                    P.wait("scalar", "tm", tm_col[pit])
                    ac_ln[pit] = emit_s(lambda e, pc=pc: e.activation(
                        out=linv[:, 0:QW], in_=pB[:, pc:pc + QW], func=AF.Ln,
                        bias=f32(0.0), scale=f32(1.0)), inc=True)
                    ac_rc[pit] = emit_s(lambda e: e.activation(
                        out=lb[:, 0:QW], in_=linv[:, 0:QW], func=AF.Exp,
                        bias=f32(0.0), scale=f32(-1.0)), inc=True)
                    osel = pit % 2
                    P.wait("vector", "ac", ac_rc[pit])
                    P.wait("vector", "tm", tm_pv_g[pit * KCH + KCH - 1])
                    if pit >= 2:
                        P.wait("vector", f"sod{osel}", sd_od[pit - 2])
                    ve_ot[pit] = emit_v(lambda e, pv=pv, osel=osel: e.tensor_tensor(
                        ot_buf[:, osel * QW:(osel + 1) * QW], pB[:, pv:pv + QW],
                        lb[:, 0:QW], ALU.mult), inc=True)
                    P.wait("sync", "ve", ve_ot[pit])
                    ph = pit % (NQ * H) // NQ
                    pqb = (pit % NQ) * QW
                    sd_od[pit] = P.dma("sync", f"sod{osel}", lambda e, ph=ph, pqb=pqb, osel=osel: e.dma_start(
                        out=odram[ph * 128:(ph + 1) * 128, pqb:pqb + QW],
                        in_=ot_buf[:, osel * QW:(osel + 1) * QW]))

                kw_se = dict(bg=bg, qb=qb, hh=hh, hq=hq, slot=slot, hp=hp, h=h)
                kw_pv = dict(bg=bg, hh=hh, s_pv=s_pv, hp=hp, it_loc=it_loc)
                for m in range(KCH // 2):
                    emit_scores_exp_pair(m, **kw_se)
                    if m == 1 and it_loc >= 1:
                        emit_tail(it_loc - 1)
                    if m >= 1:
                        emit_pv_acc_pair(m - 1, **kw_pv)
                emit_pv_acc_pair(KCH // 2 - 1, **kw_pv)
                P.wait("vector", "po", po_g[bg + 60])
                ve_acs[it_loc] = emit_v(lambda e: e.tensor_tensor(
                    accB[:, 0:QW], accD[:, 0:QW], accP[:, 0:QW], ALU.add), inc=True)
                if it_loc == H * NQ - 1:
                    emit_tail(it_loc)
                it += 1

        if hp + 1 < HP:
            P.wait("gpsimd", "tm", tm_pv_g[it * KCH - 1])
            emit_vload(hp + 1, "gpsimd")
        if hp + 2 < HP:
            P.wait("sync", "tm", tm_sc_g[it * KCH - 1])
            emit_kload(hp + 2)

    sod_tot = [P.ctr.get("sod0", 0), P.ctr.get("sod1", 0)]

    P.wait("sync", "sod0", sod_tot[0])
    P.wait("sync", "sod1", sod_tot[1])
    ld_oall = 0
    for dc in range(NCH):
        ld_oall = P.dma("sync", "lo", lambda e, dc=dc: e.dma_start(
            out=oall[:, dc * Tl:(dc + 1) * Tl], in_=odram[dc * 128:(dc + 1) * 128, :]))

    emit_v(lambda e: e.memset(rowA[0:1, 0:Tl], 0.0))
    ve_z5 = emit_v(lambda e: e.memset(rowC[0:1, 0:Tl], 0.0), inc=True)

    ve_ya = [0] * NCH
    sd_ya = [0] * NCH
    ac_sq5 = [0] * NCH
    ve_srC = 0
    tm_sum5 = [0] * NCH
    tm_sq5v = [0] * NCH
    tm_op = 0

    def emit_stats5(k):
        nonlocal ve_srC
        p = k % 2
        P.wait("tensor", "ac", ac_sq5[k])
        P.wait("tensor", "ve", ve_z5 if k == 0 else ve_srC)
        t1 = 0
        for gi, (o, w) in enumerate(tl_sl):
            t1 = emit_t(lambda e, p=p, o=o, w=w: e.matmul(
                pB[0:1, o:o + w], ones_bf[:], yab_t[:, p * Tl + o:p * Tl + o + w],
                start=True, stop=True), inc=(gi == len(tl_sl) - 1)) or t1
        tm_sum5[k] = t1
        P.wait("vector", "tm", t1)
        ve_srA = emit_v(lambda e: e.tensor_tensor(
            rowA[0:1, 0:Tl], rowA[0:1, 0:Tl], pB[0:1, 0:Tl], ALU.add), inc=True)
        P.wait("tensor", "ve", ve_srA)
        t2 = 0
        for gi, (o, w) in enumerate(tl_sl):
            t2 = emit_t(lambda e, p=p, o=o, w=w: e.matmul(
                pB[0:1, o:o + w], ones_bf[:], x2b_t[:, p * Tl + o:p * Tl + o + w],
                start=True, stop=True), inc=(gi == len(tl_sl) - 1)) or t2
        tm_sq5v[k] = t2
        P.wait("vector", "tm", t2)
        ve_srC = emit_v(lambda e: e.tensor_tensor(
            rowC[0:1, 0:Tl], rowC[0:1, 0:Tl], pB[0:1, 0:Tl], ALU.add), inc=True)

    for ech in range(NCH):
        p = ech % 2
        if ech % 4 == 0:
            P.wait("tensor", f"wl{ech // 4}", wl_w[("o", ech // 4)])
        if ech == 0:
            P.wait("tensor", "lo", ld_oall)
        else:
            P.wait("tensor", "ve", ve_ya[ech - 1])
        for dc in range(NCH):
            for gi, (o, w) in enumerate(tl_sl):
                tm_op = emit_t(lambda e, ech=ech, dc=dc, o=o, w=w: e.matmul(
                    pA[:, o:o + w],
                    wbuf[:, dc * D + ech * 128:dc * D + ech * 128 + 128],
                    oall[:, dc * Tl + o:dc * Tl + o + w],
                    start=(dc == 0), stop=(dc == NCH - 1)),
                    inc=(dc == NCH - 1 and gi == len(tl_sl) - 1)) or tm_op
        if ech >= 1:
            emit_stats5(ech - 1)
        P.wait("vector", "tm", tm_op)
        if ech >= 2:
            P.wait("vector", f"sya{p}", sd_ya[ech - 2])
            P.wait("vector", "ac", ac_sq5[ech - 2])
        ve_ya[ech] = emit_v(lambda e, p=p: e.tensor_copy(
            xf_t[:, p * Tl:(p + 1) * Tl], pA[:, 0:Tl]), inc=True)
        P.wait("sync", "ve", ve_ya[ech])
        sd_ya[ech] = P.dma("sync", f"sya{p}", lambda e, ech=ech, p=p: e.dma_start(
            out=yadram[ech * 128:(ech + 1) * 128, :], in_=xf_t[:, p * Tl:(p + 1) * Tl]))
        if ech >= 2:
            P.wait("vector", "tm", tm_sum5[ech - 2])
        ve_yab = emit_v(lambda e, p=p: e.tensor_copy(
            yab_t[:, p * Tl:(p + 1) * Tl], xf_t[:, p * Tl:(p + 1) * Tl]), inc=True)
        P.wait("scalar", "ve", ve_yab)
        if ech >= 2:
            P.wait("scalar", "tm", tm_sq5v[ech - 2])
        ac_sq5[ech] = emit_s(lambda e, p=p: e.activation(
            out=x2b_t[:, p * Tl:(p + 1) * Tl], in_=xf_t[:, p * Tl:(p + 1) * Tl],
            func=AF.Square), inc=True)
    emit_stats5(NCH - 1)

    ve_mr2 = ln_rows2(Tl, "ve", ve_srC, False)
    ve_b2 = ln_bcast(Tl, ve_mr2)

    sd_x1 = [0] * NCH
    ve_x1b = [0] * NCH
    ac_sq3 = [0] * NCH
    tm_sq3 = 0
    for c in range(NCH):
        p = c % 2
        if c < 2:
            P.wait("sync", "ac", ac_sq5[NCH - 1])
        else:
            P.wait("sync", "ve", ve_x1b[c - 2])
        P.wait("sync", f"sya{p}", sd_ya[NCH - 2 + p] if NCH >= 2 else sd_ya[0])
        P.dma("sync", f"lya{p}", lambda e, c=c, p=p: e.dma_start(
            out=xf_t[:, p * Tl:(p + 1) * Tl], in_=yadram[c * 128:(c + 1) * 128, :]))
        ld_ya = P.ctr[f"lya{p}"]
        if c >= 1:
            P.wait("sync", "sx1", sd_x1[c - 1])
            P.wait("sync", "ac", ac_sq3[c - 1])
        ld_x = P.dma("sync", "lx1", lambda e, c=c: e.dma_start(
            out=x1_t[:, 0:Tl], in_=xT[c * 128:(c + 1) * 128, :]))
        P.wait("vector", f"lya{p}", ld_ya)
        P.wait("vector", "lx1", ld_x)
        if c == 0:
            P.wait("vector", "ve", ve_b2)
        emit_v(lambda e, p=p: e.tensor_tensor(
            xf_t[:, p * Tl:(p + 1) * Tl], xf_t[:, p * Tl:(p + 1) * Tl], rbb[:, 0:Tl], ALU.mult))
        emit_v(lambda e, p=p: e.tensor_tensor(
            xf_t[:, p * Tl:(p + 1) * Tl], xf_t[:, p * Tl:(p + 1) * Tl], mrbb[:, 0:Tl], ALU.subtract))
        emit_v(lambda e, p=p: e.tensor_tensor(
            x1_t[:, 0:Tl], x1_t[:, 0:Tl], xf_t[:, p * Tl:(p + 1) * Tl], ALU.add))
        ve_x1b[c] = emit_v(lambda e, c=c: e.tensor_copy(
            xb_all[:, c * Tl:(c + 1) * Tl], x1_t[:, 0:Tl]), inc=True)
        P.wait("sync", "ve", ve_x1b[c])
        sd_x1[c] = P.dma("sync", "sx1", lambda e, c=c: e.dma_start(
            out=x1dram[c * 128:(c + 1) * 128, :], in_=x1_t[:, 0:Tl]))
        P.wait("scalar", "ve", ve_x1b[c])
        if c >= 1:
            P.wait("scalar", "tm", tm_sq3)
        ac_sq3[c] = emit_s(lambda e, c=c: e.activation(
            out=x2b_t[:, 0:Tl], in_=x1_t[:, 0:Tl], func=AF.Square), inc=True)
        P.wait("tensor", "ac", ac_sq3[c])
        if c == 0:
            P.wait("tensor", "ve", ve_b2)
        for (o, w) in tl_sl:
            emit_t(lambda e, c=c, o=o, w=w: e.matmul(
                pA[0:1, o:o + w], ones_bf[:], xb_all[:, c * Tl + o:c * Tl + o + w],
                start=(c == 0), stop=(c == NCH - 1)))
        for gi, (o, w) in enumerate(tl_sl):
            tm_sq3 = emit_t(lambda e, c=c, o=o, w=w: e.matmul(
                pB[0:1, o:o + w], ones_bf[:], x2b_t[:, o:o + w],
                start=(c == 0), stop=(c == NCH - 1)), inc=(gi == len(tl_sl) - 1)) or tm_sq3

    ve_mr3 = ln_rows2(Tl, "tm", tm_sq3, True)
    ve_b3 = ln_bcast(Tl, ve_mr3)
    P.wait("vector", "ve", ve_b3)
    ve_n3 = 0
    for c in range(NCH):
        emit_v(lambda e, c=c: e.tensor_tensor(
            yab_t[:, 0:Tl], xb_all[:, c * Tl:(c + 1) * Tl], rbb[:, 0:Tl], ALU.mult))
        ve_n3 = emit_v(lambda e, c=c: e.tensor_tensor(
            xb_all[:, c * Tl:(c + 1) * Tl], yab_t[:, 0:Tl], mrbb[:, 0:Tl], ALU.subtract), inc=True)

    n_out = 0
    so_out = {}
    tm_gu_done = {}
    wl_gsl = {}
    ve_nrm_last = 0
    ve_dn_prev = 0
    ve_b4 = 0
    sx1_tot = P.ctr.get("sx1", 0)
    for ee in range(E):
        ecol = ee * TE
        CPS = GW // 128
        ngroups = NSLAB // 2

        def emit_gu_piece(ee, slot, j):
            if ee == 0 and j < CPS:
                P.wait("gpsimd", "tm", tm_op)
            elif j < CPS:
                P.wait("gpsimd", "tm", tm_gu_done[
                    (ee - 1, (ICH - CPS + j) + (0 if slot == 0 else ICH))])
            else:
                P.wait("gpsimd", "tm", tm_gu_done[
                    (ee, (j - CPS) + (0 if slot == 0 else ICH))])
            col = j * 128 + (0 if slot == 0 else I)
            srcw = wgu[ee, :, col:col + 128].rearrange("(dc p) i -> p dc i", p=128)
            sl = gslab[:, slot * NCH * GW:(slot + 1) * NCH * GW].rearrange(
                "p (dc i) -> p dc i", dc=NCH)
            gdst = sl[:, :, (j % CPS) * 128:(j % CPS) * 128 + 128]
            wl_gsl[(ee, slot, j)] = P.dma("gpsimd", f"wg{slot}{j % CPS}",
                                          lambda e, srcw=srcw, gdst=gdst: e.dma_start(out=gdst, in_=srcw))

        for j0 in range(CPS):
            emit_gu_piece(ee, 0, j0)
            emit_gu_piece(ee, 1, j0)

        ac_sg = [0] * ICH
        ve_in = [0] * ICH
        for ihalf in range(ICH):
            for which in ("g", "u"):
                icg = ihalf if which == "g" else ihalf + ICH
                offw = (ihalf % CPS) * 128
                slot = 0 if which == "g" else 1
                base = (ihalf % 2) * 1024 + (0 if which == "g" else 512)
                P.wait("tensor", f"wg{slot}{ihalf % CPS}", wl_gsl[(ee, slot, ihalf)])
                if ee == 0 and ihalf == 0 and which == "g":
                    P.wait("tensor", "ve", ve_n3)
                if ihalf == 0 and which == "g" and ee > 0:
                    P.wait("tensor", "ve", ve_b4)
                if ihalf >= 2:
                    if which == "g":
                        P.wait("tensor", "ac", ac_sg[ihalf - 2])
                    else:
                        P.wait("tensor", "ve", ve_in[ihalf - 2])
                tmg = 0
                for dc in range(NCH):
                    for gi, (o, w) in enumerate(te_sl):
                        tmg = emit_t(lambda e, offw=offw, dc=dc, o=o, w=w,
                                     base=base, slot=slot, ecol=ecol: e.matmul(
                            pA[:, base + o:base + o + w],
                            gslab[:, slot * NCH * GW + dc * GW + offw:
                                  slot * NCH * GW + dc * GW + offw + 128],
                            xb_all[:, dc * Tl + ecol + o:dc * Tl + ecol + o + w],
                            start=(dc == 0), stop=(dc == NCH - 1)),
                            inc=(dc == NCH - 1 and gi == len(te_sl) - 1)) or tmg
                tm_gu_done[(ee, icg)] = tmg
            P.wait("scalar", "tm", tm_gu_done[(ee, ihalf)])
            if ihalf >= 2:
                P.wait("scalar", "ve", ve_in[ihalf - 2])
            ac_sg[ihalf] = emit_s(lambda e, ihalf=ihalf: e.activation(
                out=sg_t[:, (ihalf % 2) * TE:(ihalf % 2 + 1) * TE],
                in_=pA[:, (ihalf % 2) * 1024:(ihalf % 2) * 1024 + TE], func=AF.Silu), inc=True)
            P.wait("vector", "ac", ac_sg[ihalf])
            P.wait("vector", "tm", tm_gu_done[(ee, ihalf + ICH)])
            ve_in[ihalf] = emit_v(lambda e, ihalf=ihalf: e.tensor_tensor(
                inter[:, ihalf * TE:(ihalf + 1) * TE],
                pA[:, (ihalf % 2) * 1024 + 512:(ihalf % 2) * 1024 + 512 + TE],
                sg_t[:, (ihalf % 2) * TE:(ihalf % 2 + 1) * TE], ALU.mult), inc=True)
            if ihalf + CPS < ICH:
                emit_gu_piece(ee, 0, ihalf + CPS)
                emit_gu_piece(ee, 1, ihalf + CPS)

        ve_dn = [0] * NCH
        ac_c6 = [0] * NCH
        ve_c6 = [0] * NCH
        tm_sum6 = [0] * NCH
        tm_sq6 = [0] * NCH
        tm_st6 = 0

        def emit_stats6(k):
            nonlocal tm_st6
            pk = k % 2
            P.wait("tensor", "ac", ac_c6[k])
            if k == 0 and ee == 0:
                P.wait("tensor", "ve", ve_b3)
            if k == 0 and ee > 0:
                P.wait("tensor", "ve", ve_b4)
            t1 = 0
            for gi, (o, w) in enumerate(te_sl):
                t1 = emit_t(lambda e, k=k, pk=pk, o=o, w=w: e.matmul(
                    pA[0:1, o:o + w], ones_bf[:], cp6_t[:, pk * TE + o:pk * TE + o + w],
                    start=(k == 0), stop=(k == NCH - 1)), inc=(gi == len(te_sl) - 1)) or t1
            tm_sum6[k] = t1
            t2 = 0
            for gi, (o, w) in enumerate(te_sl):
                t2 = emit_t(lambda e, k=k, pk=pk, o=o, w=w: e.matmul(
                    pA[0:1, 512 + o:512 + o + w], ones_bf[:], sq6_t[:, pk * TE + o:pk * TE + o + w],
                    start=(k == 0), stop=(k == NCH - 1)), inc=(gi == len(te_sl) - 1)) or t2
            tm_sq6[k] = t2
            tm_st6 = t2

        for hc in range(NCH):
            hp2 = hc % 2
            src = wdn[ee, :, hc * 128:(hc + 1) * 128].rearrange("(ic p) hh -> p ic hh", p=128)
            if ee == 0 and hc < 2:
                P.wait("gpsimd", "tm", tm_op)
            elif hc >= 2:
                P.wait("gpsimd", "tm", tm_gu_done[("dn", ee, hc - 2)])
            else:
                P.wait("gpsimd", "tm", tm_gu_done[("dn", ee - 1, NCH - 1)])
            ddst = dslab[:, hp2 * ICH * 128:(hp2 + 1) * ICH * 128].rearrange(
                "p (ic hh) -> p ic hh", ic=ICH)
            wl_dn = P.dma("gpsimd", f"wd{hp2}", lambda e, src=src, ddst=ddst: e.dma_start(
                out=ddst, in_=src))
            P.wait("tensor", f"wd{hp2}", wl_dn)
            if hc == 0:
                P.wait("tensor", "ve", ve_in[ICH - 1])
                if ee > 0:
                    P.wait("tensor", "ve", ve_dn_prev)
            if hc >= 2:
                P.wait("tensor", "ve", ve_dn[hc - 2])
            tm_dn = 0
            for ic in range(ICH):
                for gi, (o, w) in enumerate(te_sl):
                    tm_dn = emit_t(lambda e, hp2=hp2, ic=ic, o=o, w=w: e.matmul(
                        pB[:, hp2 * 512 + o:hp2 * 512 + o + w],
                        dslab[:, hp2 * ICH * 128 + ic * 128:hp2 * ICH * 128 + ic * 128 + 128],
                        inter[:, ic * TE + o:ic * TE + o + w],
                        start=(ic == 0), stop=(ic == ICH - 1)),
                        inc=(ic == ICH - 1 and gi == len(te_sl) - 1)) or tm_dn
            tm_gu_done[("dn", ee, hc)] = tm_dn
            if hc >= 1:
                emit_stats6(hc - 1)
            P.wait("vector", "tm", tm_dn)
            if ee > 0 and hc == 0:
                P.wait("vector", "ve", ve_nrm_last)
            ve_dn[hc] = emit_v(lambda e, hc=hc, hp2=hp2: e.tensor_copy(
                yf_e[:, hc * TE:(hc + 1) * TE], pB[:, hp2 * 512:hp2 * 512 + TE]), inc=True)
            if hc >= 2:
                P.wait("vector", "tm", tm_sum6[hc - 2])
            ve_c6[hc] = emit_v(lambda e, hc=hc, hp2=hp2: e.tensor_copy(
                cp6_t[:, hp2 * TE:(hp2 + 1) * TE], yf_e[:, hc * TE:(hc + 1) * TE]), inc=True)
            P.wait("scalar", "ve", ve_c6[hc])
            if hc >= 2:
                P.wait("scalar", "tm", tm_sq6[hc - 2])
            ac_c6[hc] = emit_s(lambda e, hc=hc, hp2=hp2: e.activation(
                out=sq6_t[:, hp2 * TE:(hp2 + 1) * TE], in_=yf_e[:, hc * TE:(hc + 1) * TE],
                func=AF.Square), inc=True)
        emit_stats6(NCH - 1)
        ve_dn_prev = ve_dn[NCH - 1]

        P.wait("vector", "tm", tm_st6)
        emit_v(lambda e: e.tensor_scalar_mul(rowA[0:1, 0:TE], pA[0:1, 0:TE], konst[0:1, 0:1]))
        v1 = emit_v(lambda e: e.tensor_scalar_mul(rowC[0:1, 0:TE], pA[0:1, 512:512 + TE], konst[0:1, 0:1]), inc=True)
        P.wait("scalar", "ve", v1)
        a1 = emit_s(lambda e: e.activation(out=mrbb[0:1, 0:TE], in_=rowA[0:1, 0:TE], func=AF.Square), inc=True)
        P.wait("vector", "ac", a1)
        v2 = emit_v(lambda e: e.tensor_tensor(rowC[0:1, 0:TE], rowC[0:1, 0:TE], mrbb[0:1, 0:TE], ALU.subtract), inc=True)
        P.wait("scalar", "ve", v2)
        emit_s(lambda e: e.activation(out=rowC[0:1, 0:TE], in_=rowC[0:1, 0:TE], func=AF.Ln, bias=f32(cfg.EPS), scale=f32(1.0)))
        a2 = emit_s(lambda e: e.activation(out=rowC[0:1, 0:TE], in_=rowC[0:1, 0:TE], func=AF.Exp, bias=f32(0.0), scale=f32(-0.5)), inc=True)
        P.wait("vector", "ac", a2)
        v3 = emit_v(lambda e: e.tensor_tensor(rowA[0:1, 0:TE], rowA[0:1, 0:TE], rowC[0:1, 0:TE], ALU.mult), inc=True)
        P.wait("tensor", "ve", v3)
        for (o, w) in te_sl:
            emit_t(lambda e, o=o, w=w: e.matmul(
                pA[:, 1024 + o:1024 + o + w], ones_row[:], rowC[0:1, o:o + w], start=True, stop=True))
        tm_b4 = 0
        for gi, (o, w) in enumerate(te_sl):
            tm_b4 = emit_t(lambda e, o=o, w=w: e.matmul(
                pA[:, 1536 + o:1536 + o + w], ones_row[:], rowA[0:1, o:o + w],
                start=True, stop=True), inc=(gi == len(te_sl) - 1)) or tm_b4
        P.wait("vector", "tm", tm_b4)
        emit_v(lambda e: e.tensor_copy(rbb[:, 0:TE], pA[:, 1024:1024 + TE]))
        ve_b4 = emit_v(lambda e: e.tensor_copy(mrbb[:, 0:TE], pA[:, 1536:1536 + TE]), inc=True)

        ve_nrm = {}
        for c in range(NCH):
            slot = c % 4
            if ee == 0 and c == 0:
                P.wait("sync", "ac", ac_sq3[NCH - 1])
                P.wait("sync", "sx1", sx1_tot)
            if c >= 4:
                P.wait("sync", "ve", ve_nrm[c - 4])
            elif ee > 0:
                P.wait("sync", "ve", ve_nrm_last)
            ld_x1c = P.dma("sync", f"lx6{slot}", lambda e, c=c, ecol=ecol, slot=slot: e.dma_start(
                out=x1_t[:, slot * TE:(slot + 1) * TE],
                in_=x1dram[c * 128:(c + 1) * 128, ecol:ecol + TE]))
            P.wait("vector", f"lx6{slot}", ld_x1c)
            if c == 0:
                P.wait("vector", "ve", ve_b4)
            osel = n_out % 2
            if n_out >= 2:
                P.wait("vector", f"sout{osel}", so_out[n_out - 2])
            emit_v(lambda e, c=c: e.tensor_tensor(
                o1_t[:, 0:TE], yf_e[:, c * TE:(c + 1) * TE], rbb[:, 0:TE], ALU.mult))
            emit_v(lambda e: e.tensor_tensor(
                o1_t[:, 0:TE], o1_t[:, 0:TE], mrbb[:, 0:TE], ALU.subtract))
            ve_nrm[c] = emit_v(lambda e, slot=slot, osel=osel: e.tensor_tensor(
                out_t[:, osel * TE:(osel + 1) * TE],
                x1_t[:, slot * TE:(slot + 1) * TE], o1_t[:, 0:TE], ALU.add), inc=True)
            P.wait("sync", "ve", ve_nrm[c])
            so_out[n_out] = P.dma("sync", f"sout{osel}", lambda e, c=c, ecol=ecol, osel=osel: e.dma_start(
                out=outT[c * 128:(c + 1) * 128, ecol:ecol + TE],
                in_=out_t[:, osel * TE:(osel + 1) * TE]))
            n_out += 1
        ve_nrm_last = ve_nrm[NCH - 1]

    P.wait("sync", "sout0", P.ctr.get("sout0", 0))
    P.wait("sync", "sout1", P.ctr.get("sout1", 0))
    P.wait("gpsimd", "sout0", P.ctr.get("sout0", 0))
    P.wait("gpsimd", "sout1", P.ctr.get("sout1", 0))

    with nc.Block() as block:
        @block.sync
        def _(e):
            for fn in P.ops["sync"]:
                fn(e)

        @block.gpsimd
        def _(e):
            for fn in P.ops["gpsimd"]:
                fn(e)

        @block.tensor
        def _(e):
            for fn in P.ops["tensor"]:
                fn(e)

        @block.vector
        def _(e):
            for fn in P.ops["vector"]:
                fn(e)

        @block.scalar
        def _(e):
            for fn in P.ops["scalar"]:
                fn(e)

    return nc


def _prep_core_inputs(x, wq, wk, wv, wo, gup, dnp, cfg, core):
    b = core // cfg.R
    t = core % cfg.R
    perm = np.concatenate([np.arange(e, cfg.Tl, cfg.E) for e in range(cfg.E)])
    sl = x[b, t * cfg.Tl:(t + 1) * cfg.Tl, :][perm, :]
    return {
        "xT": np.ascontiguousarray(sl.T),
        "wqT": np.ascontiguousarray(wq.T),
        "wkT": np.ascontiguousarray(wk.T),
        "wvT": np.ascontiguousarray(wv.T),
        "woT": np.ascontiguousarray(wo.T),
        "wgu": np.ascontiguousarray(gup),
        "wdn": np.ascontiguousarray(dnp),
    }, perm


def kernel(L0_attn_x, L0_attn_w_q, L0_attn_w_k, L0_attn_w_v, L0_attn_w_o,
           L0_ffn_gate_up_proj, L0_ffn_down_proj):
    cfg = CFG()
    x = np.asarray(L0_attn_x, dtype=np.float32)
    wq = np.asarray(L0_attn_w_q, dtype=np.float32)
    wk = np.asarray(L0_attn_w_k, dtype=np.float32)
    wv = np.asarray(L0_attn_w_v, dtype=np.float32)
    wo = np.asarray(L0_attn_w_o, dtype=np.float32)
    gup = np.asarray(L0_ffn_gate_up_proj, dtype=np.float32)
    dnp = np.asarray(L0_ffn_down_proj, dtype=np.float32)

    in_maps = []
    perms = []
    for core in range(8):
        m, perm = _prep_core_inputs(x, wq, wk, wv, wo, gup, dnp, cfg, core)
        in_maps.append(m)
        perms.append(perm)

    nc = build_nc(cfg)
    res = run_bass_kernel_spmd(nc, in_maps, core_ids=list(range(8)))

    out = np.empty((2, 8192, 2048), dtype=np.float32)
    for core in range(8):
        b = core // cfg.R
        t = core % cfg.R
        shard = np.asarray(res.results[core]["outT"], dtype=np.float32).T
        sl = np.empty_like(shard)
        sl[perms[core], :] = shard
        out[b, t * cfg.Tl:(t + 1) * cfg.Tl, :] = sl
    return out


# version 22
# speedup vs baseline: 1.0125x; 1.0052x over previous
import numpy as np

import concourse.bass as bass
import concourse.mybir as mybir
from concourse.bass_utils import run_bass_kernel_spmd

F32 = mybir.dt.float32
BF16 = mybir.dt.bfloat16
AF = mybir.ActivationFunctionType
ALU = mybir.AluOpType
f32 = np.float32

ENGINES = ("sync", "gpsimd", "tensor", "vector", "scalar")
NW = 512


class CFG:
    def __init__(self, D=2048, Tl=2048, H=16, E=4, I=2048, R=4):
        self.D, self.Tl, self.H, self.E, self.I, self.R = D, Tl, H, E, I, R
        self.HD = 128
        assert H * self.HD == D
        self.S = R * Tl
        self.NCH = D // 128
        self.TCH = Tl // 128
        self.KCH = self.S // 128
        self.QW = min(1024, Tl)
        self.NQ = Tl // self.QW
        self.TE = Tl // E
        self.ICH = I // 128
        self.GW = min(512, I)
        self.NSLAB = (2 * I) // self.GW
        self.EPS = 1e-5
        self.SCALE = 1.0 / float(np.sqrt(self.HD))
        self.SHIFT = -5.0


class Plan:
    def __init__(self, sems):
        self.ops = {e: [] for e in ENGINES}
        self.ctr = {k: 0 for k in sems}
        self.sems = sems

    def emit(self, eng, fn):
        self.ops[eng].append(fn)

    def inc(self, sem, n=1):
        self.ctr[sem] += n
        return self.ctr[sem]

    def wait(self, eng, sem, v):
        if v > 0:
            s = self.sems[sem]
            self.ops[eng].append(lambda e, s=s, v=v: e.wait_ge(s, v))

    def drain(self, eng):
        self.ops[eng].append(lambda e: e.drain())


def _sl(width, nw=NW):
    out, o = [], 0
    while o < width:
        w = min(nw, width - o)
        out.append((o, w))
        o += w
    return out


def build_nc(cfg: CFG):
    nc = bass.Bass("TRN2", target_bir_lowering=False)
    D, Tl, H, E, I, R = cfg.D, cfg.Tl, cfg.H, cfg.E, cfg.I, cfg.R
    NCH, TCH, KCH, QW, NQ, TE, ICH = (cfg.NCH, cfg.TCH, cfg.KCH, cfg.QW,
                                      cfg.NQ, cfg.TE, cfg.ICH)
    S, GW, NSLAB = cfg.S, cfg.GW, cfg.NSLAB
    HP = H // 2

    xT = nc.dram_tensor("xT", [D, Tl], F32, kind="ExternalInput").ap()
    wqT = nc.dram_tensor("wqT", [D, D], F32, kind="ExternalInput").ap()
    wkT = nc.dram_tensor("wkT", [D, D], F32, kind="ExternalInput").ap()
    wvT = nc.dram_tensor("wvT", [D, D], F32, kind="ExternalInput").ap()
    woT = nc.dram_tensor("woT", [D, D], F32, kind="ExternalInput").ap()
    wgu = nc.dram_tensor("wgu", [E, D, 2 * I], F32, kind="ExternalInput").ap()
    wdn = nc.dram_tensor("wdn", [E, I, D], F32, kind="ExternalInput").ap()
    outT = nc.dram_tensor("outT", [D, Tl], F32, kind="ExternalOutput").ap()

    kloc = nc.dram_tensor("kloc", [D, Tl], BF16)
    vloc = nc.dram_tensor("vloc", [Tl, D], BF16)
    kgat_g = [nc.dram_tensor(f"kgat{g}", [R * 512, Tl], BF16) for g in range(4)]
    vgat = nc.dram_tensor("vgat", [R * Tl, D], BF16)
    qdram = nc.dram_tensor("qdram", [D, Tl], BF16).ap()
    odram = nc.dram_tensor("odram", [D, Tl], BF16).ap()
    yadram = nc.dram_tensor("yadram", [D, Tl], F32).ap()
    x1dram = nc.dram_tensor("x1dram", [D, Tl], F32).ap()

    def sb(name, shape, dtype, off):
        return nc.alloc_sbuf_tensor_at(name, list(shape), dtype, offset=ARENA + off)

    A0 = 0
    a_sz = max(NCH * Tl * 2, 2 * 2 * S * 2)
    B0 = A0 + a_sz
    b_p6 = [("gsl", 2 * NCH * GW * 2), ("dsl", 2 * ICH * 128 * 2),
            ("sg", 2 * TE * 4), ("intr", ICH * TE * 2), ("outt", 2 * TE * 4)]
    b_sz = max(NCH * D * 2, sum(s for _, s in b_p6))
    C0 = B0 + b_sz
    c_sz = max(2 * Tl * 4 + 2 * Tl * 2 + 2 * Tl * 2, 2 * D * 2, KCH * 256 * 2, NCH * TE * 4)
    D0 = C0 + c_sz
    E0 = D0 + Tl * 4
    F0 = E0 + 2 * Tl * 4
    G0 = F0 + 2 * Tl * 4
    need = G0 + 1152 + TE * 4 + 4 * TE * 2
    assert need <= nc.sbuf_bytes_remaining, (need, nc.sbuf_bytes_remaining)
    _resv = nc.alloc_sbuf_tensor("arena_reserve", [128, need], mybir.dt.uint8)
    ARENA = nc.lookup_mloc(_resv).addr

    xb_all = sb("xb_all", [128, NCH * Tl], BF16, A0)
    kp_buf = sb("kp_buf", [128, 2 * 2 * S], BF16, A0)
    oall = sb("oall", [128, NCH * Tl], BF16, A0)

    wbuf = sb("wbuf", [128, NCH * D], BF16, B0)
    off = B0
    gslab = sb("gslab", [128, 2 * NCH * GW], BF16, off); off += b_p6[0][1]
    dslab = sb("dslab", [128, 2 * ICH * 128], BF16, off); off += b_p6[1][1]
    sg_t = sb("sg_t", [128, 2 * TE], F32, off); off += b_p6[2][1]
    inter = sb("inter", [128, ICH * TE], BF16, off); off += b_p6[3][1]
    out_t = sb("out_t", [128, 2 * TE], F32, off)

    xf_t = sb("xf_t", [128, 2 * Tl], F32, C0)
    x2b_t = sb("x2b_t", [128, 2 * Tl], BF16, C0 + 2 * Tl * 4)
    y1_t = sb("y1_t", [128, Tl], BF16, C0 + 2 * Tl * 4 + 2 * Tl * 2)
    ev_t = sb("ev_t", [128, 2 * D], BF16, C0)
    vp_buf = sb("vp_buf", [128, KCH * 256], BF16, C0)
    yab_t = sb("yab_t", [128, 2 * Tl], BF16, C0 + 2 * Tl * 4 + 2 * Tl * 2)
    yf_e = sb("yf_e", [128, NCH * TE], F32, C0)

    x1_t = sb("x1_t", [128, Tl], F32, D0)
    qh_buf = sb("qh_buf", [128, 2 * Tl], BF16, D0)
    rowA = sb("rowA", [1, Tl], F32, E0)
    pt_buf = sb("pt_buf", [128, 8 * QW], BF16, E0)
    rowC = sb("rowC", [1, Tl], F32, E0 + Tl * 4)
    rbb = sb("rbb", [128, Tl], F32, F0)
    ot_buf = sb("ot_buf", [128, 2 * QW], BF16, F0)
    mrbb = sb("mrbb", [128, Tl], F32, F0 + Tl * 4)
    accD = sb("accD", [128, QW], F32, F0 + Tl * 4)
    accP = sb("accP", [128, QW], F32, F0 + Tl * 4 + QW * 4)
    ones_bf = sb("ones_bf", [128, 1], BF16, G0)
    ones_row = sb("ones_row", [1, 128], F32, G0 + 32)
    konst = sb("konst", [1, 4], F32, G0 + 544)
    ones128b = sb("ones128b", [128, 128], BF16, G0 + 576)
    o1_t = sb("o1_t", [128, TE], F32, G0 + 1152)
    accB = sb("accB", [128, QW], BF16, G0 + 1152)
    sq6_t = sb("sq6_t", [128, 2 * TE], BF16, G0 + 1152 + TE * 4)
    cp6_t = sb("cp6_t", [128, 2 * TE], BF16, G0 + 1152 + TE * 4 + 2 * TE * 2)
    linv = sb("linv", [128, QW], F32, G0 + 1152 + TE * 4)

    pA = nc.alloc_psum_tensor("pA", [128, 2048], F32)
    pB = nc.alloc_psum_tensor("pB", [128, 2048], F32)

    class LazySems(dict):
        def __missing__(self, k):
            self[k] = nc.alloc_semaphore(k)
            return self[k]

    sems = LazySems()

    class Plan2:
        def __init__(self):
            self.ops = {e: [] for e in ENGINES}
            self.ctr = {}
            self.log = {e: [] for e in ENGINES}

        def emit(self, eng, fn):
            self.ops[eng].append(fn)

        def inc(self, sem, n=1, eng=None):
            self.ctr[sem] = self.ctr.get(sem, 0) + n
            if eng is not None:
                self.log[eng].append(("inc", sem, n, False))
            return self.ctr[sem]

        def wait(self, eng, sem, v):
            if v > 0:
                s = sems[sem]
                self.ops[eng].append(lambda e, s=s, v=v: e.wait_ge(s, v))
                self.log[eng].append(("wait", sem, v, False))

        def dma(self, eng, sem, fn):
            s = sems[sem]
            self.ops[eng].append(lambda e, fn=fn, s=s: fn(e).then_inc(s, 16))
            self.log[eng].append(("inc", sem, 16, True))
            return self.inc(sem, 16)

        def drain(self, eng):
            self.ops[eng].append(lambda e: e.drain())

    P = Plan2()
    tl_sl = _sl(Tl)
    d_sl = _sl(D)
    q_sl = _sl(QW)
    te_sl = _sl(TE)

    def emit_v(fn, inc=False):
        if inc:
            P.emit("vector", lambda e, fn=fn: fn(e).then_inc(sems["ve"], 1))
            return P.inc("ve", eng="vector")
        P.emit("vector", lambda e, fn=fn: fn(e))
        return None

    def emit_s(fn, inc=False):
        if inc:
            P.emit("scalar", lambda e, fn=fn: fn(e).then_inc(sems["ac"], 1))
            return P.inc("ac", eng="scalar")
        P.emit("scalar", lambda e, fn=fn: fn(e))
        return None

    def emit_t(fn, inc=False):
        if inc:
            P.emit("tensor", lambda e, fn=fn: fn(e).then_inc(sems["tm"], 1))
            return P.inc("tm", eng="tensor")
        P.emit("tensor", lambda e, fn=fn: fn(e))
        return None

    def emit_p(fn, inc=False):
        if inc:
            P.emit("gpsimd", lambda e, fn=fn: fn(e).then_inc(sems["po"], 1))
            return P.inc("po", eng="gpsimd")
        P.emit("gpsimd", lambda e, fn=fn: fn(e))
        return None

    def ln_rows2(width, wait_sem, wait_val, from_psum):
        P.wait("vector", wait_sem, wait_val)
        if from_psum:
            emit_v(lambda e: e.tensor_scalar_mul(rowA[0:1, 0:width], pA[0:1, 0:width], konst[0:1, 0:1]))
            v1 = emit_v(lambda e: e.tensor_scalar_mul(rowC[0:1, 0:width], pB[0:1, 0:width], konst[0:1, 0:1]), inc=True)
        else:
            emit_v(lambda e: e.tensor_scalar_mul(rowA[0:1, 0:width], rowA[0:1, 0:width], konst[0:1, 0:1]))
            v1 = emit_v(lambda e: e.tensor_scalar_mul(rowC[0:1, 0:width], rowC[0:1, 0:width], konst[0:1, 0:1]), inc=True)
        P.wait("scalar", "ve", v1)
        a1 = emit_s(lambda e: e.activation(out=mrbb[0:1, 0:width], in_=rowA[0:1, 0:width], func=AF.Square), inc=True)
        P.wait("vector", "ac", a1)
        v2 = emit_v(lambda e: e.tensor_tensor(rowC[0:1, 0:width], rowC[0:1, 0:width], mrbb[0:1, 0:width], ALU.subtract), inc=True)
        P.wait("scalar", "ve", v2)
        emit_s(lambda e: e.activation(out=rowC[0:1, 0:width], in_=rowC[0:1, 0:width], func=AF.Ln, bias=f32(cfg.EPS), scale=f32(1.0)))
        a2 = emit_s(lambda e: e.activation(out=rowC[0:1, 0:width], in_=rowC[0:1, 0:width], func=AF.Exp, bias=f32(0.0), scale=f32(-0.5)), inc=True)
        P.wait("vector", "ac", a2)
        v3 = emit_v(lambda e: e.tensor_tensor(rowA[0:1, 0:width], rowA[0:1, 0:width], rowC[0:1, 0:width], ALU.mult), inc=True)
        return v3

    def ln_bcast(width, ve_mr):
        P.wait("tensor", "ve", ve_mr)
        sl = _sl(width)
        for (o, w) in sl:
            emit_t(lambda e, o=o, w=w: e.matmul(pA[:, o:o + w], ones_row[:], rowC[0:1, o:o + w], start=True, stop=True))
        t1 = 0
        for gi, (o, w) in enumerate(sl):
            t1 = emit_t(lambda e, o=o, w=w: e.matmul(pB[:, o:o + w], ones_row[:], rowA[0:1, o:o + w], start=True, stop=True),
                        inc=(gi == len(sl) - 1))
        P.wait("vector", "tm", t1)
        emit_v(lambda e: e.tensor_copy(rbb[:, 0:width], pA[:, 0:width]))
        vb = emit_v(lambda e: e.tensor_copy(mrbb[:, 0:width], pB[:, 0:width]), inc=True)
        return vb

    emit_v(lambda e: e.memset(ones_bf[:], 1.0))
    emit_v(lambda e: e.memset(ones_row[:], 1.0))
    emit_v(lambda e: e.memset(ones128b[:], 1.0))
    emit_v(lambda e: e.memset(konst[0:1, 0:1], 1.0 / D), inc=True)

    ve_cast = [0] * NCH
    tm_st1 = [0] * NCH
    ldx = [0] * NCH
    for c in range(NCH):
        p = c % 2
        if c >= 2:
            P.wait("sync", "ve", ve_cast[c - 2])
        ldx[c] = P.dma("sync", f"ldx{p}", lambda e, c=c, p=p: e.dma_start(
            out=xf_t[:, p * Tl:(p + 1) * Tl], in_=xT[c * 128:(c + 1) * 128, :]))
        P.wait("vector", f"ldx{p}", ldx[c])
        ve_cast[c] = emit_v(lambda e, c=c, p=p: e.tensor_copy(
            xb_all[:, c * Tl:(c + 1) * Tl], xf_t[:, p * Tl:(p + 1) * Tl]), inc=True)
        P.wait("scalar", "ve", ve_cast[c])
        if c >= 1:
            P.wait("scalar", "tm", tm_st1[c - 1])
        ac1 = emit_s(lambda e, c=c: e.activation(
            out=x2b_t[:, 0:Tl], in_=xb_all[:, c * Tl:(c + 1) * Tl], func=AF.Square), inc=True)
        P.wait("tensor", "ac", ac1)
        for (o, w) in tl_sl:
            emit_t(lambda e, c=c, o=o, w=w: e.matmul(
                pA[0:1, o:o + w], ones_bf[:], xb_all[:, c * Tl + o:c * Tl + o + w],
                start=(c == 0), stop=(c == NCH - 1)))
        for gi, (o, w) in enumerate(tl_sl):
            tm_st1[c] = emit_t(lambda e, c=c, o=o, w=w: e.matmul(
                pB[0:1, o:o + w], ones_bf[:], x2b_t[:, o:o + w],
                start=(c == 0), stop=(c == NCH - 1)), inc=(gi == len(tl_sl) - 1))

    ve_mr1 = ln_rows2(Tl, "tm", tm_st1[NCH - 1], True)
    ve_b1 = ln_bcast(Tl, ve_mr1)
    P.wait("vector", "ve", ve_b1)
    ve_n1 = 0
    for c in range(NCH):
        emit_v(lambda e, c=c: e.tensor_tensor(
            y1_t[:, 0:Tl], xb_all[:, c * Tl:(c + 1) * Tl], rbb[:, 0:Tl], ALU.mult))
        ve_n1 = emit_v(lambda e, c=c: e.tensor_tensor(
            xb_all[:, c * Tl:(c + 1) * Tl], y1_t[:, 0:Tl], mrbb[:, 0:Tl], ALU.subtract), inc=True)

    wl_w = {}

    def load_weight(tag, src_ap, gate_fn):
        dstf = wbuf[:].rearrange("p (dc e) -> p dc e", dc=NCH)
        for g in range(4):
            P.wait("gpsimd", "tm", gate_fn(g))
            src = src_ap[:, g * 512:(g + 1) * 512].rearrange("(dc p) e -> p dc e", p=128)
            dst = dstf[:, :, g * 512:(g + 1) * 512]
            wl_w[(tag, g)] = P.dma("gpsimd", f"wl{g}", lambda e, src=src, dst=dst: e.dma_start(out=dst, in_=src))

    tm_proj = {}
    ve_evac = {}
    sd_store = {}

    def emit_proj_fm(tag, dst_rows_fn, prev_tag=None):
        for ech in range(NCH):
            reg = pA if ech % 2 == 0 else pB
            p = ech % 2
            if ech % 4 == 0:
                P.wait("tensor", f"wl{ech // 4}", wl_w[(tag, ech // 4)])
            if ech == 0:
                P.wait("tensor", "ve", ve_n1)
            if ech < 2 and prev_tag is not None:
                P.wait("tensor", "ve", ve_evac[(prev_tag, NCH - 2 + ech)])
            if ech >= 2:
                P.wait("tensor", "ve", ve_evac[(tag, ech - 2)])
            tmv = 0
            for dc in range(NCH):
                for gi, (o, w) in enumerate(tl_sl):
                    tmv = emit_t(lambda e, ech=ech, dc=dc, o=o, w=w, reg=reg: e.matmul(
                        reg[:, o:o + w],
                        wbuf[:, dc * D + ech * 128:dc * D + ech * 128 + 128],
                        xb_all[:, dc * Tl + o:dc * Tl + o + w],
                        start=(dc == 0), stop=(dc == NCH - 1)),
                        inc=(dc == NCH - 1 and gi == len(tl_sl) - 1)) or tmv
            tm_proj[(tag, ech)] = tmv
            P.wait("vector", "tm", tmv)
            if ech >= 2:
                P.wait("vector", f"sde{p}", sd_store[(tag, ech - 2)])
            elif prev_tag is not None:
                P.wait("vector", f"sde{p}", sd_store[(prev_tag, NCH - 2 + ech)])
            ve_evac[(tag, ech)] = emit_v(lambda e, p=p, reg=reg: e.tensor_copy(
                ev_t[:, p * D:p * D + Tl], reg[:, 0:Tl]), inc=True)
            P.wait("sync", "ve", ve_evac[(tag, ech)])
            sd_store[(tag, ech)] = P.dma("sync", f"sde{p}", lambda e, ech=ech, p=p, fn=dst_rows_fn: e.dma_start(
                out=fn(ech), in_=ev_t[:, p * D:p * D + Tl]))

    rg = [[0, 1, 2, 3], [4, 5, 6, 7]]
    load_weight("k", wkT, lambda g: 0)
    emit_proj_fm("k", lambda ech: kloc.ap()[ech * 128:(ech + 1) * 128, :])
    ccK = [0] * 4
    for g in range(4):
        P.wait("gpsimd", f"sde{(4 * g + 2) % 2}", sd_store[("k", 4 * g + 2)])
        P.wait("gpsimd", f"sde{(4 * g + 3) % 2}", sd_store[("k", 4 * g + 3)])
        P.emit("gpsimd", lambda e, g=g: e.collective_compute(
            "AllGather", ALU.bypass, replica_groups=rg,
            ins=[kloc.ap()[g * 512:(g + 1) * 512, :].opt()],
            outs=[kgat_g[g].ap().opt()]).then_inc(sems["cc"], 1))
        ccK[g] = P.inc("cc", eng="gpsimd")
    load_weight("v", wvT, lambda g: tm_proj[("k", 4 * g + 3)])

    for tt in range(TCH):
        reg = pA if tt % 2 == 0 else pB
        p = tt % 2
        if tt == 0:
            for g4 in range(4):
                P.wait("tensor", f"wl{g4}", wl_w[("v", g4)])
        if tt < 2:
            P.wait("tensor", "ve", ve_evac[("k", NCH - 2 + tt)])
        if tt >= 2:
            P.wait("tensor", "ve", ve_evac[("v", tt - 2)])
        tmv = 0
        for dc in range(NCH):
            for gi, (o, w) in enumerate(d_sl):
                tmv = emit_t(lambda e, tt=tt, dc=dc, o=o, w=w, reg=reg: e.matmul(
                    reg[:, o:o + w],
                    xb_all[:, dc * Tl + tt * 128:dc * Tl + tt * 128 + 128],
                    wbuf[:, dc * D + o:dc * D + o + w],
                    start=(dc == 0), stop=(dc == NCH - 1)),
                    inc=(dc == NCH - 1 and gi == len(d_sl) - 1)) or tmv
        tm_proj[("v", tt)] = tmv
        P.wait("vector", "tm", tmv)
        if tt >= 2:
            P.wait("vector", f"sde{p}", sd_store[("v", tt - 2)])
        else:
            P.wait("vector", f"sde{p}", sd_store[("k", NCH - 2 + tt)])
        ve_evac[("v", tt)] = emit_v(lambda e, p=p, reg=reg: e.tensor_copy(
            ev_t[:, p * D:(p + 1) * D], reg[:, 0:D]), inc=True)
        P.wait("sync", "ve", ve_evac[("v", tt)])
        sd_store[("v", tt)] = P.dma("sync", f"sde{p}", lambda e, tt=tt, p=p: e.dma_start(
            out=vloc.ap()[tt * 128:(tt + 1) * 128, :], in_=ev_t[:, p * D:(p + 1) * D]))

    P.wait("gpsimd", "sde0", P.ctr.get("sde0", 0))
    P.wait("gpsimd", "sde1", P.ctr.get("sde1", 0))
    P.emit("gpsimd", lambda e: e.collective_compute(
        "AllGather", ALU.bypass, replica_groups=rg,
        ins=[vloc.ap().opt()], outs=[vgat.ap().opt()]).then_inc(sems["cc"], 1))
    cc_v = P.inc("cc", eng="gpsimd")
    load_weight("q", wqT, lambda g: tm_proj[("v", TCH - 1)])
    emit_proj_fm("q", lambda ech: qdram[ech * 128:(ech + 1) * 128, :], prev_tag="v")
    sde_tot = [P.ctr.get("sde0", 0), P.ctr.get("sde1", 0)]
    load_weight("o", woT, lambda g: tm_proj[("q", 4 * g + 3)])

    kgat_r = [t.ap().rearrange("(r d) t -> d r t", r=R) for t in kgat_g]
    vgat_ap = vgat.ap()
    NVQ = 4
    JQ = KCH // NVQ

    def pool_own(jj):
        return jj % 3 == 0 and jj < 63

    ld_k = [0] * HP
    ld_v = [[0] * NVQ for _ in range(HP)]
    ld_qh = [0] * H
    tm_sc_g = {}
    tm_pv_g = {}
    ac_g = {}
    po_g = {}
    ve_add_g = {}
    ve_acs = {}
    tm_col = {}
    ve_rc = {}
    ve_ot = {}
    sd_od = {}

    def emit_kload(hp):
        slot = hp % 2
        for hh in range(2):
            h = hp * 2 + hh
            P.wait("sync", "cc", ccK[h // 4])
            kdst = kp_buf[:, (slot * 2 + hh) * S:(slot * 2 + hh + 1) * S].rearrange(
                "p (r t) -> p r t", r=R)
            hd = (h % 4) * 128
            ld_k[hp] = P.dma("sync", f"lk{slot}", lambda e, h=h, hd=hd, kdst=kdst: e.dma_start(
                out=kdst, in_=kgat_r[h // 4][hd:hd + 128, :, :]))

    def emit_vload(hp, eng):
        for vq in range(NVQ):
            vsrc = vgat_ap[vq * (S // NVQ):(vq + 1) * (S // NVQ),
                           hp * 256:(hp + 1) * 256].rearrange("(j p) u -> p j u", p=128)
            vdst = vp_buf[:, vq * JQ * 256:(vq + 1) * JQ * 256].rearrange(
                "p (j u) -> p j u", j=JQ)
            ld_v[hp][vq] = P.dma(eng, f"lv{vq}", lambda e, vsrc=vsrc, vdst=vdst: e.dma_start(
                out=vdst, in_=vsrc))

    tm_v_done = tm_proj[("q", NCH - 1)]
    P.wait("sync", "tm", tm_v_done)
    P.wait("sync", "cc", cc_v)
    P.wait("sync", "sde0", sde_tot[0])
    P.wait("sync", "sde1", sde_tot[1])
    emit_kload(0)
    emit_vload(0, "sync")
    if HP > 1:
        emit_kload(1)

    it = 0
    for hp in range(HP):
        slot = hp % 2
        for hh in range(2):
            h = hp * 2 + hh
            hq = h % 2
            if h >= 2:
                P.wait("sync", "tm",
                       tm_sc_g[((h - 2) * NQ + NQ - 1) * KCH + KCH - 1])
            ld_qh[h] = P.dma("sync", f"lq{hq}", lambda e, h=h, hq=hq: e.dma_start(
                out=qh_buf[:, hq * Tl:(hq + 1) * Tl], in_=qdram[h * 128:(h + 1) * 128, :]))

            for qh in range(NQ):
                qb = qh * QW
                bg = it * KCH
                s_pv = (it % 2) * QW
                it_loc = it

                def emit_scores_exp_pair(mm, bg=None, qb=None, hh=None, hq=None, slot=None, hp=None, h=None):
                    for jj in (2 * mm, 2 * mm + 1):
                        g = bg + jj
                        if g >= 2:
                            P.wait("tensor", "ac", ac_g[g - 2])
                        if jj == 0:
                            P.wait("tensor", f"lk{slot}", ld_k[hp])
                            P.wait("tensor", f"lq{hq}", ld_qh[h])
                        ps = (g % 2) * QW
                        for gi, (o, w) in enumerate(q_sl):
                            r = emit_t(lambda e, hh=hh, jj=jj, o=o, w=w, qb=qb, hq=hq,
                                       slot=slot, ps=ps: e.matmul(
                                pA[:, ps + o:ps + o + w],
                                kp_buf[:, (slot * 2 + hh) * S + jj * 128:(slot * 2 + hh) * S + jj * 128 + 128],
                                qh_buf[:, hq * Tl + qb + o:hq * Tl + qb + o + w],
                                start=True, stop=True),
                                inc=(gi == len(q_sl) - 1))
                            if r:
                                tm_sc_g[g] = r
                    for jj in (2 * mm, 2 * mm + 1):
                        g = bg + jj
                        P.wait("scalar", "tm", tm_sc_g[g])
                        if jj % 2 == 0 and g >= 8:
                            P.wait("scalar", "tm", tm_pv_g[g - 7])
                            pov = [po_g[c] for c in (g - 8, g - 7) if pool_own(c % KCH)]
                            vev = [ve_add_g[c] for c in (g - 8, g - 7) if not pool_own(c % KCH)]
                            if pov:
                                P.wait("scalar", "po", max(pov))
                            if vev:
                                P.wait("scalar", "ve", max(vev))
                        ac_g[g] = emit_s(lambda e, g=g: e.activation(
                            out=pt_buf[:, (g % 8) * QW:(g % 8) * QW + QW],
                            in_=pA[:, (g % 2) * QW:(g % 2) * QW + QW],
                            func=AF.Exp, bias=f32(cfg.SHIFT), scale=f32(cfg.SCALE)), inc=True)

                def emit_pv_acc_pair(mm, bg=None, hh=None, s_pv=None, hp=None, it_loc=None):
                    for jj in (2 * mm, 2 * mm + 1):
                        g = bg + jj
                        P.wait("tensor", "ac", ac_g[g])
                        if jj % JQ == 0:
                            P.wait("tensor", f"lv{jj // JQ}", ld_v[hp][jj // JQ])
                        if jj == 0 and it_loc >= 1:
                            P.wait("tensor", "ve", ve_rc[it_loc - 1])
                        tpv = 0
                        for gi, (o, w) in enumerate(q_sl):
                            tpv = emit_t(lambda e, jj=jj, hh=hh, o=o, w=w, s_pv=s_pv,
                                         pq=(g % 8) * QW: e.matmul(
                                pB[:, s_pv + o:s_pv + o + w],
                                vp_buf[:, jj * 256 + hh * 128:jj * 256 + hh * 128 + 128],
                                pt_buf[:, pq + o:pq + o + w],
                                start=(jj == 0), stop=(jj == KCH - 1)),
                                inc=(gi == len(q_sl) - 1)) or tpv
                        tm_pv_g[g] = tpv
                    for jj in (2 * mm, 2 * mm + 1):
                        g = bg + jj
                        if pool_own(jj):
                            P.wait("gpsimd", "ac", ac_g[g])
                            if jj == 0 and it_loc >= 1:
                                P.wait("gpsimd", "ve", ve_acs[it_loc - 1])
                            if jj == 0:
                                po_g[g] = emit_p(lambda e, g=g: e.tensor_copy(
                                    accP[:, 0:QW], pt_buf[:, (g % 8) * QW:(g % 8) * QW + QW]),
                                    inc=True)
                            else:
                                po_g[g] = emit_p(lambda e, g=g: e.tensor_tensor(
                                    accP[:, 0:QW], accP[:, 0:QW],
                                    pt_buf[:, (g % 8) * QW:(g % 8) * QW + QW], ALU.add),
                                    inc=True)
                        else:
                            P.wait("vector", "ac", ac_g[g])
                            if jj == 1:
                                ve_add_g[g] = emit_v(lambda e, g=g: e.tensor_copy(
                                    accD[:, 0:QW], pt_buf[:, (g % 8) * QW:(g % 8) * QW + QW]),
                                    inc=True)
                            else:
                                ve_add_g[g] = emit_v(lambda e, g=g: e.tensor_tensor(
                                    accD[:, 0:QW], accD[:, 0:QW],
                                    pt_buf[:, (g % 8) * QW:(g % 8) * QW + QW], ALU.add),
                                    inc=True)

                def emit_tail(pit):
                    pc = ((pit + 1) % 2) * QW
                    pv = (pit % 2) * QW
                    P.wait("tensor", "ve", ve_acs[pit])
                    if pit >= 1:
                        P.wait("tensor", "ve", ve_ot[pit - 1])
                    tcol = 0
                    for gi, (o, w) in enumerate(q_sl):
                        tcol = emit_t(lambda e, o=o, w=w, pc=pc: e.matmul(
                            pB[:, pc + o:pc + o + w], ones128b[:], accB[:, o:o + w],
                            start=True, stop=True), inc=(gi == len(q_sl) - 1)) or tcol
                    tm_col[pit] = tcol
                    P.wait("vector", "tm", tm_col[pit])
                    ve_rc[pit] = emit_v(lambda e, pc=pc: e.reciprocal(
                        linv[:, 0:QW], pB[:, pc:pc + QW]), inc=True)
                    osel = pit % 2
                    P.wait("vector", "tm", tm_pv_g[pit * KCH + KCH - 1])
                    if pit >= 2:
                        P.wait("vector", f"sod{osel}", sd_od[pit - 2])
                    ve_ot[pit] = emit_v(lambda e, pv=pv, osel=osel: e.tensor_tensor(
                        ot_buf[:, osel * QW:(osel + 1) * QW], pB[:, pv:pv + QW],
                        linv[:, 0:QW], ALU.mult), inc=True)
                    P.wait("sync", "ve", ve_ot[pit])
                    ph = pit % (NQ * H) // NQ
                    pqb = (pit % NQ) * QW
                    sd_od[pit] = P.dma("sync", f"sod{osel}", lambda e, ph=ph, pqb=pqb, osel=osel: e.dma_start(
                        out=odram[ph * 128:(ph + 1) * 128, pqb:pqb + QW],
                        in_=ot_buf[:, osel * QW:(osel + 1) * QW]))

                kw_se = dict(bg=bg, qb=qb, hh=hh, hq=hq, slot=slot, hp=hp, h=h)
                kw_pv = dict(bg=bg, hh=hh, s_pv=s_pv, hp=hp, it_loc=it_loc)
                for m in range(KCH // 2):
                    emit_scores_exp_pair(m, **kw_se)
                    if m == 1 and it_loc >= 1:
                        emit_tail(it_loc - 1)
                    if m >= 1:
                        emit_pv_acc_pair(m - 1, **kw_pv)
                emit_pv_acc_pair(KCH // 2 - 1, **kw_pv)
                P.wait("vector", "po", po_g[bg + 60])
                ve_acs[it_loc] = emit_v(lambda e: e.tensor_tensor(
                    accB[:, 0:QW], accD[:, 0:QW], accP[:, 0:QW], ALU.add), inc=True)
                if it_loc == H * NQ - 1:
                    emit_tail(it_loc)
                it += 1

        if hp + 1 < HP:
            P.wait("gpsimd", "tm", tm_pv_g[it * KCH - 1])
            emit_vload(hp + 1, "gpsimd")
        if hp + 2 < HP:
            P.wait("sync", "tm", tm_sc_g[it * KCH - 1])
            emit_kload(hp + 2)

    sod_tot = [P.ctr.get("sod0", 0), P.ctr.get("sod1", 0)]

    P.wait("sync", "sod0", sod_tot[0])
    P.wait("sync", "sod1", sod_tot[1])
    ld_oall = 0
    for dc in range(NCH):
        ld_oall = P.dma("sync", "lo", lambda e, dc=dc: e.dma_start(
            out=oall[:, dc * Tl:(dc + 1) * Tl], in_=odram[dc * 128:(dc + 1) * 128, :]))

    emit_v(lambda e: e.memset(rowA[0:1, 0:Tl], 0.0))
    ve_z5 = emit_v(lambda e: e.memset(rowC[0:1, 0:Tl], 0.0), inc=True)

    ve_ya = [0] * NCH
    sd_ya = [0] * NCH
    ac_sq5 = [0] * NCH
    ve_srC = 0
    tm_sum5 = [0] * NCH
    tm_sq5v = [0] * NCH
    tm_op = 0

    def emit_stats5(k):
        nonlocal ve_srC
        p = k % 2
        P.wait("tensor", "ac", ac_sq5[k])
        P.wait("tensor", "ve", ve_z5 if k == 0 else ve_srC)
        t1 = 0
        for gi, (o, w) in enumerate(tl_sl):
            t1 = emit_t(lambda e, p=p, o=o, w=w: e.matmul(
                pB[0:1, o:o + w], ones_bf[:], yab_t[:, p * Tl + o:p * Tl + o + w],
                start=True, stop=True), inc=(gi == len(tl_sl) - 1)) or t1
        tm_sum5[k] = t1
        P.wait("vector", "tm", t1)
        ve_srA = emit_v(lambda e: e.tensor_tensor(
            rowA[0:1, 0:Tl], rowA[0:1, 0:Tl], pB[0:1, 0:Tl], ALU.add), inc=True)
        P.wait("tensor", "ve", ve_srA)
        t2 = 0
        for gi, (o, w) in enumerate(tl_sl):
            t2 = emit_t(lambda e, p=p, o=o, w=w: e.matmul(
                pB[0:1, o:o + w], ones_bf[:], x2b_t[:, p * Tl + o:p * Tl + o + w],
                start=True, stop=True), inc=(gi == len(tl_sl) - 1)) or t2
        tm_sq5v[k] = t2
        P.wait("vector", "tm", t2)
        ve_srC = emit_v(lambda e: e.tensor_tensor(
            rowC[0:1, 0:Tl], rowC[0:1, 0:Tl], pB[0:1, 0:Tl], ALU.add), inc=True)

    for ech in range(NCH):
        p = ech % 2
        if ech % 4 == 0:
            P.wait("tensor", f"wl{ech // 4}", wl_w[("o", ech // 4)])
        if ech == 0:
            P.wait("tensor", "lo", ld_oall)
        else:
            P.wait("tensor", "ve", ve_ya[ech - 1])
        for dc in range(NCH):
            for gi, (o, w) in enumerate(tl_sl):
                tm_op = emit_t(lambda e, ech=ech, dc=dc, o=o, w=w: e.matmul(
                    pA[:, o:o + w],
                    wbuf[:, dc * D + ech * 128:dc * D + ech * 128 + 128],
                    oall[:, dc * Tl + o:dc * Tl + o + w],
                    start=(dc == 0), stop=(dc == NCH - 1)),
                    inc=(dc == NCH - 1 and gi == len(tl_sl) - 1)) or tm_op
        if ech >= 1:
            emit_stats5(ech - 1)
        P.wait("vector", "tm", tm_op)
        if ech >= 2:
            P.wait("vector", f"sya{p}", sd_ya[ech - 2])
            P.wait("vector", "ac", ac_sq5[ech - 2])
        ve_ya[ech] = emit_v(lambda e, p=p: e.tensor_copy(
            xf_t[:, p * Tl:(p + 1) * Tl], pA[:, 0:Tl]), inc=True)
        P.wait("sync", "ve", ve_ya[ech])
        sd_ya[ech] = P.dma("sync", f"sya{p}", lambda e, ech=ech, p=p: e.dma_start(
            out=yadram[ech * 128:(ech + 1) * 128, :], in_=xf_t[:, p * Tl:(p + 1) * Tl]))
        if ech >= 2:
            P.wait("vector", "tm", tm_sum5[ech - 2])
        ve_yab = emit_v(lambda e, p=p: e.tensor_copy(
            yab_t[:, p * Tl:(p + 1) * Tl], xf_t[:, p * Tl:(p + 1) * Tl]), inc=True)
        P.wait("scalar", "ve", ve_yab)
        if ech >= 2:
            P.wait("scalar", "tm", tm_sq5v[ech - 2])
        ac_sq5[ech] = emit_s(lambda e, p=p: e.activation(
            out=x2b_t[:, p * Tl:(p + 1) * Tl], in_=xf_t[:, p * Tl:(p + 1) * Tl],
            func=AF.Square), inc=True)
    emit_stats5(NCH - 1)

    ve_mr2 = ln_rows2(Tl, "ve", ve_srC, False)
    ve_b2 = ln_bcast(Tl, ve_mr2)

    sd_x1 = [0] * NCH
    ve_x1b = [0] * NCH
    ac_sq3 = [0] * NCH
    tm_sq3 = 0
    for c in range(NCH):
        p = c % 2
        if c < 2:
            P.wait("sync", "ac", ac_sq5[NCH - 1])
        else:
            P.wait("sync", "ve", ve_x1b[c - 2])
        P.wait("sync", f"sya{p}", sd_ya[NCH - 2 + p] if NCH >= 2 else sd_ya[0])
        P.dma("sync", f"lya{p}", lambda e, c=c, p=p: e.dma_start(
            out=xf_t[:, p * Tl:(p + 1) * Tl], in_=yadram[c * 128:(c + 1) * 128, :]))
        ld_ya = P.ctr[f"lya{p}"]
        if c >= 1:
            P.wait("sync", "sx1", sd_x1[c - 1])
            P.wait("sync", "ac", ac_sq3[c - 1])
        ld_x = P.dma("sync", "lx1", lambda e, c=c: e.dma_start(
            out=x1_t[:, 0:Tl], in_=xT[c * 128:(c + 1) * 128, :]))
        P.wait("vector", f"lya{p}", ld_ya)
        P.wait("vector", "lx1", ld_x)
        if c == 0:
            P.wait("vector", "ve", ve_b2)
        emit_v(lambda e, p=p: e.tensor_tensor(
            xf_t[:, p * Tl:(p + 1) * Tl], xf_t[:, p * Tl:(p + 1) * Tl], rbb[:, 0:Tl], ALU.mult))
        emit_v(lambda e, p=p: e.tensor_tensor(
            xf_t[:, p * Tl:(p + 1) * Tl], xf_t[:, p * Tl:(p + 1) * Tl], mrbb[:, 0:Tl], ALU.subtract))
        emit_v(lambda e, p=p: e.tensor_tensor(
            x1_t[:, 0:Tl], x1_t[:, 0:Tl], xf_t[:, p * Tl:(p + 1) * Tl], ALU.add))
        ve_x1b[c] = emit_v(lambda e, c=c: e.tensor_copy(
            xb_all[:, c * Tl:(c + 1) * Tl], x1_t[:, 0:Tl]), inc=True)
        P.wait("sync", "ve", ve_x1b[c])
        sd_x1[c] = P.dma("sync", "sx1", lambda e, c=c: e.dma_start(
            out=x1dram[c * 128:(c + 1) * 128, :], in_=x1_t[:, 0:Tl]))
        P.wait("scalar", "ve", ve_x1b[c])
        if c >= 1:
            P.wait("scalar", "tm", tm_sq3)
        ac_sq3[c] = emit_s(lambda e, c=c: e.activation(
            out=x2b_t[:, 0:Tl], in_=x1_t[:, 0:Tl], func=AF.Square), inc=True)
        P.wait("tensor", "ac", ac_sq3[c])
        if c == 0:
            P.wait("tensor", "ve", ve_b2)
        for (o, w) in tl_sl:
            emit_t(lambda e, c=c, o=o, w=w: e.matmul(
                pA[0:1, o:o + w], ones_bf[:], xb_all[:, c * Tl + o:c * Tl + o + w],
                start=(c == 0), stop=(c == NCH - 1)))
        for gi, (o, w) in enumerate(tl_sl):
            tm_sq3 = emit_t(lambda e, c=c, o=o, w=w: e.matmul(
                pB[0:1, o:o + w], ones_bf[:], x2b_t[:, o:o + w],
                start=(c == 0), stop=(c == NCH - 1)), inc=(gi == len(tl_sl) - 1)) or tm_sq3

    ve_mr3 = ln_rows2(Tl, "tm", tm_sq3, True)
    ve_b3 = ln_bcast(Tl, ve_mr3)
    P.wait("vector", "ve", ve_b3)
    ve_n3 = 0
    for c in range(NCH):
        emit_v(lambda e, c=c: e.tensor_tensor(
            yab_t[:, 0:Tl], xb_all[:, c * Tl:(c + 1) * Tl], rbb[:, 0:Tl], ALU.mult))
        ve_n3 = emit_v(lambda e, c=c: e.tensor_tensor(
            xb_all[:, c * Tl:(c + 1) * Tl], yab_t[:, 0:Tl], mrbb[:, 0:Tl], ALU.subtract), inc=True)

    n_out = 0
    so_out = {}
    tm_gu_done = {}
    wl_gsl = {}
    ve_nrm_last = 0
    ve_dn_prev = 0
    ve_b4 = 0
    sx1_tot = P.ctr.get("sx1", 0)
    for ee in range(E):
        ecol = ee * TE
        CPS = GW // 128
        ngroups = NSLAB // 2

        def emit_gu_piece(ee, slot, j):
            if ee == 0 and j < CPS:
                P.wait("gpsimd", "tm", tm_op)
            elif j < CPS:
                P.wait("gpsimd", "tm", tm_gu_done[
                    (ee - 1, (ICH - CPS + j) + (0 if slot == 0 else ICH))])
            else:
                P.wait("gpsimd", "tm", tm_gu_done[
                    (ee, (j - CPS) + (0 if slot == 0 else ICH))])
            col = j * 128 + (0 if slot == 0 else I)
            srcw = wgu[ee, :, col:col + 128].rearrange("(dc p) i -> p dc i", p=128)
            sl = gslab[:, slot * NCH * GW:(slot + 1) * NCH * GW].rearrange(
                "p (dc i) -> p dc i", dc=NCH)
            gdst = sl[:, :, (j % CPS) * 128:(j % CPS) * 128 + 128]
            wl_gsl[(ee, slot, j)] = P.dma("gpsimd", f"wg{slot}{j % CPS}",
                                          lambda e, srcw=srcw, gdst=gdst: e.dma_start(out=gdst, in_=srcw))

        for j0 in range(CPS):
            emit_gu_piece(ee, 0, j0)
            emit_gu_piece(ee, 1, j0)

        ac_sg = [0] * ICH
        ve_in = [0] * ICH
        for ihalf in range(ICH):
            for which in ("g", "u"):
                icg = ihalf if which == "g" else ihalf + ICH
                offw = (ihalf % CPS) * 128
                slot = 0 if which == "g" else 1
                base = (ihalf % 2) * 1024 + (0 if which == "g" else 512)
                P.wait("tensor", f"wg{slot}{ihalf % CPS}", wl_gsl[(ee, slot, ihalf)])
                if ee == 0 and ihalf == 0 and which == "g":
                    P.wait("tensor", "ve", ve_n3)
                if ihalf == 0 and which == "g" and ee > 0:
                    P.wait("tensor", "ve", ve_b4)
                if ihalf >= 2:
                    if which == "g":
                        P.wait("tensor", "ac", ac_sg[ihalf - 2])
                    else:
                        P.wait("tensor", "ve", ve_in[ihalf - 2])
                tmg = 0
                for dc in range(NCH):
                    for gi, (o, w) in enumerate(te_sl):
                        tmg = emit_t(lambda e, offw=offw, dc=dc, o=o, w=w,
                                     base=base, slot=slot, ecol=ecol: e.matmul(
                            pA[:, base + o:base + o + w],
                            gslab[:, slot * NCH * GW + dc * GW + offw:
                                  slot * NCH * GW + dc * GW + offw + 128],
                            xb_all[:, dc * Tl + ecol + o:dc * Tl + ecol + o + w],
                            start=(dc == 0), stop=(dc == NCH - 1)),
                            inc=(dc == NCH - 1 and gi == len(te_sl) - 1)) or tmg
                tm_gu_done[(ee, icg)] = tmg
            P.wait("scalar", "tm", tm_gu_done[(ee, ihalf)])
            if ihalf >= 2:
                P.wait("scalar", "ve", ve_in[ihalf - 2])
            ac_sg[ihalf] = emit_s(lambda e, ihalf=ihalf: e.activation(
                out=sg_t[:, (ihalf % 2) * TE:(ihalf % 2 + 1) * TE],
                in_=pA[:, (ihalf % 2) * 1024:(ihalf % 2) * 1024 + TE], func=AF.Silu), inc=True)
            P.wait("vector", "ac", ac_sg[ihalf])
            P.wait("vector", "tm", tm_gu_done[(ee, ihalf + ICH)])
            ve_in[ihalf] = emit_v(lambda e, ihalf=ihalf: e.tensor_tensor(
                inter[:, ihalf * TE:(ihalf + 1) * TE],
                pA[:, (ihalf % 2) * 1024 + 512:(ihalf % 2) * 1024 + 512 + TE],
                sg_t[:, (ihalf % 2) * TE:(ihalf % 2 + 1) * TE], ALU.mult), inc=True)
            if ihalf + CPS < ICH:
                emit_gu_piece(ee, 0, ihalf + CPS)
                emit_gu_piece(ee, 1, ihalf + CPS)

        ve_dn = [0] * NCH
        ac_c6 = [0] * NCH
        ve_c6 = [0] * NCH
        tm_sum6 = [0] * NCH
        tm_sq6 = [0] * NCH
        tm_st6 = 0

        def emit_stats6(k):
            nonlocal tm_st6
            pk = k % 2
            P.wait("tensor", "ac", ac_c6[k])
            if k == 0 and ee == 0:
                P.wait("tensor", "ve", ve_b3)
            if k == 0 and ee > 0:
                P.wait("tensor", "ve", ve_b4)
            t1 = 0
            for gi, (o, w) in enumerate(te_sl):
                t1 = emit_t(lambda e, k=k, pk=pk, o=o, w=w: e.matmul(
                    pA[0:1, o:o + w], ones_bf[:], cp6_t[:, pk * TE + o:pk * TE + o + w],
                    start=(k == 0), stop=(k == NCH - 1)), inc=(gi == len(te_sl) - 1)) or t1
            tm_sum6[k] = t1
            t2 = 0
            for gi, (o, w) in enumerate(te_sl):
                t2 = emit_t(lambda e, k=k, pk=pk, o=o, w=w: e.matmul(
                    pA[0:1, 512 + o:512 + o + w], ones_bf[:], sq6_t[:, pk * TE + o:pk * TE + o + w],
                    start=(k == 0), stop=(k == NCH - 1)), inc=(gi == len(te_sl) - 1)) or t2
            tm_sq6[k] = t2
            tm_st6 = t2

        for hc in range(NCH):
            hp2 = hc % 2
            src = wdn[ee, :, hc * 128:(hc + 1) * 128].rearrange("(ic p) hh -> p ic hh", p=128)
            if ee == 0 and hc < 2:
                P.wait("gpsimd", "tm", tm_op)
            elif hc >= 2:
                P.wait("gpsimd", "tm", tm_gu_done[("dn", ee, hc - 2)])
            else:
                P.wait("gpsimd", "tm", tm_gu_done[("dn", ee - 1, NCH - 1)])
            ddst = dslab[:, hp2 * ICH * 128:(hp2 + 1) * ICH * 128].rearrange(
                "p (ic hh) -> p ic hh", ic=ICH)
            wl_dn = P.dma("gpsimd", f"wd{hp2}", lambda e, src=src, ddst=ddst: e.dma_start(
                out=ddst, in_=src))
            P.wait("tensor", f"wd{hp2}", wl_dn)
            if hc == 0:
                P.wait("tensor", "ve", ve_in[ICH - 1])
                if ee > 0:
                    P.wait("tensor", "ve", ve_dn_prev)
            if hc >= 2:
                P.wait("tensor", "ve", ve_dn[hc - 2])
            tm_dn = 0
            for ic in range(ICH):
                for gi, (o, w) in enumerate(te_sl):
                    tm_dn = emit_t(lambda e, hp2=hp2, ic=ic, o=o, w=w: e.matmul(
                        pB[:, hp2 * 512 + o:hp2 * 512 + o + w],
                        dslab[:, hp2 * ICH * 128 + ic * 128:hp2 * ICH * 128 + ic * 128 + 128],
                        inter[:, ic * TE + o:ic * TE + o + w],
                        start=(ic == 0), stop=(ic == ICH - 1)),
                        inc=(ic == ICH - 1 and gi == len(te_sl) - 1)) or tm_dn
            tm_gu_done[("dn", ee, hc)] = tm_dn
            if hc >= 1:
                emit_stats6(hc - 1)
            P.wait("vector", "tm", tm_dn)
            if ee > 0 and hc == 0:
                P.wait("vector", "ve", ve_nrm_last)
            ve_dn[hc] = emit_v(lambda e, hc=hc, hp2=hp2: e.tensor_copy(
                yf_e[:, hc * TE:(hc + 1) * TE], pB[:, hp2 * 512:hp2 * 512 + TE]), inc=True)
            if hc >= 2:
                P.wait("vector", "tm", tm_sum6[hc - 2])
            ve_c6[hc] = emit_v(lambda e, hc=hc, hp2=hp2: e.tensor_copy(
                cp6_t[:, hp2 * TE:(hp2 + 1) * TE], yf_e[:, hc * TE:(hc + 1) * TE]), inc=True)
            P.wait("scalar", "ve", ve_c6[hc])
            if hc >= 2:
                P.wait("scalar", "tm", tm_sq6[hc - 2])
            ac_c6[hc] = emit_s(lambda e, hc=hc, hp2=hp2: e.activation(
                out=sq6_t[:, hp2 * TE:(hp2 + 1) * TE], in_=yf_e[:, hc * TE:(hc + 1) * TE],
                func=AF.Square), inc=True)
        emit_stats6(NCH - 1)
        ve_dn_prev = ve_dn[NCH - 1]

        P.wait("vector", "tm", tm_st6)
        emit_v(lambda e: e.tensor_scalar_mul(rowA[0:1, 0:TE], pA[0:1, 0:TE], konst[0:1, 0:1]))
        v1 = emit_v(lambda e: e.tensor_scalar_mul(rowC[0:1, 0:TE], pA[0:1, 512:512 + TE], konst[0:1, 0:1]), inc=True)
        P.wait("scalar", "ve", v1)
        a1 = emit_s(lambda e: e.activation(out=mrbb[0:1, 0:TE], in_=rowA[0:1, 0:TE], func=AF.Square), inc=True)
        P.wait("vector", "ac", a1)
        v2 = emit_v(lambda e: e.tensor_tensor(rowC[0:1, 0:TE], rowC[0:1, 0:TE], mrbb[0:1, 0:TE], ALU.subtract), inc=True)
        P.wait("scalar", "ve", v2)
        emit_s(lambda e: e.activation(out=rowC[0:1, 0:TE], in_=rowC[0:1, 0:TE], func=AF.Ln, bias=f32(cfg.EPS), scale=f32(1.0)))
        a2 = emit_s(lambda e: e.activation(out=rowC[0:1, 0:TE], in_=rowC[0:1, 0:TE], func=AF.Exp, bias=f32(0.0), scale=f32(-0.5)), inc=True)
        P.wait("vector", "ac", a2)
        v3 = emit_v(lambda e: e.tensor_tensor(rowA[0:1, 0:TE], rowA[0:1, 0:TE], rowC[0:1, 0:TE], ALU.mult), inc=True)
        P.wait("tensor", "ve", v3)
        for (o, w) in te_sl:
            emit_t(lambda e, o=o, w=w: e.matmul(
                pA[:, 1024 + o:1024 + o + w], ones_row[:], rowC[0:1, o:o + w], start=True, stop=True))
        tm_b4 = 0
        for gi, (o, w) in enumerate(te_sl):
            tm_b4 = emit_t(lambda e, o=o, w=w: e.matmul(
                pA[:, 1536 + o:1536 + o + w], ones_row[:], rowA[0:1, o:o + w],
                start=True, stop=True), inc=(gi == len(te_sl) - 1)) or tm_b4
        P.wait("vector", "tm", tm_b4)
        emit_v(lambda e: e.tensor_copy(rbb[:, 0:TE], pA[:, 1024:1024 + TE]))
        ve_b4 = emit_v(lambda e: e.tensor_copy(mrbb[:, 0:TE], pA[:, 1536:1536 + TE]), inc=True)

        ve_nrm = {}
        for c in range(NCH):
            slot = c % 4
            if ee == 0 and c == 0:
                P.wait("sync", "ac", ac_sq3[NCH - 1])
                P.wait("sync", "sx1", sx1_tot)
            if c >= 4:
                P.wait("sync", "ve", ve_nrm[c - 4])
            elif ee > 0:
                P.wait("sync", "ve", ve_nrm_last)
            ld_x1c = P.dma("sync", f"lx6{slot}", lambda e, c=c, ecol=ecol, slot=slot: e.dma_start(
                out=x1_t[:, slot * TE:(slot + 1) * TE],
                in_=x1dram[c * 128:(c + 1) * 128, ecol:ecol + TE]))
            P.wait("vector", f"lx6{slot}", ld_x1c)
            if c == 0:
                P.wait("vector", "ve", ve_b4)
            osel = n_out % 2
            if n_out >= 2:
                P.wait("vector", f"sout{osel}", so_out[n_out - 2])
            emit_v(lambda e, c=c: e.tensor_tensor(
                o1_t[:, 0:TE], yf_e[:, c * TE:(c + 1) * TE], rbb[:, 0:TE], ALU.mult))
            emit_v(lambda e: e.tensor_tensor(
                o1_t[:, 0:TE], o1_t[:, 0:TE], mrbb[:, 0:TE], ALU.subtract))
            ve_nrm[c] = emit_v(lambda e, slot=slot, osel=osel: e.tensor_tensor(
                out_t[:, osel * TE:(osel + 1) * TE],
                x1_t[:, slot * TE:(slot + 1) * TE], o1_t[:, 0:TE], ALU.add), inc=True)
            P.wait("sync", "ve", ve_nrm[c])
            so_out[n_out] = P.dma("sync", f"sout{osel}", lambda e, c=c, ecol=ecol, osel=osel: e.dma_start(
                out=outT[c * 128:(c + 1) * 128, ecol:ecol + TE],
                in_=out_t[:, osel * TE:(osel + 1) * TE]))
            n_out += 1
        ve_nrm_last = ve_nrm[NCH - 1]

    P.wait("sync", "sout0", P.ctr.get("sout0", 0))
    P.wait("sync", "sout1", P.ctr.get("sout1", 0))
    P.wait("gpsimd", "sout0", P.ctr.get("sout0", 0))
    P.wait("gpsimd", "sout1", P.ctr.get("sout1", 0))

    with nc.Block() as block:
        @block.sync
        def _(e):
            for fn in P.ops["sync"]:
                fn(e)

        @block.gpsimd
        def _(e):
            for fn in P.ops["gpsimd"]:
                fn(e)

        @block.tensor
        def _(e):
            for fn in P.ops["tensor"]:
                fn(e)

        @block.vector
        def _(e):
            for fn in P.ops["vector"]:
                fn(e)

        @block.scalar
        def _(e):
            for fn in P.ops["scalar"]:
                fn(e)

    return nc


def _prep_core_inputs(x, wq, wk, wv, wo, gup, dnp, cfg, core):
    b = core // cfg.R
    t = core % cfg.R
    perm = np.concatenate([np.arange(e, cfg.Tl, cfg.E) for e in range(cfg.E)])
    sl = x[b, t * cfg.Tl:(t + 1) * cfg.Tl, :][perm, :]
    return {
        "xT": np.ascontiguousarray(sl.T),
        "wqT": np.ascontiguousarray(wq.T),
        "wkT": np.ascontiguousarray(wk.T),
        "wvT": np.ascontiguousarray(wv.T),
        "woT": np.ascontiguousarray(wo.T),
        "wgu": np.ascontiguousarray(gup),
        "wdn": np.ascontiguousarray(dnp),
    }, perm


def kernel(L0_attn_x, L0_attn_w_q, L0_attn_w_k, L0_attn_w_v, L0_attn_w_o,
           L0_ffn_gate_up_proj, L0_ffn_down_proj):
    cfg = CFG()
    x = np.asarray(L0_attn_x, dtype=np.float32)
    wq = np.asarray(L0_attn_w_q, dtype=np.float32)
    wk = np.asarray(L0_attn_w_k, dtype=np.float32)
    wv = np.asarray(L0_attn_w_v, dtype=np.float32)
    wo = np.asarray(L0_attn_w_o, dtype=np.float32)
    gup = np.asarray(L0_ffn_gate_up_proj, dtype=np.float32)
    dnp = np.asarray(L0_ffn_down_proj, dtype=np.float32)

    in_maps = []
    perms = []
    for core in range(8):
        m, perm = _prep_core_inputs(x, wq, wk, wv, wo, gup, dnp, cfg, core)
        in_maps.append(m)
        perms.append(perm)

    nc = build_nc(cfg)
    res = run_bass_kernel_spmd(nc, in_maps, core_ids=list(range(8)))

    out = np.empty((2, 8192, 2048), dtype=np.float32)
    for core in range(8):
        b = core // cfg.R
        t = core % cfg.R
        shard = np.asarray(res.results[core]["outT"], dtype=np.float32).T
        sl = np.empty_like(shard)
        sl[perms[core], :] = shard
        out[b, t * cfg.Tl:(t + 1) * cfg.Tl, :] = sl
    return out


# version 24 (final)
# speedup vs baseline: 1.0169x; 1.0043x over previous
import numpy as np

import concourse.bass as bass
import concourse.mybir as mybir
from concourse.bass_utils import run_bass_kernel_spmd

F32 = mybir.dt.float32
BF16 = mybir.dt.bfloat16
AF = mybir.ActivationFunctionType
ALU = mybir.AluOpType
f32 = np.float32

ENGINES = ("sync", "gpsimd", "tensor", "vector", "scalar")
NW = 512


class CFG:
    def __init__(self, D=2048, Tl=2048, H=16, E=4, I=2048, R=4):
        self.D, self.Tl, self.H, self.E, self.I, self.R = D, Tl, H, E, I, R
        self.HD = 128
        assert H * self.HD == D
        self.S = R * Tl
        self.NCH = D // 128
        self.TCH = Tl // 128
        self.KCH = self.S // 128
        self.QW = min(1024, Tl)
        self.NQ = Tl // self.QW
        self.TE = Tl // E
        self.ICH = I // 128
        self.GW = min(512, I)
        self.NSLAB = (2 * I) // self.GW
        self.EPS = 1e-5
        self.SCALE = 1.0 / float(np.sqrt(self.HD))
        self.SHIFT = -5.0


class Plan:
    def __init__(self, sems):
        self.ops = {e: [] for e in ENGINES}
        self.ctr = {k: 0 for k in sems}
        self.sems = sems

    def emit(self, eng, fn):
        self.ops[eng].append(fn)

    def inc(self, sem, n=1):
        self.ctr[sem] += n
        return self.ctr[sem]

    def wait(self, eng, sem, v):
        if v > 0:
            s = self.sems[sem]
            self.ops[eng].append(lambda e, s=s, v=v: e.wait_ge(s, v))

    def drain(self, eng):
        self.ops[eng].append(lambda e: e.drain())


def _sl(width, nw=NW):
    out, o = [], 0
    while o < width:
        w = min(nw, width - o)
        out.append((o, w))
        o += w
    return out


def build_nc(cfg: CFG):
    nc = bass.Bass("TRN2", target_bir_lowering=False)
    D, Tl, H, E, I, R = cfg.D, cfg.Tl, cfg.H, cfg.E, cfg.I, cfg.R
    NCH, TCH, KCH, QW, NQ, TE, ICH = (cfg.NCH, cfg.TCH, cfg.KCH, cfg.QW,
                                      cfg.NQ, cfg.TE, cfg.ICH)
    S, GW, NSLAB = cfg.S, cfg.GW, cfg.NSLAB
    HP = H // 2

    xT = nc.dram_tensor("xT", [D, Tl], F32, kind="ExternalInput").ap()
    wqT = nc.dram_tensor("wqT", [D, D], F32, kind="ExternalInput").ap()
    wkT = nc.dram_tensor("wkT", [D, D], F32, kind="ExternalInput").ap()
    wvT = nc.dram_tensor("wvT", [D, D], F32, kind="ExternalInput").ap()
    woT = nc.dram_tensor("woT", [D, D], F32, kind="ExternalInput").ap()
    wgu = nc.dram_tensor("wgu", [E, D, 2 * I], F32, kind="ExternalInput").ap()
    wdn = nc.dram_tensor("wdn", [E, I, D], F32, kind="ExternalInput").ap()
    outT = nc.dram_tensor("outT", [D, Tl], F32, kind="ExternalOutput").ap()

    kloc = nc.dram_tensor("kloc", [D, Tl], BF16)
    vloc_g = [nc.dram_tensor(f"vloc{g}", [Tl, 512], BF16) for g in range(4)]
    kgat_g = [nc.dram_tensor(f"kgat{g}", [R * 512, Tl], BF16) for g in range(4)]
    vgat_g = [nc.dram_tensor(f"vgat{g}", [R * Tl, 512], BF16) for g in range(4)]
    qdram = nc.dram_tensor("qdram", [D, Tl], BF16).ap()
    odram = nc.dram_tensor("odram", [D, Tl], BF16).ap()
    yadram = nc.dram_tensor("yadram", [D, Tl], F32).ap()
    x1dram = nc.dram_tensor("x1dram", [D, Tl], F32).ap()

    def sb(name, shape, dtype, off):
        return nc.alloc_sbuf_tensor_at(name, list(shape), dtype, offset=ARENA + off)

    A0 = 0
    a_sz = max(NCH * Tl * 2, 2 * 2 * S * 2)
    B0 = A0 + a_sz
    b_p6 = [("gsl", 2 * NCH * GW * 2), ("dsl", 2 * ICH * 128 * 2),
            ("sg", 2 * TE * 4), ("intr", ICH * TE * 2), ("outt", 2 * TE * 4)]
    b_sz = max(NCH * D * 2, sum(s for _, s in b_p6))
    C0 = B0 + b_sz
    c_sz = max(2 * Tl * 4 + 2 * Tl * 2 + 2 * Tl * 2, 2 * D * 2, KCH * 256 * 2, NCH * TE * 4)
    D0 = C0 + c_sz
    E0 = D0 + Tl * 4
    F0 = E0 + 2 * Tl * 4
    G0 = F0 + 2 * Tl * 4
    need = G0 + 1152 + TE * 4 + 4 * TE * 2
    assert need <= nc.sbuf_bytes_remaining, (need, nc.sbuf_bytes_remaining)
    _resv = nc.alloc_sbuf_tensor("arena_reserve", [128, need], mybir.dt.uint8)
    ARENA = nc.lookup_mloc(_resv).addr

    xb_all = sb("xb_all", [128, NCH * Tl], BF16, A0)
    kp_buf = sb("kp_buf", [128, 2 * 2 * S], BF16, A0)
    oall = sb("oall", [128, NCH * Tl], BF16, A0)

    wbuf = sb("wbuf", [128, NCH * D], BF16, B0)
    off = B0
    gslab = sb("gslab", [128, 2 * NCH * GW], BF16, off); off += b_p6[0][1]
    dslab = sb("dslab", [128, 2 * ICH * 128], BF16, off); off += b_p6[1][1]
    sg_t = sb("sg_t", [128, 2 * TE], F32, off); off += b_p6[2][1]
    inter = sb("inter", [128, ICH * TE], BF16, off); off += b_p6[3][1]
    out_t = sb("out_t", [128, 2 * TE], F32, off)

    xf_t = sb("xf_t", [128, 2 * Tl], F32, C0)
    x2b_t = sb("x2b_t", [128, 2 * Tl], BF16, C0 + 2 * Tl * 4)
    y1_t = sb("y1_t", [128, Tl], BF16, C0 + 2 * Tl * 4 + 2 * Tl * 2)
    ev_t = sb("ev_t", [128, 2 * D], BF16, C0)
    vp_buf = sb("vp_buf", [128, KCH * 256], BF16, C0)
    yab_t = sb("yab_t", [128, 2 * Tl], BF16, C0 + 2 * Tl * 4 + 2 * Tl * 2)
    yf_e = sb("yf_e", [128, NCH * TE], F32, C0)

    x1_t = sb("x1_t", [128, Tl], F32, D0)
    qh_buf = sb("qh_buf", [128, 2 * Tl], BF16, D0)
    rowA = sb("rowA", [1, Tl], F32, E0)
    pt_buf = sb("pt_buf", [128, 8 * QW], BF16, E0)
    rowC = sb("rowC", [1, Tl], F32, E0 + Tl * 4)
    rbb = sb("rbb", [128, Tl], F32, F0)
    ot_buf = sb("ot_buf", [128, 2 * QW], BF16, F0)
    mrbb = sb("mrbb", [128, Tl], F32, F0 + Tl * 4)
    accD = sb("accD", [128, QW], F32, F0 + Tl * 4)
    accP = sb("accP", [128, QW], F32, F0 + Tl * 4 + QW * 4)
    ones_bf = sb("ones_bf", [128, 1], BF16, G0)
    ones_row = sb("ones_row", [1, 128], F32, G0 + 32)
    konst = sb("konst", [1, 4], F32, G0 + 544)
    ones128b = sb("ones128b", [128, 128], BF16, G0 + 576)
    o1_t = sb("o1_t", [128, TE], F32, G0 + 1152)
    accB = sb("accB", [128, QW], BF16, G0 + 1152)
    sq6_t = sb("sq6_t", [128, 2 * TE], BF16, G0 + 1152 + TE * 4)
    cp6_t = sb("cp6_t", [128, 2 * TE], BF16, G0 + 1152 + TE * 4 + 2 * TE * 2)
    linv = sb("linv", [128, QW], F32, G0 + 1152 + TE * 4)

    pA = nc.alloc_psum_tensor("pA", [128, 2048], F32)
    pB = nc.alloc_psum_tensor("pB", [128, 2048], F32)

    class LazySems(dict):
        def __missing__(self, k):
            self[k] = nc.alloc_semaphore(k)
            return self[k]

    sems = LazySems()

    class Plan2:
        def __init__(self):
            self.ops = {e: [] for e in ENGINES}
            self.ctr = {}
            self.log = {e: [] for e in ENGINES}

        def emit(self, eng, fn):
            self.ops[eng].append(fn)

        def inc(self, sem, n=1, eng=None):
            self.ctr[sem] = self.ctr.get(sem, 0) + n
            if eng is not None:
                self.log[eng].append(("inc", sem, n, False))
            return self.ctr[sem]

        def wait(self, eng, sem, v):
            if v > 0:
                s = sems[sem]
                self.ops[eng].append(lambda e, s=s, v=v: e.wait_ge(s, v))
                self.log[eng].append(("wait", sem, v, False))

        def dma(self, eng, sem, fn):
            s = sems[sem]
            self.ops[eng].append(lambda e, fn=fn, s=s: fn(e).then_inc(s, 16))
            self.log[eng].append(("inc", sem, 16, True))
            return self.inc(sem, 16)

        def drain(self, eng):
            self.ops[eng].append(lambda e: e.drain())

    P = Plan2()
    tl_sl = _sl(Tl)
    d_sl = _sl(D)
    q_sl = _sl(QW)
    te_sl = _sl(TE)

    def emit_v(fn, inc=False):
        if inc:
            P.emit("vector", lambda e, fn=fn: fn(e).then_inc(sems["ve"], 1))
            return P.inc("ve", eng="vector")
        P.emit("vector", lambda e, fn=fn: fn(e))
        return None

    def emit_s(fn, inc=False):
        if inc:
            P.emit("scalar", lambda e, fn=fn: fn(e).then_inc(sems["ac"], 1))
            return P.inc("ac", eng="scalar")
        P.emit("scalar", lambda e, fn=fn: fn(e))
        return None

    def emit_t(fn, inc=False):
        if inc:
            P.emit("tensor", lambda e, fn=fn: fn(e).then_inc(sems["tm"], 1))
            return P.inc("tm", eng="tensor")
        P.emit("tensor", lambda e, fn=fn: fn(e))
        return None

    def emit_p(fn, inc=False):
        if inc:
            P.emit("gpsimd", lambda e, fn=fn: fn(e).then_inc(sems["po"], 1))
            return P.inc("po", eng="gpsimd")
        P.emit("gpsimd", lambda e, fn=fn: fn(e))
        return None

    def ln_rows2(width, wait_sem, wait_val, from_psum):
        P.wait("vector", wait_sem, wait_val)
        if from_psum:
            emit_v(lambda e: e.tensor_scalar_mul(rowA[0:1, 0:width], pA[0:1, 0:width], konst[0:1, 0:1]))
            v1 = emit_v(lambda e: e.tensor_scalar_mul(rowC[0:1, 0:width], pB[0:1, 0:width], konst[0:1, 0:1]), inc=True)
        else:
            emit_v(lambda e: e.tensor_scalar_mul(rowA[0:1, 0:width], rowA[0:1, 0:width], konst[0:1, 0:1]))
            v1 = emit_v(lambda e: e.tensor_scalar_mul(rowC[0:1, 0:width], rowC[0:1, 0:width], konst[0:1, 0:1]), inc=True)
        P.wait("scalar", "ve", v1)
        a1 = emit_s(lambda e: e.activation(out=mrbb[0:1, 0:width], in_=rowA[0:1, 0:width], func=AF.Square), inc=True)
        P.wait("vector", "ac", a1)
        v2 = emit_v(lambda e: e.tensor_tensor(rowC[0:1, 0:width], rowC[0:1, 0:width], mrbb[0:1, 0:width], ALU.subtract), inc=True)
        P.wait("scalar", "ve", v2)
        emit_s(lambda e: e.activation(out=rowC[0:1, 0:width], in_=rowC[0:1, 0:width], func=AF.Ln, bias=f32(cfg.EPS), scale=f32(1.0)))
        a2 = emit_s(lambda e: e.activation(out=rowC[0:1, 0:width], in_=rowC[0:1, 0:width], func=AF.Exp, bias=f32(0.0), scale=f32(-0.5)), inc=True)
        P.wait("vector", "ac", a2)
        v3 = emit_v(lambda e: e.tensor_tensor(rowA[0:1, 0:width], rowA[0:1, 0:width], rowC[0:1, 0:width], ALU.mult), inc=True)
        return v3

    def ln_bcast(width, ve_mr):
        P.wait("tensor", "ve", ve_mr)
        sl = _sl(width)
        for (o, w) in sl:
            emit_t(lambda e, o=o, w=w: e.matmul(pA[:, o:o + w], ones_row[:], rowC[0:1, o:o + w], start=True, stop=True))
        t1 = 0
        for gi, (o, w) in enumerate(sl):
            t1 = emit_t(lambda e, o=o, w=w: e.matmul(pB[:, o:o + w], ones_row[:], rowA[0:1, o:o + w], start=True, stop=True),
                        inc=(gi == len(sl) - 1))
        P.wait("vector", "tm", t1)
        emit_v(lambda e: e.tensor_copy(rbb[:, 0:width], pA[:, 0:width]))
        vb = emit_v(lambda e: e.tensor_copy(mrbb[:, 0:width], pB[:, 0:width]), inc=True)
        return vb

    emit_v(lambda e: e.memset(ones_bf[:], 1.0))
    emit_v(lambda e: e.memset(ones_row[:], 1.0))
    emit_v(lambda e: e.memset(ones128b[:], 1.0))
    emit_v(lambda e: e.memset(konst[0:1, 0:1], 1.0 / D), inc=True)

    ve_cast = [0] * NCH
    tm_st1 = [0] * NCH
    ldx = [0] * NCH
    for c in range(NCH):
        p = c % 2
        if c >= 2:
            P.wait("sync", "ve", ve_cast[c - 2])
        ldx[c] = P.dma("sync", f"ldx{p}", lambda e, c=c, p=p: e.dma_start(
            out=xf_t[:, p * Tl:(p + 1) * Tl], in_=xT[c * 128:(c + 1) * 128, :]))
        P.wait("vector", f"ldx{p}", ldx[c])
        ve_cast[c] = emit_v(lambda e, c=c, p=p: e.tensor_copy(
            xb_all[:, c * Tl:(c + 1) * Tl], xf_t[:, p * Tl:(p + 1) * Tl]), inc=True)
        P.wait("scalar", "ve", ve_cast[c])
        if c >= 1:
            P.wait("scalar", "tm", tm_st1[c - 1])
        ac1 = emit_s(lambda e, c=c: e.activation(
            out=x2b_t[:, 0:Tl], in_=xb_all[:, c * Tl:(c + 1) * Tl], func=AF.Square), inc=True)
        P.wait("tensor", "ac", ac1)
        for (o, w) in tl_sl:
            emit_t(lambda e, c=c, o=o, w=w: e.matmul(
                pA[0:1, o:o + w], ones_bf[:], xb_all[:, c * Tl + o:c * Tl + o + w],
                start=(c == 0), stop=(c == NCH - 1)))
        for gi, (o, w) in enumerate(tl_sl):
            tm_st1[c] = emit_t(lambda e, c=c, o=o, w=w: e.matmul(
                pB[0:1, o:o + w], ones_bf[:], x2b_t[:, o:o + w],
                start=(c == 0), stop=(c == NCH - 1)), inc=(gi == len(tl_sl) - 1))

    ve_mr1 = ln_rows2(Tl, "tm", tm_st1[NCH - 1], True)
    ve_b1 = ln_bcast(Tl, ve_mr1)
    P.wait("vector", "ve", ve_b1)
    ve_n1 = 0
    for c in range(NCH):
        emit_v(lambda e, c=c: e.tensor_tensor(
            y1_t[:, 0:Tl], xb_all[:, c * Tl:(c + 1) * Tl], rbb[:, 0:Tl], ALU.mult))
        ve_n1 = emit_v(lambda e, c=c: e.tensor_tensor(
            xb_all[:, c * Tl:(c + 1) * Tl], y1_t[:, 0:Tl], mrbb[:, 0:Tl], ALU.subtract), inc=True)

    wl_w = {}

    def load_weight(tag, src_ap, gate_fn):
        dstf = wbuf[:].rearrange("p (dc e) -> p dc e", dc=NCH)
        for g in range(4):
            P.wait("gpsimd", "tm", gate_fn(g))
            src = src_ap[:, g * 512:(g + 1) * 512].rearrange("(dc p) e -> p dc e", p=128)
            dst = dstf[:, :, g * 512:(g + 1) * 512]
            wl_w[(tag, g)] = P.dma("gpsimd", f"wl{g}", lambda e, src=src, dst=dst: e.dma_start(out=dst, in_=src))

    tm_proj = {}
    ve_evac = {}
    sd_store = {}

    def emit_proj_fm(tag, dst_rows_fn, prev_tag=None):
        for ech in range(NCH):
            reg = pA if ech % 2 == 0 else pB
            p = ech % 2
            if ech % 4 == 0:
                P.wait("tensor", f"wl{ech // 4}", wl_w[(tag, ech // 4)])
            if ech == 0:
                P.wait("tensor", "ve", ve_n1)
            if ech < 2 and prev_tag is not None:
                P.wait("tensor", "ve", ve_evac[(prev_tag, NCH - 2 + ech)])
            if ech >= 2:
                P.wait("tensor", "ve", ve_evac[(tag, ech - 2)])
            tmv = 0
            for dc in range(NCH):
                for gi, (o, w) in enumerate(tl_sl):
                    tmv = emit_t(lambda e, ech=ech, dc=dc, o=o, w=w, reg=reg: e.matmul(
                        reg[:, o:o + w],
                        wbuf[:, dc * D + ech * 128:dc * D + ech * 128 + 128],
                        xb_all[:, dc * Tl + o:dc * Tl + o + w],
                        start=(dc == 0), stop=(dc == NCH - 1)),
                        inc=(dc == NCH - 1 and gi == len(tl_sl) - 1)) or tmv
            tm_proj[(tag, ech)] = tmv
            P.wait("vector", "tm", tmv)
            if ech >= 2:
                P.wait("vector", f"sde{p}", sd_store[(tag, ech - 2)])
            elif prev_tag is not None:
                P.wait("vector", f"sde{p}", sd_store[(prev_tag, NCH - 2 + ech)])
            ve_evac[(tag, ech)] = emit_v(lambda e, p=p, reg=reg: e.tensor_copy(
                ev_t[:, p * D:p * D + Tl], reg[:, 0:Tl]), inc=True)
            P.wait("sync", "ve", ve_evac[(tag, ech)])
            sd_store[(tag, ech)] = P.dma("sync", f"sde{p}", lambda e, ech=ech, p=p, fn=dst_rows_fn: e.dma_start(
                out=fn(ech), in_=ev_t[:, p * D:p * D + Tl]))

    rg = [[0, 1, 2, 3], [4, 5, 6, 7]]
    load_weight("k", wkT, lambda g: 0)
    emit_proj_fm("k", lambda ech: kloc.ap()[ech * 128:(ech + 1) * 128, :])
    ccK = [0] * 4
    for g in range(4):
        P.wait("gpsimd", f"sde{(4 * g + 2) % 2}", sd_store[("k", 4 * g + 2)])
        P.wait("gpsimd", f"sde{(4 * g + 3) % 2}", sd_store[("k", 4 * g + 3)])
        P.emit("gpsimd", lambda e, g=g: e.collective_compute(
            "AllGather", ALU.bypass, replica_groups=rg,
            ins=[kloc.ap()[g * 512:(g + 1) * 512, :].opt()],
            outs=[kgat_g[g].ap().opt()]).then_inc(sems["cc"], 1))
        ccK[g] = P.inc("cc", eng="gpsimd")
    load_weight("v", wvT, lambda g: tm_proj[("k", 4 * g + 3)])

    for tt in range(TCH):
        reg = pA if tt % 2 == 0 else pB
        p = tt % 2
        if tt == 0:
            for g4 in range(4):
                P.wait("tensor", f"wl{g4}", wl_w[("v", g4)])
        if tt < 2:
            P.wait("tensor", "ve", ve_evac[("k", NCH - 2 + tt)])
        if tt >= 2:
            P.wait("tensor", "ve", ve_evac[("v", tt - 2)])
        tmv = 0
        for dc in range(NCH):
            for gi, (o, w) in enumerate(d_sl):
                tmv = emit_t(lambda e, tt=tt, dc=dc, o=o, w=w, reg=reg: e.matmul(
                    reg[:, o:o + w],
                    xb_all[:, dc * Tl + tt * 128:dc * Tl + tt * 128 + 128],
                    wbuf[:, dc * D + o:dc * D + o + w],
                    start=(dc == 0), stop=(dc == NCH - 1)),
                    inc=(dc == NCH - 1 and gi == len(d_sl) - 1)) or tmv
        tm_proj[("v", tt)] = tmv
        P.wait("vector", "tm", tmv)
        if tt >= 2:
            P.wait("vector", f"sde{p}", sd_store[("v", tt - 2)])
        else:
            P.wait("vector", f"sde{p}", sd_store[("k", NCH - 2 + tt)])
        ve_evac[("v", tt)] = emit_v(lambda e, p=p, reg=reg: e.tensor_copy(
            ev_t[:, p * D:(p + 1) * D], reg[:, 0:D]), inc=True)
        P.wait("sync", "ve", ve_evac[("v", tt)])
        for g4 in range(4):
            sd_store[("v", tt)] = P.dma("sync", f"sde{p}", lambda e, tt=tt, p=p, g4=g4: e.dma_start(
                out=vloc_g[g4].ap()[tt * 128:(tt + 1) * 128, :],
                in_=ev_t[:, p * D + g4 * 512:p * D + (g4 + 1) * 512]))

    P.wait("gpsimd", "sde0", P.ctr.get("sde0", 0))
    P.wait("gpsimd", "sde1", P.ctr.get("sde1", 0))
    ccV = [0] * 4
    for g in range(4):
        P.emit("gpsimd", lambda e, g=g: e.collective_compute(
            "AllGather", ALU.bypass, replica_groups=rg,
            ins=[vloc_g[g].ap().opt()],
            outs=[vgat_g[g].ap().opt()]).then_inc(sems["cc"], 1))
        ccV[g] = P.inc("cc", eng="gpsimd")
    cc_v = ccV[0]
    load_weight("q", wqT, lambda g: tm_proj[("v", TCH - 1)])
    emit_proj_fm("q", lambda ech: qdram[ech * 128:(ech + 1) * 128, :], prev_tag="v")
    sde_tot = [P.ctr.get("sde0", 0), P.ctr.get("sde1", 0)]
    load_weight("o", woT, lambda g: tm_proj[("q", 4 * g + 3)])

    kgat_r = [t.ap().rearrange("(r d) t -> d r t", r=R) for t in kgat_g]
    NVQ = 4
    JQ = KCH // NVQ

    def pool_own(jj):
        return jj % 3 == 0 and jj < 63

    ld_k = [0] * HP
    ld_v = [[0] * NVQ for _ in range(HP)]
    ld_qh = [0] * H
    tm_sc_g = {}
    tm_pv_g = {}
    ac_g = {}
    po_g = {}
    ve_add_g = {}
    ve_acs = {}
    tm_col = {}
    ve_rc = {}
    ve_ot = {}
    sd_od = {}

    def emit_kload(hp):
        slot = hp % 2
        for hh in range(2):
            h = hp * 2 + hh
            P.wait("sync", "cc", ccK[h // 4])
            kdst = kp_buf[:, (slot * 2 + hh) * S:(slot * 2 + hh + 1) * S].rearrange(
                "p (r t) -> p r t", r=R)
            hd = (h % 4) * 128
            ld_k[hp] = P.dma("sync", f"lk{slot}", lambda e, h=h, hd=hd, kdst=kdst: e.dma_start(
                out=kdst, in_=kgat_r[h // 4][hd:hd + 128, :, :]))

    def emit_vload(hp, eng):
        P.wait(eng, "cc", ccV[hp // 2])
        vga = vgat_g[hp // 2].ap()
        co = (hp % 2) * 256
        for vq in range(NVQ):
            vsrc = vga[vq * (S // NVQ):(vq + 1) * (S // NVQ),
                       co:co + 256].rearrange("(j p) u -> p j u", p=128)
            vdst = vp_buf[:, vq * JQ * 256:(vq + 1) * JQ * 256].rearrange(
                "p (j u) -> p j u", j=JQ)
            ld_v[hp][vq] = P.dma(eng, f"lv{vq}", lambda e, vsrc=vsrc, vdst=vdst: e.dma_start(
                out=vdst, in_=vsrc))

    tm_v_done = tm_proj[("q", NCH - 1)]
    P.wait("sync", "tm", tm_v_done)
    P.wait("sync", "cc", cc_v)
    P.wait("sync", "sde0", sde_tot[0])
    P.wait("sync", "sde1", sde_tot[1])
    emit_kload(0)
    emit_vload(0, "sync")
    if HP > 1:
        emit_kload(1)

    it = 0
    for hp in range(HP):
        slot = hp % 2
        for hh in range(2):
            h = hp * 2 + hh
            hq = h % 2
            if h >= 2:
                P.wait("sync", "tm",
                       tm_sc_g[((h - 2) * NQ + NQ - 1) * KCH + KCH - 1])
            ld_qh[h] = P.dma("sync", f"lq{hq}", lambda e, h=h, hq=hq: e.dma_start(
                out=qh_buf[:, hq * Tl:(hq + 1) * Tl], in_=qdram[h * 128:(h + 1) * 128, :]))

            for qh in range(NQ):
                qb = qh * QW
                bg = it * KCH
                s_pv = (it % 2) * QW
                it_loc = it

                def emit_scores_exp_pair(mm, bg=None, qb=None, hh=None, hq=None, slot=None, hp=None, h=None):
                    for jj in (2 * mm, 2 * mm + 1):
                        g = bg + jj
                        if g >= 2:
                            P.wait("tensor", "ac", ac_g[g - 2])
                        if jj == 0:
                            P.wait("tensor", f"lk{slot}", ld_k[hp])
                            P.wait("tensor", f"lq{hq}", ld_qh[h])
                        ps = (g % 2) * QW
                        for gi, (o, w) in enumerate(q_sl):
                            r = emit_t(lambda e, hh=hh, jj=jj, o=o, w=w, qb=qb, hq=hq,
                                       slot=slot, ps=ps: e.matmul(
                                pA[:, ps + o:ps + o + w],
                                kp_buf[:, (slot * 2 + hh) * S + jj * 128:(slot * 2 + hh) * S + jj * 128 + 128],
                                qh_buf[:, hq * Tl + qb + o:hq * Tl + qb + o + w],
                                start=True, stop=True),
                                inc=(gi == len(q_sl) - 1))
                            if r:
                                tm_sc_g[g] = r
                    for jj in (2 * mm, 2 * mm + 1):
                        g = bg + jj
                        P.wait("scalar", "tm", tm_sc_g[g])
                        if jj % 2 == 0 and g >= 8:
                            P.wait("scalar", "tm", tm_pv_g[g - 7])
                            pov = [po_g[c] for c in (g - 8, g - 7) if pool_own(c % KCH)]
                            vev = [ve_add_g[c] for c in (g - 8, g - 7) if not pool_own(c % KCH)]
                            if pov:
                                P.wait("scalar", "po", max(pov))
                            if vev:
                                P.wait("scalar", "ve", max(vev))
                        ac_g[g] = emit_s(lambda e, g=g: e.activation(
                            out=pt_buf[:, (g % 8) * QW:(g % 8) * QW + QW],
                            in_=pA[:, (g % 2) * QW:(g % 2) * QW + QW],
                            func=AF.Exp, bias=f32(cfg.SHIFT), scale=f32(cfg.SCALE)), inc=True)

                def emit_pv_acc_pair(mm, bg=None, hh=None, s_pv=None, hp=None, it_loc=None):
                    for jj in (2 * mm, 2 * mm + 1):
                        g = bg + jj
                        P.wait("tensor", "ac", ac_g[g])
                        if jj % JQ == 0:
                            P.wait("tensor", f"lv{jj // JQ}", ld_v[hp][jj // JQ])
                        if jj == 0 and it_loc >= 1:
                            P.wait("tensor", "ve", ve_rc[it_loc - 1])
                        tpv = 0
                        for gi, (o, w) in enumerate(q_sl):
                            tpv = emit_t(lambda e, jj=jj, hh=hh, o=o, w=w, s_pv=s_pv,
                                         pq=(g % 8) * QW: e.matmul(
                                pB[:, s_pv + o:s_pv + o + w],
                                vp_buf[:, jj * 256 + hh * 128:jj * 256 + hh * 128 + 128],
                                pt_buf[:, pq + o:pq + o + w],
                                start=(jj == 0), stop=(jj == KCH - 1)),
                                inc=(gi == len(q_sl) - 1)) or tpv
                        tm_pv_g[g] = tpv
                    for jj in (2 * mm, 2 * mm + 1):
                        g = bg + jj
                        if pool_own(jj):
                            P.wait("gpsimd", "ac", ac_g[g])
                            if jj == 0 and it_loc >= 1:
                                P.wait("gpsimd", "ve", ve_acs[it_loc - 1])
                            if jj == 0:
                                po_g[g] = emit_p(lambda e, g=g: e.tensor_copy(
                                    accP[:, 0:QW], pt_buf[:, (g % 8) * QW:(g % 8) * QW + QW]),
                                    inc=True)
                            else:
                                po_g[g] = emit_p(lambda e, g=g: e.tensor_tensor(
                                    accP[:, 0:QW], accP[:, 0:QW],
                                    pt_buf[:, (g % 8) * QW:(g % 8) * QW + QW], ALU.add),
                                    inc=True)
                        else:
                            P.wait("vector", "ac", ac_g[g])
                            if jj == 1:
                                ve_add_g[g] = emit_v(lambda e, g=g: e.tensor_copy(
                                    accD[:, 0:QW], pt_buf[:, (g % 8) * QW:(g % 8) * QW + QW]),
                                    inc=True)
                            else:
                                ve_add_g[g] = emit_v(lambda e, g=g: e.tensor_tensor(
                                    accD[:, 0:QW], accD[:, 0:QW],
                                    pt_buf[:, (g % 8) * QW:(g % 8) * QW + QW], ALU.add),
                                    inc=True)

                def emit_tail(pit):
                    pc = ((pit + 1) % 2) * QW
                    pv = (pit % 2) * QW
                    P.wait("tensor", "ve", ve_acs[pit])
                    if pit >= 1:
                        P.wait("tensor", "ve", ve_ot[pit - 1])
                    tcol = 0
                    for gi, (o, w) in enumerate(q_sl):
                        tcol = emit_t(lambda e, o=o, w=w, pc=pc: e.matmul(
                            pB[:, pc + o:pc + o + w], ones128b[:], accB[:, o:o + w],
                            start=True, stop=True), inc=(gi == len(q_sl) - 1)) or tcol
                    tm_col[pit] = tcol
                    P.wait("vector", "tm", tm_col[pit])
                    ve_rc[pit] = emit_v(lambda e, pc=pc: e.reciprocal(
                        linv[:, 0:QW], pB[:, pc:pc + QW]), inc=True)
                    osel = pit % 2
                    P.wait("vector", "tm", tm_pv_g[pit * KCH + KCH - 1])
                    if pit >= 2:
                        P.wait("vector", f"sod{osel}", sd_od[pit - 2])
                    ve_ot[pit] = emit_v(lambda e, pv=pv, osel=osel: e.tensor_tensor(
                        ot_buf[:, osel * QW:(osel + 1) * QW], pB[:, pv:pv + QW],
                        linv[:, 0:QW], ALU.mult), inc=True)
                    P.wait("sync", "ve", ve_ot[pit])
                    ph = pit % (NQ * H) // NQ
                    pqb = (pit % NQ) * QW
                    sd_od[pit] = P.dma("sync", f"sod{osel}", lambda e, ph=ph, pqb=pqb, osel=osel: e.dma_start(
                        out=odram[ph * 128:(ph + 1) * 128, pqb:pqb + QW],
                        in_=ot_buf[:, osel * QW:(osel + 1) * QW]))

                kw_se = dict(bg=bg, qb=qb, hh=hh, hq=hq, slot=slot, hp=hp, h=h)
                kw_pv = dict(bg=bg, hh=hh, s_pv=s_pv, hp=hp, it_loc=it_loc)
                for m in range(KCH // 2):
                    emit_scores_exp_pair(m, **kw_se)
                    if m == 1 and it_loc >= 1:
                        emit_tail(it_loc - 1)
                    if m >= 1:
                        emit_pv_acc_pair(m - 1, **kw_pv)
                emit_pv_acc_pair(KCH // 2 - 1, **kw_pv)
                P.wait("vector", "po", po_g[bg + 60])
                ve_acs[it_loc] = emit_v(lambda e: e.tensor_tensor(
                    accB[:, 0:QW], accD[:, 0:QW], accP[:, 0:QW], ALU.add), inc=True)
                if it_loc == H * NQ - 1:
                    emit_tail(it_loc)
                it += 1

        if hp + 1 < HP:
            P.wait("gpsimd", "tm", tm_pv_g[it * KCH - 1])
            emit_vload(hp + 1, "gpsimd")
        if hp + 2 < HP:
            P.wait("sync", "tm", tm_sc_g[it * KCH - 1])
            emit_kload(hp + 2)

    sod_tot = [P.ctr.get("sod0", 0), P.ctr.get("sod1", 0)]

    P.wait("sync", "sod0", sod_tot[0])
    P.wait("sync", "sod1", sod_tot[1])
    ld_oall = 0
    for dc in range(NCH):
        ld_oall = P.dma("sync", "lo", lambda e, dc=dc: e.dma_start(
            out=oall[:, dc * Tl:(dc + 1) * Tl], in_=odram[dc * 128:(dc + 1) * 128, :]))

    emit_v(lambda e: e.memset(rowA[0:1, 0:Tl], 0.0))
    ve_z5 = emit_v(lambda e: e.memset(rowC[0:1, 0:Tl], 0.0), inc=True)

    ve_ya = [0] * NCH
    sd_ya = [0] * NCH
    ac_sq5 = [0] * NCH
    ve_srC = 0
    tm_sum5 = [0] * NCH
    tm_sq5v = [0] * NCH
    tm_op = 0

    def emit_stats5(k):
        nonlocal ve_srC
        p = k % 2
        P.wait("tensor", "ac", ac_sq5[k])
        P.wait("tensor", "ve", ve_z5 if k == 0 else ve_srC)
        t1 = 0
        for gi, (o, w) in enumerate(tl_sl):
            t1 = emit_t(lambda e, p=p, o=o, w=w: e.matmul(
                pB[0:1, o:o + w], ones_bf[:], yab_t[:, p * Tl + o:p * Tl + o + w],
                start=True, stop=True), inc=(gi == len(tl_sl) - 1)) or t1
        tm_sum5[k] = t1
        P.wait("vector", "tm", t1)
        ve_srA = emit_v(lambda e: e.tensor_tensor(
            rowA[0:1, 0:Tl], rowA[0:1, 0:Tl], pB[0:1, 0:Tl], ALU.add), inc=True)
        P.wait("tensor", "ve", ve_srA)
        t2 = 0
        for gi, (o, w) in enumerate(tl_sl):
            t2 = emit_t(lambda e, p=p, o=o, w=w: e.matmul(
                pB[0:1, o:o + w], ones_bf[:], x2b_t[:, p * Tl + o:p * Tl + o + w],
                start=True, stop=True), inc=(gi == len(tl_sl) - 1)) or t2
        tm_sq5v[k] = t2
        P.wait("vector", "tm", t2)
        ve_srC = emit_v(lambda e: e.tensor_tensor(
            rowC[0:1, 0:Tl], rowC[0:1, 0:Tl], pB[0:1, 0:Tl], ALU.add), inc=True)

    for ech in range(NCH):
        p = ech % 2
        if ech % 4 == 0:
            P.wait("tensor", f"wl{ech // 4}", wl_w[("o", ech // 4)])
        if ech == 0:
            P.wait("tensor", "lo", ld_oall)
        else:
            P.wait("tensor", "ve", ve_ya[ech - 1])
        for dc in range(NCH):
            for gi, (o, w) in enumerate(tl_sl):
                tm_op = emit_t(lambda e, ech=ech, dc=dc, o=o, w=w: e.matmul(
                    pA[:, o:o + w],
                    wbuf[:, dc * D + ech * 128:dc * D + ech * 128 + 128],
                    oall[:, dc * Tl + o:dc * Tl + o + w],
                    start=(dc == 0), stop=(dc == NCH - 1)),
                    inc=(dc == NCH - 1 and gi == len(tl_sl) - 1)) or tm_op
        if ech >= 1:
            emit_stats5(ech - 1)
        P.wait("vector", "tm", tm_op)
        if ech >= 2:
            P.wait("vector", f"sya{p}", sd_ya[ech - 2])
            P.wait("vector", "ac", ac_sq5[ech - 2])
        ve_ya[ech] = emit_v(lambda e, p=p: e.tensor_copy(
            xf_t[:, p * Tl:(p + 1) * Tl], pA[:, 0:Tl]), inc=True)
        P.wait("sync", "ve", ve_ya[ech])
        sd_ya[ech] = P.dma("sync", f"sya{p}", lambda e, ech=ech, p=p: e.dma_start(
            out=yadram[ech * 128:(ech + 1) * 128, :], in_=xf_t[:, p * Tl:(p + 1) * Tl]))
        if ech >= 2:
            P.wait("vector", "tm", tm_sum5[ech - 2])
        ve_yab = emit_v(lambda e, p=p: e.tensor_copy(
            yab_t[:, p * Tl:(p + 1) * Tl], xf_t[:, p * Tl:(p + 1) * Tl]), inc=True)
        P.wait("scalar", "ve", ve_yab)
        if ech >= 2:
            P.wait("scalar", "tm", tm_sq5v[ech - 2])
        ac_sq5[ech] = emit_s(lambda e, p=p: e.activation(
            out=x2b_t[:, p * Tl:(p + 1) * Tl], in_=xf_t[:, p * Tl:(p + 1) * Tl],
            func=AF.Square), inc=True)
    emit_stats5(NCH - 1)

    ve_mr2 = ln_rows2(Tl, "ve", ve_srC, False)
    ve_b2 = ln_bcast(Tl, ve_mr2)

    sd_x1 = [0] * NCH
    ve_x1b = [0] * NCH
    ac_sq3 = [0] * NCH
    tm_sq3 = 0
    for c in range(NCH):
        p = c % 2
        if c < 2:
            P.wait("sync", "ac", ac_sq5[NCH - 1])
        else:
            P.wait("sync", "ve", ve_x1b[c - 2])
        P.wait("sync", f"sya{p}", sd_ya[NCH - 2 + p] if NCH >= 2 else sd_ya[0])
        P.dma("sync", f"lya{p}", lambda e, c=c, p=p: e.dma_start(
            out=xf_t[:, p * Tl:(p + 1) * Tl], in_=yadram[c * 128:(c + 1) * 128, :]))
        ld_ya = P.ctr[f"lya{p}"]
        if c >= 1:
            P.wait("sync", "sx1", sd_x1[c - 1])
            P.wait("sync", "ac", ac_sq3[c - 1])
        ld_x = P.dma("sync", "lx1", lambda e, c=c: e.dma_start(
            out=x1_t[:, 0:Tl], in_=xT[c * 128:(c + 1) * 128, :]))
        P.wait("vector", f"lya{p}", ld_ya)
        P.wait("vector", "lx1", ld_x)
        if c == 0:
            P.wait("vector", "ve", ve_b2)
        emit_v(lambda e, p=p: e.tensor_tensor(
            xf_t[:, p * Tl:(p + 1) * Tl], xf_t[:, p * Tl:(p + 1) * Tl], rbb[:, 0:Tl], ALU.mult))
        emit_v(lambda e, p=p: e.tensor_tensor(
            xf_t[:, p * Tl:(p + 1) * Tl], xf_t[:, p * Tl:(p + 1) * Tl], mrbb[:, 0:Tl], ALU.subtract))
        emit_v(lambda e, p=p: e.tensor_tensor(
            x1_t[:, 0:Tl], x1_t[:, 0:Tl], xf_t[:, p * Tl:(p + 1) * Tl], ALU.add))
        ve_x1b[c] = emit_v(lambda e, c=c: e.tensor_copy(
            xb_all[:, c * Tl:(c + 1) * Tl], x1_t[:, 0:Tl]), inc=True)
        P.wait("sync", "ve", ve_x1b[c])
        sd_x1[c] = P.dma("sync", "sx1", lambda e, c=c: e.dma_start(
            out=x1dram[c * 128:(c + 1) * 128, :], in_=x1_t[:, 0:Tl]))
        P.wait("scalar", "ve", ve_x1b[c])
        if c >= 1:
            P.wait("scalar", "tm", tm_sq3)
        ac_sq3[c] = emit_s(lambda e, c=c: e.activation(
            out=x2b_t[:, 0:Tl], in_=x1_t[:, 0:Tl], func=AF.Square), inc=True)
        P.wait("tensor", "ac", ac_sq3[c])
        if c == 0:
            P.wait("tensor", "ve", ve_b2)
        for (o, w) in tl_sl:
            emit_t(lambda e, c=c, o=o, w=w: e.matmul(
                pA[0:1, o:o + w], ones_bf[:], xb_all[:, c * Tl + o:c * Tl + o + w],
                start=(c == 0), stop=(c == NCH - 1)))
        for gi, (o, w) in enumerate(tl_sl):
            tm_sq3 = emit_t(lambda e, c=c, o=o, w=w: e.matmul(
                pB[0:1, o:o + w], ones_bf[:], x2b_t[:, o:o + w],
                start=(c == 0), stop=(c == NCH - 1)), inc=(gi == len(tl_sl) - 1)) or tm_sq3

    ve_mr3 = ln_rows2(Tl, "tm", tm_sq3, True)
    ve_b3 = ln_bcast(Tl, ve_mr3)
    P.wait("vector", "ve", ve_b3)
    ve_n3 = 0
    for c in range(NCH):
        emit_v(lambda e, c=c: e.tensor_tensor(
            yab_t[:, 0:Tl], xb_all[:, c * Tl:(c + 1) * Tl], rbb[:, 0:Tl], ALU.mult))
        ve_n3 = emit_v(lambda e, c=c: e.tensor_tensor(
            xb_all[:, c * Tl:(c + 1) * Tl], yab_t[:, 0:Tl], mrbb[:, 0:Tl], ALU.subtract), inc=True)

    n_out = 0
    so_out = {}
    tm_gu_done = {}
    wl_gsl = {}
    ve_nrm_last = 0
    ve_dn_prev = 0
    ve_b4 = 0
    sx1_tot = P.ctr.get("sx1", 0)
    for ee in range(E):
        ecol = ee * TE
        CPS = GW // 128
        ngroups = NSLAB // 2

        def emit_gu_piece(ee, slot, j):
            if ee == 0 and j < CPS:
                P.wait("gpsimd", "tm", tm_op)
            elif j < CPS:
                P.wait("gpsimd", "tm", tm_gu_done[
                    (ee - 1, (ICH - CPS + j) + (0 if slot == 0 else ICH))])
            else:
                P.wait("gpsimd", "tm", tm_gu_done[
                    (ee, (j - CPS) + (0 if slot == 0 else ICH))])
            col = j * 128 + (0 if slot == 0 else I)
            srcw = wgu[ee, :, col:col + 128].rearrange("(dc p) i -> p dc i", p=128)
            sl = gslab[:, slot * NCH * GW:(slot + 1) * NCH * GW].rearrange(
                "p (dc i) -> p dc i", dc=NCH)
            gdst = sl[:, :, (j % CPS) * 128:(j % CPS) * 128 + 128]
            wl_gsl[(ee, slot, j)] = P.dma("gpsimd", f"wg{slot}{j % CPS}",
                                          lambda e, srcw=srcw, gdst=gdst: e.dma_start(out=gdst, in_=srcw))

        for j0 in range(CPS):
            emit_gu_piece(ee, 0, j0)
            emit_gu_piece(ee, 1, j0)

        ac_sg = [0] * ICH
        ve_in = [0] * ICH
        for ihalf in range(ICH):
            for which in ("g", "u"):
                icg = ihalf if which == "g" else ihalf + ICH
                offw = (ihalf % CPS) * 128
                slot = 0 if which == "g" else 1
                base = (ihalf % 2) * 1024 + (0 if which == "g" else 512)
                P.wait("tensor", f"wg{slot}{ihalf % CPS}", wl_gsl[(ee, slot, ihalf)])
                if ee == 0 and ihalf == 0 and which == "g":
                    P.wait("tensor", "ve", ve_n3)
                if ihalf == 0 and which == "g" and ee > 0:
                    P.wait("tensor", "ve", ve_b4)
                if ihalf >= 2:
                    if which == "g":
                        P.wait("tensor", "ac", ac_sg[ihalf - 2])
                    else:
                        P.wait("tensor", "ve", ve_in[ihalf - 2])
                tmg = 0
                for dc in range(NCH):
                    for gi, (o, w) in enumerate(te_sl):
                        tmg = emit_t(lambda e, offw=offw, dc=dc, o=o, w=w,
                                     base=base, slot=slot, ecol=ecol: e.matmul(
                            pA[:, base + o:base + o + w],
                            gslab[:, slot * NCH * GW + dc * GW + offw:
                                  slot * NCH * GW + dc * GW + offw + 128],
                            xb_all[:, dc * Tl + ecol + o:dc * Tl + ecol + o + w],
                            start=(dc == 0), stop=(dc == NCH - 1)),
                            inc=(dc == NCH - 1 and gi == len(te_sl) - 1)) or tmg
                tm_gu_done[(ee, icg)] = tmg
            P.wait("scalar", "tm", tm_gu_done[(ee, ihalf)])
            if ihalf >= 2:
                P.wait("scalar", "ve", ve_in[ihalf - 2])
            ac_sg[ihalf] = emit_s(lambda e, ihalf=ihalf: e.activation(
                out=sg_t[:, (ihalf % 2) * TE:(ihalf % 2 + 1) * TE],
                in_=pA[:, (ihalf % 2) * 1024:(ihalf % 2) * 1024 + TE], func=AF.Silu), inc=True)
            P.wait("vector", "ac", ac_sg[ihalf])
            P.wait("vector", "tm", tm_gu_done[(ee, ihalf + ICH)])
            ve_in[ihalf] = emit_v(lambda e, ihalf=ihalf: e.tensor_tensor(
                inter[:, ihalf * TE:(ihalf + 1) * TE],
                pA[:, (ihalf % 2) * 1024 + 512:(ihalf % 2) * 1024 + 512 + TE],
                sg_t[:, (ihalf % 2) * TE:(ihalf % 2 + 1) * TE], ALU.mult), inc=True)
            if ihalf + CPS < ICH:
                emit_gu_piece(ee, 0, ihalf + CPS)
                emit_gu_piece(ee, 1, ihalf + CPS)

        ve_dn = [0] * NCH
        ac_c6 = [0] * NCH
        ve_c6 = [0] * NCH
        tm_sum6 = [0] * NCH
        tm_sq6 = [0] * NCH
        tm_st6 = 0

        def emit_stats6(k):
            nonlocal tm_st6
            pk = k % 2
            P.wait("tensor", "ac", ac_c6[k])
            if k == 0 and ee == 0:
                P.wait("tensor", "ve", ve_b3)
            if k == 0 and ee > 0:
                P.wait("tensor", "ve", ve_b4)
            t1 = 0
            for gi, (o, w) in enumerate(te_sl):
                t1 = emit_t(lambda e, k=k, pk=pk, o=o, w=w: e.matmul(
                    pA[0:1, o:o + w], ones_bf[:], cp6_t[:, pk * TE + o:pk * TE + o + w],
                    start=(k == 0), stop=(k == NCH - 1)), inc=(gi == len(te_sl) - 1)) or t1
            tm_sum6[k] = t1
            t2 = 0
            for gi, (o, w) in enumerate(te_sl):
                t2 = emit_t(lambda e, k=k, pk=pk, o=o, w=w: e.matmul(
                    pA[0:1, 512 + o:512 + o + w], ones_bf[:], sq6_t[:, pk * TE + o:pk * TE + o + w],
                    start=(k == 0), stop=(k == NCH - 1)), inc=(gi == len(te_sl) - 1)) or t2
            tm_sq6[k] = t2
            tm_st6 = t2

        for hc in range(NCH):
            hp2 = hc % 2
            src = wdn[ee, :, hc * 128:(hc + 1) * 128].rearrange("(ic p) hh -> p ic hh", p=128)
            if ee == 0 and hc < 2:
                P.wait("gpsimd", "tm", tm_op)
            elif hc >= 2:
                P.wait("gpsimd", "tm", tm_gu_done[("dn", ee, hc - 2)])
            else:
                P.wait("gpsimd", "tm", tm_gu_done[("dn", ee - 1, NCH - 1)])
            ddst = dslab[:, hp2 * ICH * 128:(hp2 + 1) * ICH * 128].rearrange(
                "p (ic hh) -> p ic hh", ic=ICH)
            wl_dn = P.dma("gpsimd", f"wd{hp2}", lambda e, src=src, ddst=ddst: e.dma_start(
                out=ddst, in_=src))
            P.wait("tensor", f"wd{hp2}", wl_dn)
            if hc == 0:
                P.wait("tensor", "ve", ve_in[ICH - 1])
                if ee > 0:
                    P.wait("tensor", "ve", ve_dn_prev)
            if hc >= 2:
                P.wait("tensor", "ve", ve_dn[hc - 2])
            tm_dn = 0
            for ic in range(ICH):
                for gi, (o, w) in enumerate(te_sl):
                    tm_dn = emit_t(lambda e, hp2=hp2, ic=ic, o=o, w=w: e.matmul(
                        pB[:, hp2 * 512 + o:hp2 * 512 + o + w],
                        dslab[:, hp2 * ICH * 128 + ic * 128:hp2 * ICH * 128 + ic * 128 + 128],
                        inter[:, ic * TE + o:ic * TE + o + w],
                        start=(ic == 0), stop=(ic == ICH - 1)),
                        inc=(ic == ICH - 1 and gi == len(te_sl) - 1)) or tm_dn
            tm_gu_done[("dn", ee, hc)] = tm_dn
            if hc >= 1:
                emit_stats6(hc - 1)
            P.wait("vector", "tm", tm_dn)
            if ee > 0 and hc == 0:
                P.wait("vector", "ve", ve_nrm_last)
            ve_dn[hc] = emit_v(lambda e, hc=hc, hp2=hp2: e.tensor_copy(
                yf_e[:, hc * TE:(hc + 1) * TE], pB[:, hp2 * 512:hp2 * 512 + TE]), inc=True)
            if hc >= 2:
                P.wait("vector", "tm", tm_sum6[hc - 2])
            ve_c6[hc] = emit_v(lambda e, hc=hc, hp2=hp2: e.tensor_copy(
                cp6_t[:, hp2 * TE:(hp2 + 1) * TE], yf_e[:, hc * TE:(hc + 1) * TE]), inc=True)
            P.wait("scalar", "ve", ve_c6[hc])
            if hc >= 2:
                P.wait("scalar", "tm", tm_sq6[hc - 2])
            ac_c6[hc] = emit_s(lambda e, hc=hc, hp2=hp2: e.activation(
                out=sq6_t[:, hp2 * TE:(hp2 + 1) * TE], in_=yf_e[:, hc * TE:(hc + 1) * TE],
                func=AF.Square), inc=True)
        emit_stats6(NCH - 1)
        ve_dn_prev = ve_dn[NCH - 1]

        P.wait("vector", "tm", tm_st6)
        emit_v(lambda e: e.tensor_scalar_mul(rowA[0:1, 0:TE], pA[0:1, 0:TE], konst[0:1, 0:1]))
        v1 = emit_v(lambda e: e.tensor_scalar_mul(rowC[0:1, 0:TE], pA[0:1, 512:512 + TE], konst[0:1, 0:1]), inc=True)
        P.wait("scalar", "ve", v1)
        a1 = emit_s(lambda e: e.activation(out=mrbb[0:1, 0:TE], in_=rowA[0:1, 0:TE], func=AF.Square), inc=True)
        P.wait("vector", "ac", a1)
        v2 = emit_v(lambda e: e.tensor_tensor(rowC[0:1, 0:TE], rowC[0:1, 0:TE], mrbb[0:1, 0:TE], ALU.subtract), inc=True)
        P.wait("scalar", "ve", v2)
        emit_s(lambda e: e.activation(out=rowC[0:1, 0:TE], in_=rowC[0:1, 0:TE], func=AF.Ln, bias=f32(cfg.EPS), scale=f32(1.0)))
        a2 = emit_s(lambda e: e.activation(out=rowC[0:1, 0:TE], in_=rowC[0:1, 0:TE], func=AF.Exp, bias=f32(0.0), scale=f32(-0.5)), inc=True)
        P.wait("vector", "ac", a2)
        v3 = emit_v(lambda e: e.tensor_tensor(rowA[0:1, 0:TE], rowA[0:1, 0:TE], rowC[0:1, 0:TE], ALU.mult), inc=True)
        P.wait("tensor", "ve", v3)
        for (o, w) in te_sl:
            emit_t(lambda e, o=o, w=w: e.matmul(
                pA[:, 1024 + o:1024 + o + w], ones_row[:], rowC[0:1, o:o + w], start=True, stop=True))
        tm_b4 = 0
        for gi, (o, w) in enumerate(te_sl):
            tm_b4 = emit_t(lambda e, o=o, w=w: e.matmul(
                pA[:, 1536 + o:1536 + o + w], ones_row[:], rowA[0:1, o:o + w],
                start=True, stop=True), inc=(gi == len(te_sl) - 1)) or tm_b4
        P.wait("vector", "tm", tm_b4)
        emit_v(lambda e: e.tensor_copy(rbb[:, 0:TE], pA[:, 1024:1024 + TE]))
        ve_b4 = emit_v(lambda e: e.tensor_copy(mrbb[:, 0:TE], pA[:, 1536:1536 + TE]), inc=True)

        ve_nrm = {}
        for c in range(NCH):
            slot = c % 4
            if ee == 0 and c == 0:
                P.wait("sync", "ac", ac_sq3[NCH - 1])
                P.wait("sync", "sx1", sx1_tot)
            if c >= 4:
                P.wait("sync", "ve", ve_nrm[c - 4])
            elif ee > 0:
                P.wait("sync", "ve", ve_nrm_last)
            ld_x1c = P.dma("sync", f"lx6{slot}", lambda e, c=c, ecol=ecol, slot=slot: e.dma_start(
                out=x1_t[:, slot * TE:(slot + 1) * TE],
                in_=x1dram[c * 128:(c + 1) * 128, ecol:ecol + TE]))
            P.wait("vector", f"lx6{slot}", ld_x1c)
            if c == 0:
                P.wait("vector", "ve", ve_b4)
            osel = n_out % 2
            if n_out >= 2:
                P.wait("vector", f"sout{osel}", so_out[n_out - 2])
            emit_v(lambda e, c=c: e.tensor_tensor(
                o1_t[:, 0:TE], yf_e[:, c * TE:(c + 1) * TE], rbb[:, 0:TE], ALU.mult))
            emit_v(lambda e: e.tensor_tensor(
                o1_t[:, 0:TE], o1_t[:, 0:TE], mrbb[:, 0:TE], ALU.subtract))
            ve_nrm[c] = emit_v(lambda e, slot=slot, osel=osel: e.tensor_tensor(
                out_t[:, osel * TE:(osel + 1) * TE],
                x1_t[:, slot * TE:(slot + 1) * TE], o1_t[:, 0:TE], ALU.add), inc=True)
            P.wait("sync", "ve", ve_nrm[c])
            so_out[n_out] = P.dma("sync", f"sout{osel}", lambda e, c=c, ecol=ecol, osel=osel: e.dma_start(
                out=outT[c * 128:(c + 1) * 128, ecol:ecol + TE],
                in_=out_t[:, osel * TE:(osel + 1) * TE]))
            n_out += 1
        ve_nrm_last = ve_nrm[NCH - 1]

    P.wait("sync", "sout0", P.ctr.get("sout0", 0))
    P.wait("sync", "sout1", P.ctr.get("sout1", 0))
    P.wait("gpsimd", "sout0", P.ctr.get("sout0", 0))
    P.wait("gpsimd", "sout1", P.ctr.get("sout1", 0))

    with nc.Block() as block:
        @block.sync
        def _(e):
            for fn in P.ops["sync"]:
                fn(e)

        @block.gpsimd
        def _(e):
            for fn in P.ops["gpsimd"]:
                fn(e)

        @block.tensor
        def _(e):
            for fn in P.ops["tensor"]:
                fn(e)

        @block.vector
        def _(e):
            for fn in P.ops["vector"]:
                fn(e)

        @block.scalar
        def _(e):
            for fn in P.ops["scalar"]:
                fn(e)

    return nc


def _prep_core_inputs(x, wq, wk, wv, wo, gup, dnp, cfg, core):
    b = core // cfg.R
    t = core % cfg.R
    perm = np.concatenate([np.arange(e, cfg.Tl, cfg.E) for e in range(cfg.E)])
    sl = x[b, t * cfg.Tl:(t + 1) * cfg.Tl, :][perm, :]
    return {
        "xT": np.ascontiguousarray(sl.T),
        "wqT": np.ascontiguousarray(wq.T),
        "wkT": np.ascontiguousarray(wk.T),
        "wvT": np.ascontiguousarray(wv.T),
        "woT": np.ascontiguousarray(wo.T),
        "wgu": np.ascontiguousarray(gup),
        "wdn": np.ascontiguousarray(dnp),
    }, perm


def kernel(L0_attn_x, L0_attn_w_q, L0_attn_w_k, L0_attn_w_v, L0_attn_w_o,
           L0_ffn_gate_up_proj, L0_ffn_down_proj):
    cfg = CFG()
    x = np.asarray(L0_attn_x, dtype=np.float32)
    wq = np.asarray(L0_attn_w_q, dtype=np.float32)
    wk = np.asarray(L0_attn_w_k, dtype=np.float32)
    wv = np.asarray(L0_attn_w_v, dtype=np.float32)
    wo = np.asarray(L0_attn_w_o, dtype=np.float32)
    gup = np.asarray(L0_ffn_gate_up_proj, dtype=np.float32)
    dnp = np.asarray(L0_ffn_down_proj, dtype=np.float32)

    in_maps = []
    perms = []
    for core in range(8):
        m, perm = _prep_core_inputs(x, wq, wk, wv, wo, gup, dnp, cfg, core)
        in_maps.append(m)
        perms.append(perm)

    nc = build_nc(cfg)
    res = run_bass_kernel_spmd(nc, in_maps, core_ids=list(range(8)))

    out = np.empty((2, 8192, 2048), dtype=np.float32)
    for core in range(8):
        b = core // cfg.R
        t = core % cfg.R
        shard = np.asarray(res.results[core]["outT"], dtype=np.float32).T
        sl = np.empty_like(shard)
        sl[perms[core], :] = shard
        out[b, t * cfg.Tl:(t + 1) * cfg.Tl, :] = sl
    return out
